# version 03
# baseline (speedup 1.0000x reference)
import numpy as np
import ml_dtypes
from contextlib import ExitStack

import concourse.bass as bass
import concourse.mybir as mybir
from concourse.bass_utils import run_bass_kernel_spmd

F32 = mybir.dt.float32
BF16 = mybir.dt.bfloat16
AF = mybir.ActivationFunctionType
ALU = mybir.AluOpType
AX = mybir.AxisListType

T = 4096
D = 1024
NT = T // 128
NC_CORES = 8
H0 = 16
H1 = 16
HF = 8
DH = 64
NEG = -1e30


def _build_l0_masks():
    masks = {}
    for tc in range(6):
        t = tc if tc < 5 else 5
        k0 = max(0, 128 * (t - 4))
        wlen = 128 * (t - max(0, t - 4) + 1)
        m = np.full((128, 128 + wlen), NEG, np.float32)
        m[:, 0:16] = 0.0
        i = 128 * t + np.arange(128)[:, None]
        j = k0 + np.arange(wlen)[None, :]
        valid = (j >= 16) & (i >= 16) & (i - j >= 0) & (i - j <= 512)
        m[:, 128:][valid] = 0.0
        masks[tc] = m
    return masks


def _build_wf_masks():
    mw = np.full((4, 128, 1024), NEG, np.float32)
    for qt in range(4):
        r = np.arange(128)[:, None]
        c = np.arange(1024)[None, :]
        valid = (c >= 128 * qt + r) & (c <= 512 + 128 * qt + r)
        mw[qt][valid] = 0.0
    return mw


def _build_ebias(rel_bias):
    E = np.empty((16, 128, 8064), np.float32)
    for h in range(16):
        r = np.asarray(rel_bias[:, h], np.float32)
        rpad = np.concatenate([np.repeat(r[:1], 3071), r, np.repeat(r[-1:], 3071)])
        rpadf = rpad[::-1]
        for i in range(128):
            E[h, i, :] = rpadf[127 - i: 127 - i + 8064]
    return E


def _build_rotary():
    half_idx = np.arange(0, 64, 2, dtype=np.float64)
    inv_freq = 1.0 / 10000.0 ** (half_idx / 64)
    tt = np.arange(T, dtype=np.float64)
    freqs = tt[:, None] * inv_freq[None, :]
    freqs = np.concatenate([freqs, freqs], -1)
    base = (half_idx + 0.4 * 64) / (1.4 * 64)
    power = (tt - T // 2) / 256.0
    scale = base[None, :] ** power[:, None]
    scale = np.concatenate([scale, scale], -1)
    cs = (np.cos(freqs) * scale).T.astype(np.float32)
    ss = (np.sin(freqs) * scale).T.astype(np.float32)
    ci = (np.cos(freqs) / scale).T.astype(np.float32)
    si = (np.sin(freqs) / scale).T.astype(np.float32)
    return cs, ss, ci, si


def prep_shared_inputs(ins):
    c = {}
    Wqkv0 = np.asarray(ins["L0_attn_W_qkv"], np.float32)
    c["l0_wqT"] = np.ascontiguousarray(Wqkv0[0:1024].T)
    c["l0_wkT"] = np.ascontiguousarray(Wqkv0[1024:2048].T)
    c["l0_wvT"] = np.ascontiguousarray(Wqkv0[2048:3072].T)
    c["l0_woT"] = np.ascontiguousarray(np.asarray(ins["L0_attn_W_out"], np.float32).T)
    c["l0_fw"] = np.asarray(ins["L0_ffn_weight"], np.float32)
    c["l0_alpha"] = np.asarray(ins["L0_ffn_alpha"], np.float32).reshape(1, 1)
    c["l0_bias"] = np.asarray(ins["L0_ffn_bias"], np.float32).reshape(1, D)
    Wqkv1 = np.asarray(ins["L1_attn_W_qkv"], np.float32)
    c["l1_wq"] = np.ascontiguousarray(Wqkv1[:, 0:1024])
    c["l1_wk"] = np.ascontiguousarray(Wqkv1[:, 1024:2048])
    c["l1_wv"] = np.ascontiguousarray(Wqkv1[:, 2048:3072])
    bq = np.asarray(ins["L1_attn_b_qkv"], np.float32)
    c["l1_bq"] = np.ascontiguousarray(bq[0:1024].reshape(16, 64).T)
    c["l1_bk"] = np.ascontiguousarray(bq[1024:2048].reshape(16, 64).T)
    c["l1_bv"] = np.ascontiguousarray(bq[2048:3072].reshape(1, D))
    c["l1_wo"] = np.asarray(ins["L1_attn_W_out"], np.float32)
    c["l1_bo"] = np.asarray(ins["L1_attn_b_out"], np.float32).reshape(1, D)
    c["ebias"] = _build_ebias(np.asarray(ins["L1_attn_rel_bias"], np.float32))
    c["mf_wqkv"] = np.asarray(ins["L1_ffn_wqkv"], np.float32)
    qs = np.asarray(ins["L1_ffn_q_scale"], np.float32)
    ks = np.asarray(ins["L1_ffn_k_scale"], np.float32)
    c["mf_qs"] = np.concatenate([qs, qs], axis=1)
    c["mf_ks"] = np.concatenate([ks, ks], axis=1)
    c["mf_wo"] = np.asarray(ins["L1_ffn_wo"], np.float32)
    c["mf_lgT"] = np.ascontiguousarray(
        np.asarray(ins["L1_ffn_ln_g"], np.float32).reshape(2, 8, 128).transpose(0, 2, 1))
    c["mf_lbT"] = np.ascontiguousarray(
        np.asarray(ins["L1_ffn_ln_b"], np.float32).reshape(2, 8, 128).transpose(0, 2, 1))
    c["mf_w1"] = np.asarray(ins["L1_ffn_w1"], np.float32)
    c["mf_b1"] = np.asarray(ins["L1_ffn_b1"], np.float32)
    c["mf_w2"] = np.asarray(ins["L1_ffn_w2"], np.float32)
    c["mf_b2T"] = np.ascontiguousarray(
        np.asarray(ins["L1_ffn_b2"], np.float32).reshape(2, 8, 128).transpose(0, 2, 1))
    m0 = _build_l0_masks()
    for tc in range(6):
        c[f"m0_{tc}"] = m0[tc]
    mw = _build_wf_masks()
    for i in range(4):
        c[f"mw_{i}"] = np.ascontiguousarray(mw[i])
    cs, ss, ci, si = _build_rotary()
    c["rot_cs"] = cs; c["rot_ss"] = ss; c["rot_ci"] = ci; c["rot_si"] = si
    c["ident_bf"] = np.eye(128, dtype=ml_dtypes.bfloat16)
    c["ident_f32"] = np.eye(128, dtype=np.float32)
    c["ones_f32"] = np.ones((1, 128), np.float32)
    c["ones128c_bf"] = np.ones((128, 1), ml_dtypes.bfloat16)
    c["ones64c_f32"] = np.ones((64, 1), np.float32)
    c["ones128c_f32"] = np.ones((128, 1), np.float32)
    return c


SHARED_SPECS = {
    "l0_wqT": ((1024, 1024), F32), "l0_wkT": ((1024, 1024), F32),
    "l0_wvT": ((1024, 1024), F32), "l0_woT": ((1024, 1024), F32),
    "l0_fw": ((1024, 1024), F32), "l0_alpha": ((1, 1), F32), "l0_bias": ((1, D), F32),
    "l1_wq": ((1024, 1024), F32), "l1_wk": ((1024, 1024), F32), "l1_wv": ((1024, 1024), F32),
    "l1_bq": ((64, 16), F32), "l1_bk": ((64, 16), F32), "l1_bv": ((1, D), F32),
    "l1_wo": ((1024, 1024), F32), "l1_bo": ((1, D), F32),
    "ebias": ((16, 128, 8064), F32),
    "mf_wqkv": ((2, 1024, 1536), F32), "mf_qs": ((2, 128), F32), "mf_ks": ((2, 128), F32),
    "mf_wo": ((2, 512, 1024), F32), "mf_lgT": ((2, 128, 8), F32), "mf_lbT": ((2, 128, 8), F32),
    "mf_w1": ((2, 1024, 5460), F32), "mf_b1": ((2, 5460), F32),
    "mf_w2": ((2, 2730, 1024), F32), "mf_b2T": ((2, 128, 8), F32),
    "m0_0": ((128, 128 + 128), F32), "m0_1": ((128, 128 + 256), F32),
    "m0_2": ((128, 128 + 384), F32), "m0_3": ((128, 128 + 512), F32),
    "m0_4": ((128, 128 + 640), F32), "m0_5": ((128, 128 + 640), F32),
    "mw_0": ((128, 1024), F32), "mw_1": ((128, 1024), F32),
    "mw_2": ((128, 1024), F32), "mw_3": ((128, 1024), F32),
    "rot_cs": ((64, T), F32), "rot_ss": ((64, T), F32),
    "rot_ci": ((64, T), F32), "rot_si": ((64, T), F32),
    "ident_bf": ((128, 128), BF16), "ident_f32": ((128, 128), F32),
    "ones_f32": ((1, 128), F32), "ones128c_bf": ((128, 1), BF16),
    "ones64c_f32": ((64, 1), F32), "ones128c_f32": ((128, 1), F32),
}

BUF_SPECS = {
    "x0": ((T, D), F32),
    "xn0b": ((D, T), BF16),
    "q0": ((H0, 64, T), BF16), "k0": ((H0, 64, T), BF16), "v0": ((NT, 128, D), BF16),
    "ao0": ((NT, 128, D), BF16),
    "x1": ((T, D), F32),
    "xn1b": ((D, T), BF16),
    "q1": ((H1, 64, T), BF16), "k1": ((H1, 64, T), BF16), "v1": ((NT, 128, D), BF16),
    "ao1": ((NT, 128, D), BF16),
    "y": ((T, D), F32),
    "xf_a": ((D, T + 1), F32),
    "xf_b": ((D, T + 1), F32),
    "qf": ((HF, 64, T), BF16), "kf": ((HF, 64, T), BF16), "vf": ((NT, 128, 512), BF16),
    "aof": ((NT, 128, 512), BF16),
    "out": ((T, D), F32),
}



class Phase:

    def __init__(self, nc, st, tag):
        self.nc = nc
        self.st = st
        self.tag = tag
        self.n = {}

    def sb(self, name, shape, dt):
        return self.st.enter_context(self.nc.sbuf_tensor(f"{self.tag}_{name}", shape, dt))

    def ps(self, name, shape, dt=F32):
        return self.st.enter_context(self.nc.psum_tensor(f"{self.tag}_{name}", shape, dt))

    def sem(self, name):
        s = self.st.enter_context(self.nc.semaphore(f"{self.tag}_{name}"))
        self.n[name] = 0
        return s

    def inc(self, name, amt=1):
        self.n[name] += amt
        return self.n[name]



def ph_ln_to_fm(nc, H, src_name, dst_name, tag, out_f32=False):
    src = H[src_name]
    dst = H[dst_name]
    ODT = F32 if out_f32 else BF16
    dst_off = 1 if out_f32 else 0
    dst_stride = dst.shape[1]
    with ExitStack() as st:
        P = Phase(nc, st, tag)
        xt = [P.sb(f"xt{i}", [128, D], F32) for i in range(2)]
        x2 = [P.sb(f"x2{i}", [128, D], F32) for i in range(2)]
        xnb = [P.sb(f"xnb{i}", [128, D], ODT) for i in range(2)]
        fmb = [P.sb(f"fmb{i}", [128, D], ODT) for i in range(2)]
        stats = P.sb("stats", [128, 8], F32)
        idb = P.sb("idb", [128, 128], ODT)
        pst = [P.ps(f"pst{i}", [128, D], ODT) for i in range(2)]
        psdm = P.ps("psdm", [1, 1], F32)
        sd = P.sem("sd"); sa = P.sem("sa"); sv = P.sem("sv")
        sp = P.sem("sp"); sg = P.sem("sg")
        DPT = 2 if "dbg" in H else 1
        with nc.Block() as blk:
            @blk.sync
            def _(sync):
                sync.dma_start(idb[:], H["ident_f32" if out_f32 else "ident_bf"]).then_inc(sd, 16)
                for i in range(NT):
                    if i >= 2:
                        sync.wait_ge(sv, (i - 2) * 7 + 6)
                    sync.dma_start(xt[i % 2][:], src[i * 128:(i + 1) * 128, :]).then_inc(sd, 16)

            @blk.scalar
            def _(scalar):
                for i in range(NT):
                    scalar.wait_ge(sd, 16 * (i + 2))
                    if i >= 2:
                        scalar.wait_ge(sv, (i - 2) * 7 + 2)
                    scalar.activation(x2[i % 2][:], xt[i % 2][:], AF.Square).then_inc(sa, 1)
                    scalar.wait_ge(sv, i * 7 + 4)
                    scalar.activation(stats[:, 6:7], stats[:, 5:6], AF.Ln).then_inc(sa, 1)
                    scalar.wait_ge(sv, i * 7 + 5)
                    scalar.activation(stats[:, 7:8], stats[:, 6:7], AF.Exp).then_inc(sa, 1)

            @blk.vector
            def _(vector):
                for i in range(NT):
                    vector.wait_ge(sd, 16 * (i + 2))
                    vector.reduce_sum(stats[:, 0:1], xt[i % 2][:], AX.X).then_inc(sv, 1)
                    vector.wait_ge(sv, i * 7 + 1)
                    vector.tensor_scalar_mul(stats[:, 2:3], stats[:, 0:1], 1.0 / D)
                    vector.wait_ge(sa, i * 3 + 1)
                    vector.reduce_sum(stats[:, 1:2], x2[i % 2][:], AX.X).then_inc(sv, 1)
                    vector.wait_ge(sv, i * 7 + 2)
                    vector.tensor_scalar_mul(stats[:, 3:4], stats[:, 1:2], 1.0 / D)
                    vector.tensor_mul(stats[:, 4:5], stats[:, 2:3], stats[:, 2:3]).then_inc(sv, 1)
                    vector.wait_ge(sv, i * 7 + 3)
                    vector.scalar_tensor_tensor(stats[:, 5:6], stats[:, 3:4], 1e-5,
                                                stats[:, 4:5], ALU.add, ALU.subtract).then_inc(sv, 1)
                    vector.wait_ge(sa, i * 3 + 2)
                    vector.tensor_scalar_mul(stats[:, 6:7], stats[:, 6:7], -0.5).then_inc(sv, 1)
                    vector.wait_ge(sa, i * 3 + 3)
                    if i >= 2:
                        vector.wait_ge(sp, i - 1)
                    vector.tensor_scalar(xnb[i % 2][:], xt[i % 2][:], stats[:, 2:3],
                                         stats[:, 7:8], ALU.subtract, ALU.mult).then_inc(sv, 1)
                    vector.wait_ge(sp, i + 1)
                    if i >= 2:
                        vector.wait_ge(sg, 16 * (DPT * (i - 2) + 1))
                    vector.tensor_copy(fmb[i % 2][:], pst[i % 2][:]).then_inc(sv, 1)

            @blk.tensor
            def _(tensor):
                tensor.wait_ge(sd, 16)
                for i in range(NT):
                    tensor.wait_ge(sv, i * 7 + 6)
                    if i >= 2:
                        tensor.wait_ge(sv, (i - 2) * 7 + 7)
                    settle(tensor, psdm[0:1, 0:1], idb[:, 0:1], k=2)
                    for cidx in range(8):
                        tensor.transpose(pst[i % 2][:, cidx * 128:(cidx + 1) * 128],
                                         xnb[i % 2][:, cidx * 128:(cidx + 1) * 128], idb[:])
                    tensor.matmul(psdm[0:1, 0:1], idb[:, 0:1], idb[:, 0:1],
                                  start=True, stop=True).then_inc(sp, 1)

            @blk.gpsimd
            def _(gpsimd):
                ndma = 0
                for i in range(NT):
                    gpsimd.wait_ge(sv, i * 7 + 7)
                    dap = bass.AP(dst.tensor, dst_off + i * 128,
                                  [[dst_stride, 128], [128 * dst_stride, 8], [1, 128]])
                    gpsimd.dma_start(dap, fmb[i % 2].ap().rearrange("p (c t) -> p c t", c=8)).then_inc(sg, 16)
                    ndma += 1
                    if "dbg" in H:
                        gpsimd.dma_start(H["dbg"][i * 128:(i + 1) * 128, :], xnb[i % 2][:]).then_inc(sg, 16)
                        ndma += 1
                gpsimd.wait_ge(sg, 16 * ndma)
    return



def settle(e, ps_ap, opa, k=3):
    for _ in range(k):
        e.matmul(ps_ap, opa, opa, start=True, stop=True)


def ecopy(e, out, in_):
    if hasattr(e, "tensor_copy"):
        return e.tensor_copy(out, in_)
    return e.copy(out, in_)


class Sched:
    ENGINES = ("sync", "tensor", "vector", "scalar", "gpsimd")

    def __init__(self):
        self.ops = {e: [] for e in self.ENGINES}

    def add(self, eng, fn):
        self.ops[eng].append(fn)

    def run(self, nc):
        with nc.Block() as blk:
            for eng in self.ENGINES:
                if not self.ops[eng]:
                    continue
                fns = self.ops[eng]
                getattr(blk, eng)(lambda e, fns=fns: [f(e) for f in fns])



def ph_qkv01(nc, H, tag, src, wq, wk, wv, q_out, k_out, v_out,
             bqk=None, bv=None):
    with ExitStack() as st:
        P = Phase(nc, st, tag)
        xn = P.sb("xn", [128, 8 * T], BF16)
        wst = P.sb("wst", [128, 8 * 1024], F32)
        wqb = P.sb("wqb", [128, 8 * 1024], BF16)
        wkb = P.sb("wkb", [128, 8 * 1024], BF16)
        wvb = P.sb("wvb", [128, 8 * 1024], BF16)
        bq_sb = P.sb("bq", [64, 16], F32) if bqk else None
        bk_sb = P.sb("bk", [64, 16], F32) if bqk else None
        bv_sb = P.sb("bv", [1, D], F32) if bv else None
        ones1 = P.sb("ones1", [1, 128], F32) if bv else None
        qe = [P.sb(f"qe{i}", [64, 1024], BF16) for i in range(2)]
        ve = [P.sb(f"ve{i}", [128, D], BF16) for i in range(2)]
        ps_q = [P.ps(f"psq{i}", [64, 1024]) for i in range(2)]
        ps_v = [P.ps(f"psv{i}", [128, 512]) for i in range(2)]
        sd = P.sem("sd"); sw = P.sem("sw"); sp = P.sem("sp"); spv = P.sem("spv")
        sv = P.sem("sv"); sa = P.sem("sa"); sg = P.sem("sg"); so = P.sem("so")
        S = Sched()
        nld = [0]

        def load(dst, src_ap, pre_wait=None):
            nld[0] += 1
            def f(e, d=dst, s=src_ap, pw=pre_wait):
                if pw is not None:
                    e.wait_ge(pw[0], pw[1])
                e.dma_start(d, s).then_inc(sd, 16)
            S.add("sync", f)
            return 16 * nld[0]

        for c in range(8):
            load(xn[:, c * T:(c + 1) * T], src[c * 128:(c + 1) * 128, :])
        for wi, (wraw, wcast) in enumerate([(wq, wqb), (wk, wkb), (wv, wvb)]):
            for c in range(8):
                pw = (sw, 8 * wi) if (wi > 0 and c == 0) else None
                load(wst[:, c * 1024:(c + 1) * 1024],
                     wraw[c * 128:(c + 1) * 128, :], pre_wait=pw)
            lded = 16 * nld[0]
            for c in range(8):
                def cast(e, c=c, wc=wcast, l=lded, first=(c == 0)):
                    if first:
                        e.wait_ge(sd, l)
                    ecopy(e, wc[:, c * 1024:(c + 1) * 1024],
                          wst[:, c * 1024:(c + 1) * 1024]).then_inc(sw, 1)
                S.add("scalar", cast)
        if bqk:
            load(bq_sb[:], H[bqk[0]])
            load(bk_sb[:], H[bqk[1]])
        if bv:
            load(bv_sb[:], H[bv])
            load(ones1[:], H["ones_f32"])
        all_loads = 16 * nld[0]

        for side, (wcast, obuf, bsb) in enumerate(
                [(wqb, q_out, bq_sb), (wkb, k_out, bk_sb)]):
            for h in range(16):
                for quar in range(4):
                    g = side * 64 + h * 4 + quar
                    par = g % 2

                    def pre(e, g=g):
                        if g == 0:
                            e.wait_ge(sd, all_loads)
                            e.wait_ge(sw, 16)
                        if g >= 2:
                            e.wait_ge(sa, g - 1)
                    S.add("tensor", pre)
                    for nq in range(2):
                        for c in range(8):
                            last = (nq == 1 and c == 7)
                            def mm(e, par=par, nq=nq, c=c, h=h, quar=quar,
                                   wcast=wcast, last=last):
                                i = e.matmul(
                                    ps_q[par][:, nq * 512:(nq + 1) * 512],
                                    wcast[:, c * 1024 + h * 64: c * 1024 + h * 64 + 64],
                                    xn[:, c * T + quar * 1024 + nq * 512:
                                       c * T + quar * 1024 + (nq + 1) * 512],
                                    start=(c == 0), stop=(c == 7))
                                if last:
                                    i.then_inc(sp, 1)
                            S.add("tensor", mm)

                    def evac(e, par=par, h=h, bsb=bsb, g=g):
                        e.wait_ge(sp, g + 1)
                        if g >= 2:
                            e.wait_ge(so, 16 * (g - 1))
                        if bsb is not None:
                            e.activation(qe[par][:], ps_q[par][:], AF.Identity,
                                         bias=bsb[:, h:h + 1]).then_inc(sa, 1)
                        else:
                            e.activation(qe[par][:], ps_q[par][:], AF.Identity).then_inc(sa, 1)
                    S.add("scalar", evac)

                    def store(e, par=par, h=h, quar=quar, obuf=obuf, g=g):
                        e.wait_ge(sa, g + 1)
                        e.dma_start(obuf[h, :, quar * 1024:(quar + 1) * 1024],
                                    qe[par][:]).then_inc(so, 16)
                    S.add("gpsimd", store)

        for ti in range(NT):
            par = ti % 2

            def prev(e, ti=ti):
                if ti == 0:
                    e.wait_ge(sw, 24)
                if ti >= 1:
                    e.wait_ge(sv, 2 * ti)
            S.add("tensor", prev)
            for nq in range(2):
                for c in range(8):
                    last = (c == 7 and bv is None)
                    def mmv(e, nq=nq, c=c, ti=ti, last=last):
                        i = e.matmul(ps_v[nq][:],
                                     xn[:, c * T + ti * 128: c * T + (ti + 1) * 128],
                                     wvb[:, c * 1024 + nq * 512: c * 1024 + (nq + 1) * 512],
                                     start=(c == 0), stop=(c == 7))
                        if last:
                            i.then_inc(spv, 1)
                    S.add("tensor", mmv)
                if bv:
                    def mmb(e, nq=nq):
                        e.matmul(ps_v[nq][:], ones1[:],
                                 bv_sb[:, nq * 512:(nq + 1) * 512],
                                 start=False, stop=True).then_inc(spv, 1)
                    S.add("tensor", mmb)

                def vevac(e, par=par, nq=nq, ti=ti):
                    e.wait_ge(spv, 2 * ti + nq + 1)
                    if ti >= 2 and nq == 0:
                        e.wait_ge(sg, 16 * (ti - 1))
                    e.tensor_copy(ve[par][:, nq * 512:(nq + 1) * 512],
                                  ps_v[nq][:]).then_inc(sv, 1)
                S.add("vector", vevac)

            def vstore(e, par=par, ti=ti):
                e.wait_ge(sv, 2 * ti + 2)
                e.dma_start(v_out[ti], ve[par][:]).then_inc(sg, 16)
            S.add("gpsimd", vstore)

        def fin(e):
            e.wait_ge(so, 16 * 128)
            e.wait_ge(sg, 16 * NT)
        S.add("gpsimd", fin)
        S.run(nc)
    return



class AttnCfg:
    def __init__(self, nh, v_cols, scale, ranges, bias, head_flush, lw,
                 qt_list=None):
        self.nh = nh; self.v_cols = v_cols; self.scale = scale
        self.ranges = ranges; self.bias = bias
        self.head_flush = head_flush; self.lw = lw
        self.qt_list = qt_list if qt_list is not None else list(range(NT))


def ph_attn(nc, H, tag, q_buf, k_buf, v_buf, ao_buf, cfg):
    nh, v_cols, lw = cfg.nh, cfg.v_cols, cfg.lw
    hf = cfg.head_flush
    gcols = hf * 64
    use_E = cfg.bias[0] == "E"
    with ExitStack() as st:
        P = Phase(nc, st, tag)
        v_sb = P.sb("v", [128, NT * v_cols], BF16)
        ao_sb = P.sb("ao", [128, NT * gcols], BF16)
        q_sb = P.sb("q", [64, T], BF16)
        k_sb = P.sb("k", [64, T], BF16)
        u_sb = P.sb("u", [128, lw], F32)
        p_sb = P.sb("p", [128, lw], BF16)
        pt_sb = P.sb("pt", [128, lw], BF16)
        negm = P.sb("negm", [128, 1], F32)
        l_sb = P.sb("l", [128, 1], F32)
        rl_sb = P.sb("rl", [128, 1], F32)
        idb = P.sb("idb", [128, 128], BF16)
        E_sb = P.sb("E", [128, 8064], F32) if use_E else None
        masks = {}
        if not use_E:
            for qt in range(NT):
                mn, _ = cfg.bias[1](qt)
                if mn not in masks:
                    shape = SHARED_SPECS[mn][0]
                    masks[mn] = P.sb(f"msk_{mn}", list(shape), F32)
        ps_s = [P.ps(f"pss{i}", [128, 1024]) for i in range(2)]
        ps_t = [P.ps(f"pst{i}", [128, 1024], BF16) for i in range(2)]
        ps_av = P.ps("psav", [128, 64])
        psdm = P.ps("psdm", [1, 1], F32)
        sd = P.sem("sd"); sp = P.sem("sp"); sv = P.sem("sv")
        sa = P.sem("sa"); sg = P.sem("sg")
        S = Sched()
        n = dict(sd=0, sp=0, sv=0, sa=0, sg=0)

        def load(dst, src_ap, waits=()):
            n["sd"] += 16
            def f(e, d=dst, s=src_ap, w=tuple(waits)):
                for sem, val in w:
                    e.wait_ge(sem, val)
                e.dma_start(d, s).then_inc(sd, 16)
            S.add("sync", f)
            return n["sd"]

        load(idb[:], H["ident_bf"])
        for mn, msb in masks.items():
            load(msb[:], H[mn])
        for ti in range(NT):
            load(v_sb[:, ti * v_cols:(ti + 1) * v_cols], v_buf[ti])

        qk_end = {}; stt_end = {}; exp_m = {}; transp_m = {}; av_m = {}
        recip_m = {}; ao_m = {}; flush_m = {}
        pss_last_stt = [0, 0]; pst_last_evac = [0, 0]
        spar = [0]; tpar = [0]

        for h in range(nh):
            waits = []
            if h > 0:
                waits.append((sp, qk_end[h - 1]))
            ldm = load(q_sb[:], q_buf[h], waits)
            ldm = load(k_sb[:], k_buf[h])
            if use_E:
                ew = [(sv, stt_end[h - 1])] if h > 0 else []
                ldm = load(E_sb[:], H[cfg.bias[1]][h], ew)
            load_m = ldm
            eoff = None

            for qt in cfg.qt_list:
                ranges = cfg.ranges(qt)
                W = sum(w for _, w in ranges)
                if use_E:
                    bias_ap = E_sb[:, 3968 - 128 * qt: 3968 - 128 * qt + T]
                else:
                    mn, moff = cfg.bias[1](qt)
                    bias_ap = masks[mn][:, moff:]
                qprev = cfg.qt_list[cfg.qt_list.index(qt) - 1] if cfg.qt_list.index(qt) > 0 else None
                lastq = cfg.qt_list[-1]
                prev_exp_u = exp_m.get((h, qprev)) or (exp_m.get((h - 1, lastq)) if h else None)
                pieces = []
                off = 0
                for koff, w in ranges:
                    o2 = 0
                    while o2 < w:
                        so = off + o2
                        pw = min(512 - so % 512, w - o2)
                        pieces.append((so, koff + o2, pw))
                        o2 += pw
                    off += w
                quarters = []
                for so, ko, pw in pieces:
                    qi = so // 1024
                    while len(quarters) <= qi:
                        quarters.append([])
                    quarters[qi].append((so, ko, pw))
                for qi, qp in enumerate(quarters):
                    par = spar[0] % 2
                    spar[0] += 1
                    qoff = qi * 1024
                    qw = sum(pw for _, _, pw in qp)
                    prewaits = []
                    if pss_last_stt[par]:
                        prewaits.append((sv, pss_last_stt[par]))
                    if h == 0 and qt == 0 and qi == 0:
                        prewaits.append((sd, n["sd"]))
                    elif qt == 0 and qi == 0:
                        prewaits.append((sd, load_m))
                    for pi, (so, ko, pw) in enumerate(qp):
                        last = pi == len(qp) - 1
                        if last:
                            n["sp"] += 1
                        def mm(e, par=par, so=so, ko=ko, pw=pw, qt=qt,
                               qoff=qoff, last=last, prewaits=tuple(prewaits),
                               first=(pi == 0)):
                            if first:
                                for sem, val in prewaits:
                                    e.wait_ge(sem, val)
                            i = e.matmul(ps_s[par][:, so - qoff: so - qoff + pw],
                                         q_sb[:, qt * 128:(qt + 1) * 128],
                                         k_sb[:, ko: ko + pw],
                                         start=True, stop=True)
                            if last:
                                i.then_inc(sp, 1)
                        S.add("tensor", mm)
                    qk_m = n["sp"]
                    n["sv"] += 1
                    pss_last_stt[par] = n["sv"]
                    def stt(e, par=par, qoff=qoff, qw=qw, qi=qi,
                            bias_ap=bias_ap, qk_m=qk_m, m=n["sv"],
                            peu=prev_exp_u):
                        e.wait_ge(sp, qk_m)
                        if qi == 0 and peu:
                            e.wait_ge(sa, peu)
                        e.scalar_tensor_tensor(
                            u_sb[:, qoff: qoff + qw], ps_s[par][:, 0:qw],
                            cfg.scale, bias_ap[:, qoff: qoff + qw],
                            ALU.mult, ALU.add).then_inc(sv, 1)
                    S.add("vector", stt)
                qk_end[h] = n["sp"]
                stt_end[h] = n["sv"]
                n["sv"] += 1
                max_m = n["sv"]
                prev_exp = exp_m.get((h, qprev)) or (exp_m.get((h - 1, lastq)) if h else None)
                def domax(e, W=W, sm=n["sv"], pe=prev_exp):
                    e.wait_ge(sv, sm - 1)
                    if pe:
                        e.wait_ge(sa, pe)
                    e.reduce_max(negm[:], u_sb[:, 0:W], AX.X, negate=True).then_inc(sv, 1)
                S.add("vector", domax)
                n["sa"] += 1
                exp_m[(h, qt)] = n["sa"]
                prev_tr = transp_m.get((h, qprev)) or (transp_m.get((h - 1, lastq)) if h else None)
                def doexp(e, W=W, mm_=max_m, pt_=prev_tr):
                    e.wait_ge(sv, mm_)
                    if pt_:
                        e.wait_ge(sp, pt_)
                    e.activation(p_sb[:, 0:W], u_sb[:, 0:W], AF.Exp,
                                 bias=negm[:, 0:1],
                                 accum_out=l_sb[:, 0:1]).then_inc(sa, 1)
                S.add("scalar", doexp)
                npieces = W // 128
                glist = [list(range(g * 8, min((g + 1) * 8, npieces)))
                         for g in range((npieces + 7) // 8)]
                ptevac_m = []
                for gi, gp in enumerate(glist):
                    tp = tpar[0] % 2
                    tpar[0] += 1
                    twaits = [(sa, exp_m[(h, qt)])]
                    if pst_last_evac[tp]:
                        twaits.append((sv, pst_last_evac[tp]))
                    n["sp"] += 1
                    def dotr(e, gp=tuple(gp), tp=tp, tw=tuple(twaits), m=n["sp"]):
                        for sem, val in tw:
                            e.wait_ge(sem, val)
                        settle(e, psdm[0:1, 0:1], idb[:, 0:1], k=2)
                        for j in gp:
                            e.transpose(ps_t[tp][:, (j % 8) * 128:(j % 8 + 1) * 128],
                                        p_sb[:, j * 128:(j + 1) * 128], idb[:])
                        e.matmul(psdm[0:1, 0:1], idb[:, 0:1], idb[:, 0:1],
                                 start=True, stop=True).then_inc(sp, 1)
                    S.add("tensor", dotr)
                    tr_m = n["sp"]
                    n["sv"] += 1
                    pst_last_evac[tp] = n["sv"]
                    ptevac_m.append(n["sv"])
                    pav = av_m.get((h, qprev)) or (av_m.get((h - 1, lastq)) if h else None)
                    def doev(e, gp=tuple(gp), tp=tp, tr_m=tr_m,
                             first=(gi == 0), pav=pav):
                        e.wait_ge(sp, tr_m)
                        if first and pav:
                            e.wait_ge(sp, pav)
                        e.tensor_copy(pt_sb[:, gp[0] * 128: (gp[-1] + 1) * 128],
                                      ps_t[tp][:, 0:(len(gp)) * 128]).then_inc(sv, 1)
                    S.add("vector", doev)
                transp_m[(h, qt)] = n["sp"]
                vtiles = []
                for koff, w in ranges:
                    for o2 in range(0, w, 128):
                        vtiles.append((koff + o2) // 128)
                pao = ao_m.get((h, qprev)) or (ao_m.get((h - 1, lastq)) if h else None)
                n["sp"] += 1
                av_m[(h, qt)] = n["sp"]
                hloc = h % hf
                def doav(e, vtiles=tuple(vtiles), pev=tuple(ptevac_m), h=h,
                         pao=pao, m=n["sp"]):
                    for j, vt in enumerate(vtiles):
                        if j % 8 == 0:
                            e.wait_ge(sv, pev[j // 8])
                            settle(e, psdm[0:1, 0:1], idb[:, 0:1], k=2)
                        if j == 0 and pao:
                            e.wait_ge(sv, pao)
                        i = e.matmul(ps_av[:], pt_sb[:, j * 128:(j + 1) * 128],
                                     v_sb[:, vt * v_cols + h * 64: vt * v_cols + h * 64 + 64],
                                     start=(j == 0), stop=(j == len(vtiles) - 1))
                    i.then_inc(sp, 1)
                S.add("tensor", doav)
                n["sv"] += 2
                ao_m[(h, qt)] = n["sv"]
                fl = flush_m.get(h // hf - 1) if (h % hf == 0 and h and qt == 0) else None
                def dono(e, qt=qt, hloc=hloc, avm=av_m[(h, qt)], m=n["sv"], fl=fl):
                    e.wait_ge(sp, avm)
                    e.reciprocal(rl_sb[:], l_sb[:]).then_inc(sv, 1)
                    e.wait_ge(sv, m - 1)
                    if fl:
                        e.wait_ge(sg, fl)
                    e.tensor_scalar_mul(
                        ao_sb[:, qt * gcols + hloc * 64: qt * gcols + hloc * 64 + 64],
                        ps_av[:], rl_sb[:, 0:1]).then_inc(sv, 1)
                S.add("vector", dono)
            if (h + 1) % hf == 0:
                grp = h // hf
                aodone = n["sv"]
                for ti in range(NT):
                    n["sg"] += 16
                    def dofl(e, ti=ti, grp=grp, aodone=aodone, first=(ti == 0)):
                        if first:
                            e.wait_ge(sv, aodone)
                        e.dma_start(
                            ao_buf[ti, :, grp * gcols:(grp + 1) * gcols],
                            ao_sb[:, ti * gcols:(ti + 1) * gcols]).then_inc(sg, 16)
                    S.add("gpsimd", dofl)
                flush_m[grp] = n["sg"]

        if "dbg_u" in H:
            def dmp(e, sv_v=n["sv"], sa_v=n["sa"], sp_v=n["sp"]):
                e.wait_ge(sv, sv_v); e.wait_ge(sa, sa_v); e.wait_ge(sp, sp_v)
                e.dma_start(H["dbg_u"], u_sb[:]).then_inc(sg, 16)
                e.dma_start(H["dbg_p"], p_sb[:]).then_inc(sg, 16)
                e.dma_start(H["dbg_pt"], pt_sb[:]).then_inc(sg, 16)
                e.dma_start(H["dbg_l"][:, 0:1], negm[:], allow_slow_non_contiguous=True).then_inc(sg, 16)
                e.dma_start(H["dbg_l"][:, 1:2], l_sb[:], allow_slow_non_contiguous=True).then_inc(sg, 16)
                e.dma_start(H["dbg_l"][:, 2:3], rl_sb[:], allow_slow_non_contiguous=True).then_inc(sg, 16)
            S.add("gpsimd", dmp)
            n["sg"] += 96

        def fin(e, tot=n["sg"]):
            e.wait_ge(sg, tot)
        S.add("gpsimd", fin)
        S.run(nc)
    return


def _l0_ranges(qt):
    k0 = max(0, 128 * (qt - 4))
    wlen = 128 * (qt - max(0, qt - 4) + 1)
    return [(0, 128), (k0, wlen)]


def _mini_ranges(qt):
    w = qt // 4
    k0 = max(0, 512 * (w - 1))
    return [(k0, 512 * w + 512 - k0)]



def ph_proj(nc, H, tag, ao_buf, wo_name, resid_buf, out_buf, bias_name=None,
            l0ffn=None, ln_fm_out=None):
    with ExitStack() as st:
        P = Phase(nc, st, tag)
        wo_st = P.sb("wo_st", [128, 8 * 1024], F32)
        wo_bf = P.sb("wo_bf", [128, 8 * 1024], BF16)
        xt = [P.sb(f"xt{i}", [128, D], F32) for i in range(2)]
        at = [P.sb(f"at{i}", [128, D], BF16) for i in range(2)]
        aoT = [P.sb(f"aoT{i}", [128, D], BF16) for i in range(2)]
        ot = [P.sb(f"ot{i}", [128, D], F32) for i in range(2)]
        idb = P.sb("idb", [128, 128], BF16)
        bias_sb = P.sb("bias", [1, D], F32) if bias_name else None
        ones1 = P.sb("ones1", [1, 128], F32)
        ones128c = P.sb("ones128c", [128, 1], F32)
        if l0ffn:
            fw = wo_st
            weff = P.sb("weff", [128, 8 * 1024], BF16)
            weffT = P.sb("weffT", [128, 8 * 1024], BF16)
            xnf = P.sb("xnf", [128, 8 * T], BF16)
            tabs = P.sb("tabs", [128, 1024], F32)
            t2 = P.sb("t2", [128, 1024], F32)
            srow = P.sb("srow", [128, 4], F32)
            tot1 = P.sb("tot1", [1, 1], F32)
            alph = P.sb("alph", [128, 1], F32)
        if ln_fm_out:
            x2 = P.sb("x2", [128, D], F32)
            xnb = [P.sb(f"xnb{i}", [128, D], BF16) for i in range(2)]
            fmb = [P.sb(f"fmb{i}", [128, D], BF16) for i in range(2)]
            stats = P.sb("stats", [128, 8], F32)
        ps_t0 = [P.ps(f"pst0{i}", [128, 1024], BF16) for i in range(2)]
        ps_o = [P.ps(f"pso{i}", [128, 512]) for i in range(2)]
        ps_t1 = [P.ps(f"pst1{i}", [128, 1024], BF16) for i in range(2)]
        ps_sc = P.ps("pssc", [128, 1])
        psdm = P.ps("psdm", [1, 1], F32)
        sd = P.sem("sd"); sp = P.sem("sp"); sv = P.sem("sv")
        sa = P.sem("sa"); sw = P.sem("sw"); sg = P.sem("sg")
        S = Sched()
        n = dict(sd=0, sp=0, sv=0, sa=0, sw=0, sg=0)

        def load(dst, src_ap, waits=()):
            n["sd"] += 16
            def f(e, d=dst, s=src_ap, w=tuple(waits)):
                for sem, val in w:
                    e.wait_ge(sem, val)
                e.dma_start(d, s).then_inc(sd, 16)
            S.add("sync", f)
            return n["sd"]

        load(idb[:], H["ident_bf"])
        load(ones1[:], H["ones_f32"])
        load(ones128c[:], H["ones128c_f32"])
        if bias_name:
            load(bias_sb[:], H[bias_name])
        for c in range(8):
            load(wo_st[:, c * 1024:(c + 1) * 1024],
                 H[wo_name][c * 128:(c + 1) * 128, :])
        wo_ld = n["sd"]
        for c in range(8):
            n["sw"] += 1
            def cast(e, c=c, first=(c == 0), m=wo_ld):
                if first:
                    e.wait_ge(sd, m)
                e.tensor_copy(wo_bf[:, c * 1024:(c + 1) * 1024],
                              wo_st[:, c * 1024:(c + 1) * 1024]).then_inc(sw, 1)
            S.add("gpsimd", cast)
        wo_cast = n["sw"]

        tern_done = 0
        if l0ffn:
            fw_name, alpha_name, xn_buf = l0ffn
            load(alph[:], bass.AP(H[alpha_name].tensor, 0, [[0, 128], [1, 1]]))
            for c in range(8):
                pw = [(sw, wo_cast)] if c == 0 else []
                load(fw[:, c * 1024:(c + 1) * 1024],
                     H[fw_name][c * 128:(c + 1) * 128, :], pw)
            fw_ld = n["sd"]
            for c in range(8):
                load(xnf[:, c * T:(c + 1) * T], xn_buf[c * 128:(c + 1) * 128, :])
            abs_m = []
            red_m = []
            acc_m = []
            for c in range(8):
                n["sa"] += 1
                abs_m.append(n["sa"])
                def doabs(e, c=c, first=(c == 0), m=fw_ld,
                          tw=(red_m[c - 1] if c else 0)):
                    if first:
                        e.wait_ge(sd, m)
                    else:
                        e.wait_ge(sv, tw)
                    e.activation(tabs[:], fw[:, c * 1024:(c + 1) * 1024],
                                 AF.Abs).then_inc(sa, 1)
                S.add("scalar", doabs)
                n["sv"] += 2
                red_m.append(n["sv"] - 1)
                acc_m.append(n["sv"])
                def dosum(e, c=c, am=abs_m[c], m=n["sv"]):
                    e.wait_ge(sa, am)
                    e.reduce_sum(srow[:, 0:1], tabs[:], AX.X).then_inc(sv, 1)
                    e.wait_ge(sv, m - 1)
                    if c == 0:
                        e.tensor_copy(srow[:, 1:2], srow[:, 0:1]).then_inc(sv, 1)
                    else:
                        e.tensor_add(srow[:, 1:2], srow[:, 1:2], srow[:, 0:1]).then_inc(sv, 1)
                    e.wait_ge(sv, m)
                S.add("vector", dosum)
            n["sp"] += 1
            totm = n["sp"]
            def dotot(e, m=acc_m[-1], sm=n["sp"]):
                e.wait_ge(sv, m)
                e.matmul(ps_sc[0:1, 0:1], ones128c[:], srow[:, 1:2],
                         start=True, stop=True).then_inc(sp, 1)
            S.add("tensor", dotot)
            n["sv"] += 1
            tot_ev = n["sv"]
            def dototev(e, pm=totm, m=n["sv"]):
                e.wait_ge(sp, pm)
                e.tensor_copy(tot1[:], ps_sc[0:1, 0:1]).then_inc(sv, 1)
                e.wait_ge(sv, m)
            S.add("vector", dototev)
            n["sp"] += 1
            bcm = n["sp"]
            def dobc(e, m=tot_ev, sm=n["sp"]):
                e.wait_ge(sv, m)
                e.matmul(ps_sc[:, 0:1], ones1[:], tot1[:],
                         start=True, stop=True).then_inc(sp, 1)
            S.add("tensor", dobc)
            n["sv"] += 2
            del_m = n["sv"]
            def dodel(e, pm=bcm, m=n["sv"]):
                e.wait_ge(sp, pm)
                e.tensor_scalar_mul(srow[:, 2:3], ps_sc[:, 0:1],
                                    0.7 / (1024.0 * 1024.0)).then_inc(sv, 1)
                e.tensor_scalar_mul(srow[:, 3:4], ps_sc[:, 0:1],
                                    -0.7 / (1024.0 * 1024.0)).then_inc(sv, 1)
                e.wait_ge(sv, m)
            S.add("vector", dodel)
            al_m = []
            for c in range(8):
                n["sv"] += 3
                tv = n["sv"]
                def dotern(e, c=c, m=tv, dm=del_m,
                           aw=(al_m[c - 1] if c else 0)):
                    if c == 0:
                        e.wait_ge(sv, dm)
                    else:
                        e.wait_ge(sa, aw)
                    e.tensor_scalar(tabs[:], fw[:, c * 1024:(c + 1) * 1024],
                                    srow[:, 2:3], None, ALU.is_gt).then_inc(sv, 1)
                    e.tensor_scalar(t2[:], fw[:, c * 1024:(c + 1) * 1024],
                                    srow[:, 3:4], -1.0, ALU.is_lt, ALU.mult).then_inc(sv, 1)
                    e.wait_ge(sv, m - 1)
                    e.tensor_add(tabs[:], tabs[:], t2[:]).then_inc(sv, 1)
                    e.wait_ge(sv, m)
                S.add("vector", dotern)
                n["sa"] += 1
                al_m.append(n["sa"])
                def doal(e, c=c, vm=tv, am=n["sa"]):
                    e.wait_ge(sv, vm)
                    e.mul(weff[:, c * 1024:(c + 1) * 1024], tabs[:],
                          alph[:, 0:1]).then_inc(sa, 1)
                S.add("scalar", doal)
            al_done = al_m[-1]
            wtr_m = []
            wev_m = []
            for co in range(8):
                n["sp"] += 1
                wtr_m.append(n["sp"])
                def dotr(e, co=co, am=al_done, m=n["sp"],
                         ew=(wev_m[co - 1] if co else 0)):
                    if co == 0:
                        e.wait_ge(sa, am)
                    else:
                        e.wait_ge(sv, ew)
                    settle(e, psdm[0:1, 0:1], idb[:, 0:1], k=2)
                    for ci in range(8):
                        e.transpose(
                            ps_t0[0][:, ci * 128:(ci + 1) * 128],
                            weff[:, co * 1024 + ci * 128: co * 1024 + (ci + 1) * 128],
                            idb[:])
                    e.matmul(psdm[0:1, 0:1], idb[:, 0:1], idb[:, 0:1],
                             start=True, stop=True).then_inc(sp, 1)
                S.add("tensor", dotr)
                n["sv"] += 1
                wev_m.append(n["sv"])
                def doev(e, co=co, pm=wtr_m[co], m=n["sv"]):
                    e.wait_ge(sp, pm)
                    e.tensor_copy(
                        weffT.ap().rearrange("p (ci co x) -> p ci co x",
                                             ci=8, co=8)[:, :, co, :],
                        ps_t0[0].ap().rearrange("p (ci x) -> p ci x", ci=8)
                    ).then_inc(sv, 1)
                S.add("vector", doev)
            tern_done = wev_m[-1]

        all_ld = n["sd"]

        ev_marks = {}; tr_marks = {}; aoT_marks = {}; st_marks = {}
        xnb_marks = {}; trl_marks = {}; fm_marks = {}; stf_marks = {}; sq_marks = {}
        for ti in range(NT):
            par = ti % 2
            lw1 = [(sv, ev_marks[ti - 2])] if ti >= 2 else []
            lm1 = load(xt[par][:], resid_buf[ti * 128:(ti + 1) * 128, :], lw1)
            lw2 = [(sp, tr_marks[ti - 2])] if ti >= 2 else []
            lm2 = load(at[par][:], ao_buf[ti], lw2)
            n["sp"] += 1
            tr_marks[ti] = n["sp"]
            def dotr0(e, par=par, ti=ti, lm=lm2, tern=tern_done,
                      aw=(aoT_marks.get(ti - 2, 0))):
                e.wait_ge(sd, lm)
                if ti == 0 and tern:
                    e.wait_ge(sv, tern)
                if ti >= 2:
                    e.wait_ge(sv, aw)
                for c in range(8):
                    e.transpose(ps_t0[par][:, c * 128:(c + 1) * 128],
                                at[par][:, c * 128:(c + 1) * 128], idb[:])
                e.matmul(psdm[0:1, 0:1], idb[:, 0:1], idb[:, 0:1],
                         start=True, stop=True).then_inc(sp, 1)
            S.add("tensor", dotr0)
            n["sv"] += 1
            aoT_marks[ti] = n["sv"]
            def doaoT(e, par=par, pm=tr_marks[ti], m=n["sv"]):
                e.wait_ge(sp, pm)
                e.tensor_copy(aoT[par][:], ps_t0[par][:]).then_inc(sv, 1)
            S.add("vector", doaoT)
            aoT_m = n["sv"]
            mm_first = n["sp"] + 1
            for half in range(2):
                n["sp"] += 1
                def domm(e, par=par, half=half, ti=ti, am=aoT_m,
                         has_f=bool(l0ffn), first=(ti == 0 and half == 0),
                         evw=(ev_marks.get(ti - 1, 0))):
                    e.wait_ge(sv, am)
                    if first:
                        e.wait_ge(sw, wo_cast)
                        e.wait_ge(sd, all_ld)
                    if ti >= 1 and half == 0:
                        e.wait_ge(sv, evw)
                    settle(e, psdm[0:1, 0:1], idb[:, 0:1], k=2)
                    specs = [(aoT[par][:, c * 128:(c + 1) * 128],
                              wo_bf[:, c * 1024 + half * 512: c * 1024 + (half + 1) * 512])
                             for c in range(8)]
                    if has_f:
                        specs += [(xnf[:, c * T + ti * 128: c * T + (ti + 1) * 128],
                                   weffT[:, c * 1024 + half * 512: c * 1024 + (half + 1) * 512])
                                  for c in range(8)]
                    if bias_name:
                        specs.append((ones1[:], bias_sb[:, half * 512:(half + 1) * 512]))
                    for idx, (lh, rh) in enumerate(specs):
                        i = e.matmul(ps_o[half][:], lh, rh,
                                     start=(idx == 0), stop=(idx == len(specs) - 1))
                    i.then_inc(sp, 1)
                S.add("tensor", domm)
            mm_m = n["sp"]
            n["sv"] += 2
            ev_marks[ti] = n["sv"]
            def doev2(e, par=par, pm=mm_m, lm=lm1, ti=ti, m=n["sv"],
                      stw=(st_marks.get(ti - 2, 0)),
                      xw=(xnb_marks.get(ti - 2, 0) if ln_fm_out else 0)):
                e.wait_ge(sp, pm)
                e.wait_ge(sd, lm)
                if ti >= 2:
                    e.wait_ge(sg, stw)
                    if xw:
                        e.wait_ge(sv, xw)
                e.tensor_add(ot[par][:, 0:512], ps_o[0][:], xt[par][:, 0:512]).then_inc(sv, 1)
                e.tensor_add(ot[par][:, 512:1024], ps_o[1][:], xt[par][:, 512:1024]).then_inc(sv, 1)
            S.add("vector", doev2)
            ot_m = n["sv"]
            n["sg"] += 16
            st_marks[ti] = n["sg"]
            def dost(e, par=par, ti=ti, m=ot_m):
                e.wait_ge(sv, m)
                e.dma_start(out_buf[ti * 128:(ti + 1) * 128, :], ot[par][:]).then_inc(sg, 16)
            S.add("gpsimd", dost)
            if ln_fm_out:
                base = ot_m
                n["sa"] += 3
                sq_marks[ti] = n["sa"] - 2
                def doln_a(e, par=par, base=base, sq=n["sa"] - 2):
                    e.wait_ge(sv, base)
                    e.activation(x2[:], ot[par][:], AF.Square).then_inc(sa, 1)
                    e.wait_ge(sv, base + 4)
                    e.activation(stats[:, 6:7], stats[:, 5:6], AF.Ln).then_inc(sa, 1)
                    e.wait_ge(sv, base + 5)
                    e.activation(stats[:, 7:8], stats[:, 6:7], AF.Exp).then_inc(sa, 1)
                S.add("scalar", doln_a)
                n["sv"] += 5
                def doln_v(e, par=par, base=base, sq=sq_marks[ti]):
                    e.wait_ge(sv, base)
                    e.reduce_sum(stats[:, 0:1], ot[par][:], AX.X).then_inc(sv, 1)
                    e.wait_ge(sv, base + 1)
                    e.tensor_scalar_mul(stats[:, 2:3], stats[:, 0:1], 1.0 / D)
                    e.wait_ge(sa, sq)
                    e.reduce_sum(stats[:, 1:2], x2[:], AX.X).then_inc(sv, 1)
                    e.wait_ge(sv, base + 2)
                    e.tensor_scalar_mul(stats[:, 3:4], stats[:, 1:2], 1.0 / D)
                    e.tensor_mul(stats[:, 4:5], stats[:, 2:3], stats[:, 2:3]).then_inc(sv, 1)
                    e.wait_ge(sv, base + 3)
                    e.scalar_tensor_tensor(stats[:, 5:6], stats[:, 3:4], 1e-5,
                                           stats[:, 4:5], ALU.add, ALU.subtract).then_inc(sv, 1)
                    e.wait_ge(sa, sq + 1)
                    e.tensor_scalar_mul(stats[:, 6:7], stats[:, 6:7], -0.5).then_inc(sv, 1)
                S.add("vector", doln_v)
                n["sv"] += 1
                xnb_marks[ti] = n["sv"]
                def doxn(e, par=par, am=sq_marks[ti] + 2, ti=ti, m=n["sv"],
                         tw=(trl_marks.get(ti - 2, 0))):
                    e.wait_ge(sa, am)
                    if ti >= 2:
                        e.wait_ge(sp, tw)
                    e.tensor_scalar(xnb[par][:], ot[par][:], stats[:, 2:3],
                                    stats[:, 7:8], ALU.subtract, ALU.mult).then_inc(sv, 1)
                S.add("vector", doxn)
                n["sp"] += 1
                trl_marks[ti] = n["sp"]
                def dotr1(e, par=par, vm=xnb_marks[ti], ti=ti,
                          fw_=(fm_marks.get(ti - 2, 0))):
                    e.wait_ge(sv, vm)
                    if ti >= 2:
                        e.wait_ge(sv, fw_)
                    settle(e, psdm[0:1, 0:1], idb[:, 0:1], k=2)
                    for c in range(8):
                        e.transpose(ps_t1[par][:, c * 128:(c + 1) * 128],
                                    xnb[par][:, c * 128:(c + 1) * 128], idb[:])
                    e.matmul(psdm[0:1, 0:1], idb[:, 0:1], idb[:, 0:1],
                             start=True, stop=True).then_inc(sp, 1)
                S.add("tensor", dotr1)
                n["sv"] += 1
                fm_marks[ti] = n["sv"]
                def dofm(e, par=par, pm=trl_marks[ti], ti=ti,
                         sw_=(stf_marks.get(ti - 2, 0))):
                    e.wait_ge(sp, pm)
                    if ti >= 2:
                        e.wait_ge(sg, sw_)
                    e.tensor_copy(fmb[par][:], ps_t1[par][:]).then_inc(sv, 1)
                S.add("vector", dofm)
                n["sg"] += 16
                stf_marks[ti] = n["sg"]
                def dostf(e, par=par, ti=ti, m=fm_marks[ti]):
                    e.wait_ge(sv, m)
                    dst = H[ln_fm_out]
                    dap = bass.AP(dst.tensor, ti * 128,
                                  [[T, 128], [128 * T, 8], [1, 128]])
                    e.dma_start(dap, fmb[par].ap().rearrange("p (c t) -> p c t", c=8)).then_inc(sg, 16)
                S.add("gpsimd", dostf)

        def fin(e, tot=n["sg"]):
            e.wait_ge(sg, tot)
        S.add("gpsimd", fin)
        S.run(nc)
    return



def ph_mqkv(nc, H, tag, l, xf_buf, qf_buf, kf_buf, vf_buf):
    with ExitStack() as st:
        P = Phase(nc, st, tag)
        hbf = P.sb("hbf", [128, 8 * T], BF16)
        wst = P.sb("wst", [128, 8192], F32)
        wqk = P.sb("wqk", [128, 8192], BF16)
        wv = P.sb("wv", [128, 4096], BF16)
        cA = P.sb("cA", [64, T], F32)
        cB = P.sb("cB", [64, T], F32)
        qsc = P.sb("qsc", [64, 2], F32)
        q_sb = P.sb("q_sb", [64, 2048], F32)
        qsq = P.sb("qsq", [64, 2048], F32)
        rtmp = P.sb("rtmp", [64, 2048], F32)
        qout = P.sb("qout", [64, 2048], BF16)
        nsq = P.sb("nsq", [1, 2048], F32)
        rn = P.sb("rn", [1, 2048], F32)
        ones64 = P.sb("ones64", [64, 1], F32)
        ones1 = P.sb("ones1", [1, 128], F32)
        ve = [P.sb(f"ve{i}", [128, 512], BF16) for i in range(2)]
        ps_q = [P.ps(f"psq{i}", [64, 1024]) for i in range(2)]
        ps_n = P.ps("psn", [1, 512])
        ps_b = P.ps("psb", [64, 1024])
        ps_v = P.ps("psv", [128, 512])
        sd = P.sem("sd"); sw = P.sem("sw"); sp = P.sem("sp")
        sv = P.sem("sv"); sa = P.sem("sa"); sg = P.sem("sg")
        S = Sched()
        n = dict(sd=0, sw=0, sp=0, sv=0, sa=0, sg=0)

        def load(dst, src_ap, waits=()):
            n["sd"] += 16
            def f(e, d=dst, s=src_ap, w=tuple(waits)):
                for sem, val in w:
                    e.wait_ge(sem, val)
                e.dma_start(d, s).then_inc(sd, 16)
            S.add("sync", f)
            return n["sd"]

        load(ones64[:], H["ones64c_f32"])
        load(ones1[:], H["ones_f32"])
        for c in range(8):
            load(wst[:, c * 1024:(c + 1) * 1024],
                 H["mf_wqkv"][l, c * 128:(c + 1) * 128, 0:1024])
        m = n["sd"]
        for c in range(8):
            n["sw"] += 1
            def cst(e, c=c, first=(c == 0), m=m):
                if first:
                    e.wait_ge(sd, m)
                ecopy(e, wqk[:, c * 1024:(c + 1) * 1024],
                      wst[:, c * 1024:(c + 1) * 1024]).then_inc(sw, 1)
            S.add("scalar", cst)
        qk_cast = n["sw"]
        for c in range(8):
            pw = [(sw, qk_cast)] if c == 0 else []
            load(wst[:, c * 512:(c + 1) * 512],
                 H["mf_wqkv"][l, c * 128:(c + 1) * 128, 1024:1536], pw)
        m = n["sd"]
        for c in range(8):
            n["sw"] += 1
            def cst2(e, c=c, first=(c == 0), m=m):
                if first:
                    e.wait_ge(sd, m)
                ecopy(e, wv[:, c * 512:(c + 1) * 512],
                      wst[:, c * 512:(c + 1) * 512]).then_inc(sw, 1)
            S.add("scalar", cst2)
        v_cast = n["sw"]
        prev_hc = v_cast
        for c in range(8):
            pw = [(sw, prev_hc)]
            m = load(wst[:, 0:T + 1], xf_buf[c * 128:(c + 1) * 128, :], pw)
            nincs = 2 if c >= 4 else 1
            n["sw"] += nincs
            prev_hc = n["sw"]
            off = 1 if c < 4 else 0
            def hcst(e, c=c, m=m, off=off, fin=prev_hc):
                e.wait_ge(sd, m)
                if c >= 4:
                    e.memset(wst[:, 0:1], 0.0).then_inc(sw, 1)
                    e.wait_ge(sw, fin - 1)
                e.tensor_copy(hbf[:, c * T:(c + 1) * T],
                              wst[:, off:off + T]).then_inc(sw, 1)
            S.add("vector", hcst)
        h_cast = n["sw"]
        all_ld = n["sd"]

        evq_m = [0, 0]
        side_done = 0
        prev_rot = 0
        prev_qst = 0
        for side in range(2):
            cnames = ("rot_cs", "rot_ss") if side == 0 else ("rot_ci", "rot_si")
            sname = "mf_qs" if side == 0 else "mf_ks"
            obuf = qf_buf if side == 0 else kf_buf
            wcol = side * 512
            guard = [(sv, side_done)] if side else []
            load(cA[:], H[cnames[0]], guard)
            load(cB[:], H[cnames[1]])
            load(qsc[:, 0:1], bass.AP(H[sname].tensor, l * 128, [[1, 64], [1, 1]]))
            lq2 = load(qsc[:, 1:2], bass.AP(H[sname].tensor, l * 128 + 32, [[1, 64], [1, 1]]))
            n["sv"] += 2
            cdone = n["sv"]
            def doconst(e, lds=lq2, m=n["sv"], mul2=(8.0 if side == 0 else None)):
                e.wait_ge(sd, lds)
                if mul2 is not None:
                    e.tensor_scalar(cA[:], cA[:], qsc[:, 0:1], mul2,
                                    ALU.mult, ALU.mult).then_inc(sv, 1)
                    e.tensor_scalar(cB[:], cB[:], qsc[:, 1:2], mul2,
                                    ALU.mult, ALU.mult).then_inc(sv, 1)
                else:
                    e.tensor_scalar(cA[:], cA[:], qsc[:, 0:1], None,
                                    ALU.mult).then_inc(sv, 1)
                    e.tensor_scalar(cB[:], cB[:], qsc[:, 1:2], None,
                                    ALU.mult).then_inc(sv, 1)
                e.wait_ge(sv, m)
            S.add("vector", doconst)
            for h in range(HF):
                for hh in range(2):
                    tok0 = hh * 2048
                    qm = []
                    for quar in range(2):
                        n["sp"] += 1
                        qm.append(n["sp"])
                        def domm(e, h=h, quar=quar, tok0=tok0, wcol=wcol,
                                 first=(side == 0 and h == 0 and hh == 0 and quar == 0),
                                 ev=evq_m[quar], m=n["sp"]):
                            if first:
                                e.wait_ge(sd, all_ld)
                                e.wait_ge(sw, h_cast)
                            if ev:
                                e.wait_ge(sa, ev)
                            settle(e, ps_v[0:1, 0:1], wqk[:, 0:1], k=2)
                            for nq in range(2):
                                for c in range(8):
                                    i = e.matmul(
                                        ps_q[quar][:, nq * 512:(nq + 1) * 512],
                                        wqk[:, c * 1024 + wcol + h * 64:
                                            c * 1024 + wcol + h * 64 + 64],
                                        hbf[:, c * T + tok0 + quar * 1024 + nq * 512:
                                            c * T + tok0 + quar * 1024 + (nq + 1) * 512],
                                        start=(c == 0), stop=(c == 7))
                            i.then_inc(sp, 1)
                        S.add("tensor", domm)
                    for quar in range(2):
                        n["sa"] += 1
                        evq_m[quar] = n["sa"]
                        def doev(e, quar=quar, pm=qm[quar], am=n["sa"], rw=prev_rot):
                            e.wait_ge(sp, pm)
                            if rw:
                                e.wait_ge(sv, rw)
                            e.activation(q_sb[:, quar * 1024:(quar + 1) * 1024],
                                         ps_q[quar][:], AF.Copy).then_inc(sa, 1)
                            e.wait_ge(sa, am)
                        S.add("scalar", doev)
                    n["sa"] += 1
                    sq_m = n["sa"]
                    def dosq(e):
                        e.activation(qsq[:], q_sb[:], AF.Square).then_inc(sa, 1)
                    S.add("scalar", dosq)
                    nev = []
                    for j in range(4):
                        n["sp"] += 1
                        pmj = n["sp"]
                        def donm(e, j=j, am=sq_m, m=pmj,
                                 ew=(nev[j - 1] if j >= 1 else 0)):
                            if j == 0:
                                e.wait_ge(sa, am)
                            else:
                                e.wait_ge(sv, ew)
                            settle(e, ps_v[0:1, 0:1], wqk[:, 0:1], k=1)
                            e.matmul(ps_n[0:1, :], ones64[:],
                                     qsq[:, j * 512:(j + 1) * 512],
                                     start=True, stop=True).then_inc(sp, 1)
                        S.add("tensor", donm)
                        n["sv"] += 1
                        nev.append(n["sv"])
                        def donev(e, j=j, pm=pmj, m=n["sv"]):
                            e.wait_ge(sp, pm)
                            e.tensor_copy(nsq[0:1, j * 512:(j + 1) * 512],
                                          ps_n[0:1, :]).then_inc(sv, 1)
                        S.add("vector", donev)
                    n["sa"] += 1
                    sqrt_m = n["sa"]
                    def dosqrt(e, vm=nev[-1]):
                        e.wait_ge(sv, vm)
                        e.activation(rn[:], nsq[:], AF.Sqrt).then_inc(sa, 1)
                    S.add("scalar", dosqrt)
                    n["sv"] += 2
                    rec_m = n["sv"]
                    def dorec(e, am=sqrt_m, m=n["sv"]):
                        e.wait_ge(sa, am)
                        e.tensor_scalar_max(rn[:], rn[:], 1e-12).then_inc(sv, 1)
                        e.wait_ge(sv, m - 1)
                        e.reciprocal(rn[:], rn[:]).then_inc(sv, 1)
                        e.wait_ge(sv, m)
                    S.add("vector", dorec)
                    qh_m = []
                    for quar in range(2):
                        n["sp"] += 1
                        bm = n["sp"]
                        def dobc(e, quar=quar, rm=rec_m, m=bm,
                                 qw=(qh_m[quar - 1] if quar else 0)):
                            e.wait_ge(sv, rm)
                            if quar:
                                e.wait_ge(sv, qw)
                            settle(e, ps_v[0:1, 0:1], wqk[:, 0:1], k=2)
                            for j in range(2):
                                i = e.matmul(ps_b[:, j * 512:(j + 1) * 512],
                                             ones1[0:1, 0:64],
                                             rn[0:1, quar * 1024 + j * 512:
                                                quar * 1024 + (j + 1) * 512],
                                             start=True, stop=True)
                            i.then_inc(sp, 1)
                        S.add("tensor", dobc)
                        n["sv"] += 1
                        qh_m.append(n["sv"])
                        def doqh(e, quar=quar, pm=bm, m=n["sv"]):
                            e.wait_ge(sp, pm)
                            e.tensor_mul(q_sb[:, quar * 1024:(quar + 1) * 1024],
                                         q_sb[:, quar * 1024:(quar + 1) * 1024],
                                         ps_b[:]).then_inc(sv, 1)
                            e.wait_ge(sv, m)
                        S.add("vector", doqh)
                    n["sv"] += 5
                    rot_done = n["sv"]
                    prev_rot = rot_done
                    def dorot(e, tok0=tok0, m=rot_done, gw=prev_qst):
                        e.tensor_scalar_mul(rtmp[0:32, :], q_sb[32:64, :], -1.0).then_inc(sv, 1)
                        e.tensor_copy(rtmp[32:64, :], q_sb[0:32, :]).then_inc(sv, 1)
                        e.wait_ge(sv, m - 3)
                        e.tensor_mul(q_sb[:], q_sb[:],
                                     cA[:, tok0:tok0 + 2048]).then_inc(sv, 1)
                        e.tensor_mul(rtmp[:], rtmp[:],
                                     cB[:, tok0:tok0 + 2048]).then_inc(sv, 1)
                        e.wait_ge(sv, m - 1)
                        if gw:
                            e.wait_ge(sg, gw)
                        e.tensor_add(qout[:], q_sb[:], rtmp[:]).then_inc(sv, 1)
                        e.wait_ge(sv, m)
                    S.add("vector", dorot)
                    n["sg"] += 16
                    prev_qst = n["sg"]
                    def dostq(e, obuf=obuf, h=h, tok0=tok0, vm=rot_done):
                        e.wait_ge(sv, vm)
                        e.dma_start(obuf[h, :, tok0:tok0 + 2048],
                                    qout[:]).then_inc(sg, 16)
                    S.add("gpsimd", dostq)
            side_done = n["sv"]
        qk_stores = n["sg"]

        vev_m = {}
        for ti in range(NT):
            par = ti % 2
            n["sp"] += 1
            vm = n["sp"]
            def dovm(e, ti=ti, m=vm, ew=vev_m.get(ti - 1, 0)):
                if ew:
                    e.wait_ge(sv, ew)
                for c in range(8):
                    i = e.matmul(ps_v[:],
                                 hbf[:, c * T + ti * 128: c * T + (ti + 1) * 128],
                                 wv[:, c * 512:(c + 1) * 512],
                                 start=(c == 0), stop=(c == 7))
                i.then_inc(sp, 1)
            S.add("tensor", dovm)
            n["sv"] += 1
            vev_m[ti] = n["sv"]
            def dovev(e, par=par, pm=vm, ti=ti, base=qk_stores):
                e.wait_ge(sp, pm)
                if ti >= 2:
                    e.wait_ge(sg, base + 16 * (ti - 1))
                e.tensor_copy(ve[par][:], ps_v[:]).then_inc(sv, 1)
            S.add("vector", dovev)
            n["sg"] += 16
            def dovst(e, par=par, ti=ti, vm2=vev_m[ti]):
                e.wait_ge(sv, vm2)
                e.dma_start(vf_buf[ti], ve[par][:]).then_inc(sg, 16)
            S.add("gpsimd", dovst)

        def fin(e, tot=n["sg"]):
            e.wait_ge(sg, tot)
        S.add("gpsimd", fin)
        S.run(nc)
    return



MWO_GROUPS = 8
MWO_PARTS = 3

def ph_mwo(nc, H, tag, l, xf_in, xf_out, aof_buf):
    with ExitStack() as st:
        P = Phase(nc, st, tag)
        wst = P.sb("wst", [128, 4096], F32)
        wo_bf = P.sb("wo_bf", [128, 4096], BF16)
        at = [P.sb(f"at{i}", [128, 512], BF16) for i in range(2)]
        aT = P.sb("aT", [128, 2048], BF16)
        xsl = [P.sb(f"xsl{i}", [128, 512], F32) for i in range(2)]
        osl = [P.sb(f"osl{i}", [128, 512], F32) for i in range(2)]
        idb = P.sb("idb", [128, 128], BF16)
        ps_t = [P.ps(f"ps_t{i}", [128, 512], BF16) for i in range(2)]
        ps_u = [P.ps(f"psu{i}", [128, 512]) for i in range(2)]
        psdm = P.ps("psdm", [1, 1], F32)
        sd = P.sem("sd"); sw = P.sem("sw"); sp = P.sem("sp")
        sv = P.sem("sv"); sg = P.sem("sg")
        S = Sched()
        n = dict(sd=0, sw=0, sp=0, sv=0, sg=0)

        def load(dst, src_ap, waits=()):
            n["sd"] += 16
            def f(e, d=dst, s=src_ap, w=tuple(waits)):
                for sem, val in w:
                    e.wait_ge(sem, val)
                e.dma_start(d, s).then_inc(sd, 16)
            S.add("sync", f)
            return n["sd"]

        def xf_slice_ap(buf, oc, g):
            return bass.AP(buf.tensor, oc * 128 * (T + 1) + 1 + g * 512,
                           [[T + 1, 128], [1, 512]])

        load(idb[:], H["ident_bf"])
        for c in range(4):
            load(wst[:, c * 1024:(c + 1) * 1024],
                 H["mf_wo"][l, c * 128:(c + 1) * 128, :])
        m = n["sd"]
        for c in range(4):
            n["sw"] += 1
            def cst(e, c=c, first=(c == 0), m=m):
                if first:
                    e.wait_ge(sd, m)
                ecopy(e, wo_bf[:, c * 1024:(c + 1) * 1024],
                      wst[:, c * 1024:(c + 1) * 1024]).then_inc(sw, 1)
            S.add("scalar", cst)
        w_cast = n["sw"]

        tr_m = {}; evT_m = {}; um_m = {}; add_m = {}; st_m = {}; evT_prev = {}
        for g in range(MWO_GROUPS):
            for tl in range(4 if MWO_PARTS & 1 else 0):
                ti = g * 4 + tl
                par = ti % 2
                lw = [(sp, tr_m[ti - 2])] if ti >= 2 else []
                lm = load(at[par][:], aof_buf[ti], lw)
                n["sp"] += 1
                tr_m[ti] = n["sp"]
                def dotr2(e, par=par, tl=tl, lm=lm, ti=ti,
                          ew=(evT_prev.get(ti - 2, 0))):
                    e.wait_ge(sd, lm)
                    if ew:
                        e.wait_ge(sv, ew)
                    for c in range(4):
                        e.transpose(ps_t[ti % 2][:, c * 128:(c + 1) * 128],
                                    at[par][:, c * 128:(c + 1) * 128], idb[:])
                    e.matmul(psdm[0:1, 0:1], idb[:, 0:1], idb[:, 0:1],
                             start=True, stop=True).then_inc(sp, 1)
                S.add("tensor", dotr2)
                n["sv"] += 1
                evT_m[(g, tl)] = n["sv"]
                def doevT(e, tl=tl, ti=ti, pm=tr_m[ti], m=n["sv"],
                          uw=(um_m.get(g - 1, 0) if tl == 0 else 0)):
                    e.wait_ge(sp, pm)
                    if uw:
                        e.wait_ge(sp, uw)
                    for c in range(4):
                        i = e.tensor_copy(
                            aT[:, c * 512 + tl * 128: c * 512 + (tl + 1) * 128],
                            ps_t[ti % 2][:, c * 128:(c + 1) * 128])
                    i.then_inc(sv, 1)
                S.add("vector", doevT)
                evT_prev[ti] = n["sv"]
            aT_done = n["sv"]
            for oc in range(8 if MWO_PARTS & 2 else 0):
                par = oc % 2
                prev_add = (add_m.get((g, oc - 2), 0) if oc >= 2 else
                            add_m.get((g - 1, 6 + oc), 0))
                prev_st = (st_m.get((g, oc - 2), 0) if oc >= 2 else
                           st_m.get((g - 1, 6 + oc), 0))
                lw = [(sv, prev_add)] if prev_add else []
                lm = load(xsl[par][:], xf_slice_ap(xf_in, oc, g), lw)
                n["sp"] += 1
                um = n["sp"]
                def doum(e, par=par, oc=oc, am=aT_done, m=um,
                         first=(g == 0 and oc == 0), aw=prev_add):
                    if am:
                        e.wait_ge(sv, am)
                    if first:
                        e.wait_ge(sw, w_cast)
                    if aw:
                        e.wait_ge(sv, aw)
                    settle(e, psdm[0:1, 0:1], idb[:, 0:1])
                    for c in range(4):
                        i = e.matmul(ps_u[par][:],
                                     wo_bf[:, c * 1024 + oc * 128: c * 1024 + (oc + 1) * 128],
                                     aT[:, c * 512:(c + 1) * 512],
                                     start=(c == 0), stop=(c == 3))
                    i.then_inc(sp, 1)
                S.add("tensor", doum)
                um_m[g] = n["sp"]
                n["sv"] += 1
                add_m[(g, oc)] = n["sv"]
                def doadd(e, par=par, pm=um, lm=lm, m=n["sv"], gw=prev_st):
                    e.wait_ge(sp, pm)
                    e.wait_ge(sd, lm)
                    if gw:
                        e.wait_ge(sg, gw)
                    e.tensor_add(osl[par][:], ps_u[par][:], xsl[par][:]).then_inc(sv, 1)
                S.add("vector", doadd)
                n["sg"] += 16
                st_m[(g, oc)] = n["sg"]
                def dostx(e, par=par, oc=oc, g=g, m=add_m[(g, oc)]):
                    e.wait_ge(sv, m)
                    e.dma_start(xf_slice_ap(xf_out, oc, g), osl[par][:]).then_inc(sg, 16)
                S.add("gpsimd", dostx)

        def fin(e, tot=n["sg"]):
            e.wait_ge(sg, tot)
        S.add("gpsimd", fin)
        S.run(nc)
    return



MFFN_CAST_DVE = True

def ph_mffn(nc, H, tag, l, xf_in, xf_out):
    HID = 2730
    NHC = 22
    with ExitStack() as st:
        P = Phase(nc, st, tag)
        hst = P.sb("hst", [128, 8 * 1024], F32)
        hbfc = P.sb("hbfc", [128, 512], BF16)
        sqc = P.sb("sqc", [128, 512], BF16)
        xb = P.sb("xb", [128, 8 * 1024], BF16)
        mrow = P.sb("mrow", [1, 1024], F32)
        qrow = P.sb("qrow", [1, 1024], F32)
        vrow = P.sb("vrow", [1, 1024], F32)
        rrow = P.sb("rrow", [1, 1024], F32)
        lg = P.sb("lg", [128, 8], F32)
        lb = P.sb("lb", [128, 8], F32)
        b2 = P.sb("b2", [128, 8], F32)
        wst = P.sb("wstf", [128, 4096], F32)
        wa = P.sb("wa", [128, 4096], BF16)
        wg = P.sb("wg", [128, 4096], BF16)
        w2st = P.sb("w2st", [128, NHC * 128], F32)
        w2b = P.sb("w2b", [128, NHC * 128], BF16)
        b1r = P.sb("b1r", [1, 5460], F32)
        gel = P.sb("gel", [128, 512], BF16)
        h4 = P.sb("h4", [128, 512], BF16)
        h4T = P.sb("h4T", [128, NHC * 1024], BF16)
        xsl = [P.sb(f"xsl{i}", [128, 512], F32) for i in range(2)]
        osl = [P.sb(f"osl{i}", [128, 512], F32) for i in range(2)]
        ones128b = P.sb("ones128b", [128, 1], BF16)
        dbgag = P.sb("dbgag", [128, 2048], F32) if "dbg_ag" in H else None
        ones1 = P.sb("ones1", [1, 128], F32)
        idb = P.sb("idb", [128, 128], BF16)
        ps_m = P.ps("ps_m", [1, 512])
        ps_q = P.ps("ps_q", [1, 512])
        ps_ag = [P.ps(f"psag{i}", [128, 512]) for i in range(2)]
        ps_t = [P.ps(f"pstf{i}", [128, 1024], BF16) for i in range(2)]
        ps_u = [P.ps(f"psuf{i}", [128, 512]) for i in range(2)]
        sd = P.sem("sd"); sw = P.sem("sw"); sp = P.sem("sp")
        sv = P.sem("sv"); sa = P.sem("sa"); sg = P.sem("sg")
        S = Sched()
        n = dict(sd=0, sw=0, sp=0, sv=0, sa=0, sg=0)

        def load(dst, src_ap, waits=()):
            n["sd"] += 16
            def f(e, d=dst, s=src_ap, w=tuple(waits)):
                for sem, val in w:
                    e.wait_ge(sem, val)
                e.dma_start(d, s).then_inc(sd, 16)
            S.add("sync", f)
            return n["sd"]

        load(idb[:], H["ident_bf"])
        load(ones1[:], H["ones_f32"])
        load(ones128b[:], H["ones128c_bf"])
        load(lg[:], H["mf_lgT"][l])
        load(lb[:], H["mf_lbT"][l])
        load(b2[:], H["mf_b2T"][l])
        load(b1r[:], bass.AP(H["mf_b1"].tensor, l * 5460, [[1, 1], [1, 5460]]))

        def xf_ap(buf, oc, coloff, width):
            return bass.AP(buf.tensor, oc * 128 * (T + 1) + coloff,
                           [[T + 1, 128], [1, width]])

        stats_done = {}
        stats_read = [0]
        h4_mark = [0]
        trp_mark = [0]
        evh = {}
        upd_done = {}
        add_m = {}
        st_m = {}
        w1cast_last = [0]
        w2cast = {}
        upd_m = {}

        for g in range(4):
            g0 = g * 1024
            hld = 0
            for c in range(8):
                off = 1 + g0 - (1 if c >= 4 else 0)
                w_ = [(sv, stats_done.get(g - 1, 0))] if (g and c == 0) else []
                hld = load(hst[:, c * 1024:(c + 1) * 1024], xf_ap(xf_in, c, off, 1024), w_)
            zs = 0
            if g == 0:
                n["sw"] += 1
                zs = n["sw"]
                def dozero(e, m=hld, sm=n["sw"]):
                    e.wait_ge(sd, m)
                    for c in range(4, 8):
                        i = e.memset(hst[:, c * 1024: c * 1024 + 1], 0.0)
                    i.then_inc(sw, 1)
                    e.wait_ge(sw, sm)
                S.add("vector", dozero)
            sub_rstd = []
            prev_stats_v = stats_read[0]
            for sb2 in range(2):
                so = sb2 * 512
                stat_pm = 0
                for c in range(8):
                    n["sw"] += 1
                    cm = n["sw"]
                    def docst(e, c=c, so=so, m=hld, zs=zs, cm=cm, pw=stat_pm):
                        if c == 0:
                            e.wait_ge(sd, m)
                            if zs:
                                e.wait_ge(sw, zs)
                        if pw:
                            e.wait_ge(sp, pw)
                        e.tensor_copy(hbfc[:], hst[:, c * 1024 + so: c * 1024 + so + 512]).then_inc(sw, 1)
                    S.add("vector" if MFFN_CAST_DVE else "gpsimd", docst)
                    n["sa"] += 1
                    am = n["sa"]
                    def dosqa(e, c=c, so=so, pw=stat_pm, am=am, m=hld, zs=zs):
                        if c == 0:
                            e.wait_ge(sd, m)
                            if zs:
                                e.wait_ge(sw, zs)
                        if pw:
                            e.wait_ge(sp, pw)
                        e.activation(sqc[:], hst[:, c * 1024 + so: c * 1024 + so + 512],
                                     AF.Square).then_inc(sa, 1)
                    S.add("scalar", dosqa)
                    n["sp"] += 1
                    stat_pm = n["sp"]
                    def dostat(e, c=c, cm=cm, am=am, m=stat_pm, pv=prev_stats_v):
                        e.wait_ge(sw, cm)
                        e.wait_ge(sa, am)
                        if c == 0 and pv:
                            e.wait_ge(sv, pv)
                        settle(e, ps_ag[0][0:1, 0:1], ones128b[:], k=2)
                        e.matmul(ps_m[0:1, :], ones128b[:], hbfc[:],
                                 start=(c == 0), stop=(c == 7))
                        e.matmul(ps_q[0:1, :], ones128b[:], sqc[:],
                                 start=(c == 0), stop=(c == 7)).then_inc(sp, 1)
                    S.add("tensor", dostat)
                n["sv"] += 4
                vm = n["sv"]
                prev_stats_v = vm - 2
                def dostats(e, so=so, pm=stat_pm, m=vm):
                    e.wait_ge(sp, pm)
                    e.tensor_scalar_mul(mrow[0:1, so:so + 512], ps_m[0:1, :], 1.0 / D).then_inc(sv, 1)
                    e.tensor_scalar_mul(qrow[0:1, so:so + 512], ps_q[0:1, :], 1.0 / D).then_inc(sv, 1)
                    e.wait_ge(sv, m - 2)
                    e.tensor_mul(vrow[0:1, so:so + 512], mrow[0:1, so:so + 512],
                                 mrow[0:1, so:so + 512]).then_inc(sv, 1)
                    e.wait_ge(sv, m - 1)
                    e.scalar_tensor_tensor(vrow[0:1, so:so + 512], qrow[0:1, so:so + 512],
                                           1e-5, vrow[0:1, so:so + 512],
                                           ALU.subtract if False else ALU.add, ALU.subtract).then_inc(sv, 1)
                    e.wait_ge(sv, m)
                S.add("vector", dostats)
                n["sa"] += 1
                lnm = n["sa"]
                def doln(e, so=so, vm=vm, am=lnm):
                    e.wait_ge(sv, vm)
                    e.activation(rrow[0:1, so:so + 512], vrow[0:1, so:so + 512],
                                 AF.Ln).then_inc(sa, 1)
                S.add("scalar", doln)
                n["sv"] += 1
                hm = n["sv"]
                def dohalf(e, so=so, am=lnm, m=hm):
                    e.wait_ge(sa, am)
                    e.tensor_scalar_mul(rrow[0:1, so:so + 512],
                                        rrow[0:1, so:so + 512], -0.5).then_inc(sv, 1)
                S.add("vector", dohalf)
                n["sa"] += 1
                em = n["sa"]
                def doexp(e, so=so, vm=hm, am=em):
                    e.wait_ge(sv, vm)
                    e.activation(rrow[0:1, so:so + 512], rrow[0:1, so:so + 512],
                                 AF.Exp).then_inc(sa, 1)
                S.add("scalar", doexp)
                sub_rstd.append(em)
                stats_read[0] = vm
            last_norm = 0
            for sb2 in range(2):
                so = sb2 * 512
                n["sp"] += 1
                bm = n["sp"]
                def dobc(e, so=so, am=sub_rstd[sb2], m=bm, xw=last_norm):
                    e.wait_ge(sa, am)
                    if xw:
                        e.wait_ge(sv, xw)
                    e.matmul(ps_ag[0][:], ones1[:], mrow[0:1, so:so + 512],
                             start=True, stop=True)
                    e.matmul(ps_ag[1][:], ones1[:], rrow[0:1, so:so + 512],
                             start=True, stop=True).then_inc(sp, 1)
                S.add("tensor", dobc)
                for c in range(8):
                    n["sv"] += 3
                    last_norm = n["sv"]
                    def donorm(e, c=c, so=so, pm=bm, m=n["sv"]):
                        e.wait_ge(sp, pm)
                        e.tensor_sub(hst[:, c * 1024 + so: c * 1024 + so + 512],
                                     hst[:, c * 1024 + so: c * 1024 + so + 512],
                                     ps_ag[0][:]).then_inc(sv, 1)
                        e.wait_ge(sv, m - 2)
                        e.tensor_mul(hst[:, c * 1024 + so: c * 1024 + so + 512],
                                     hst[:, c * 1024 + so: c * 1024 + so + 512],
                                     ps_ag[1][:]).then_inc(sv, 1)
                        e.wait_ge(sv, m - 1)
                        e.tensor_scalar(xb[:, c * 1024 + so: c * 1024 + so + 512],
                                        hst[:, c * 1024 + so: c * 1024 + so + 512],
                                        lg[:, c:c + 1], lb[:, c:c + 1],
                                        ALU.mult, ALU.add).then_inc(sv, 1)
                        e.wait_ge(sv, m)
                    S.add("vector", donorm)
            stats_done[g] = n["sv"]
            xb_done = n["sv"]
            if g == 0 and "dbg_xb" in H:
                n["sg"] += 32
                def dodbg(e, m=xb_done):
                    e.wait_ge(sv, m)
                    e.dma_start(H["dbg_xb"], xb[:]).then_inc(sg, 16)
                    e.dma_start(H["dbg_rows"][0:1, :], mrow[:]).then_inc(sg, 16)
                S.add("gpsimd", dodbg)
                n["sg"] += 32
                def dodbg2(e, m=xb_done):
                    e.wait_ge(sv, m)
                    e.dma_start(H["dbg_rows"][1:2, :], vrow[:]).then_inc(sg, 16)
                    e.dma_start(H["dbg_rows"][2:3, :], rrow[:]).then_inc(sg, 16)
                S.add("gpsimd", dodbg2)

            blocks = [(b * 512, min(512, HID - b * 512)) for b in range(6)]
            for bi, (bo, bw) in enumerate(blocks):
                for half_w, (wdst, wcoff) in enumerate([(wa, 0), (wg, 2730)]):
                    lw1 = 0
                    for c in range(8):
                        w_ = [(sw, w1cast_last[0])] if c == 0 else []
                        lw1 = load(wst[:, c * 512:c * 512 + bw],
                                   bass.AP(H["mf_w1"].tensor,
                                           l * 1024 * 5460 + (c * 128) * 5460 + wcoff + bo,
                                           [[5460, 128], [1, bw]]), w_)
                    for c in range(8):
                        n["sw"] += 1
                        def dcw(e, c=c, m=lw1, first=(c == 0), bw=bw, wdst=wdst):
                            if first:
                                e.wait_ge(sd, m)
                            ecopy(e, wdst[:, c * 512:c * 512 + bw],
                              wst[:, c * 512:c * 512 + bw]).then_inc(sw, 1)
                        S.add("scalar", dcw)
                    w1cast_last[0] = n["sw"]
                wc = n["sw"]
                for tl in range(8):
                    n["sp"] += 1
                    agm = n["sp"]
                    def doag(e, tl=tl, bo=bo, bw=bw, xm=xb_done, wm=wc, m=agm,
                             hw=h4_mark[0], first=(tl == 0)):
                        if first:
                            e.wait_ge(sv, xm)
                            e.wait_ge(sw, wm)
                        if hw:
                            e.wait_ge(sv, hw)
                        settle(e, ps_m[0:1, 0:1], ones128b[:])
                        for c in range(8):
                            e.matmul(ps_ag[0][:, 0:bw],
                                     xb[:, c * 1024 + tl * 128: c * 1024 + (tl + 1) * 128],
                                     wa[:, c * 512:c * 512 + bw],
                                     start=(c == 0), stop=False)
                        e.matmul(ps_ag[0][:, 0:bw], ones1[:],
                                 b1r[0:1, bo:bo + bw], start=False, stop=True)
                        for c in range(8):
                            e.matmul(ps_ag[1][:, 0:bw],
                                     xb[:, c * 1024 + tl * 128: c * 1024 + (tl + 1) * 128],
                                     wg[:, c * 512:c * 512 + bw],
                                     start=(c == 0), stop=False)
                        e.matmul(ps_ag[1][:, 0:bw], ones1[:],
                                 b1r[0:1, 2730 + bo:2730 + bo + bw],
                                 start=False, stop=True).then_inc(sp, 1)
                    S.add("tensor", doag)
                    n["sa"] += 1
                    gm = n["sa"]
                    def dogel(e, pm=agm, am=gm, bw=bw):
                        e.wait_ge(sp, pm)
                        e.activation(gel[:, 0:bw], ps_ag[1][:, 0:bw],
                                     AF.Gelu).then_inc(sa, 1)
                    S.add("scalar", dogel)
                    n["sv"] += 1
                    h4_mark[0] = n["sv"]
                    dbg_first = (g == 0 and bi == 0 and tl == 0 and "dbg_ag" in H)
                    def doh4(e, am=gm, m=n["sv"], bw=bw, tw=trp_mark[0], dbg=dbg_first):
                        e.wait_ge(sa, am)
                        if tw:
                            e.wait_ge(sp, tw)
                        if dbg:
                            e.tensor_copy(dbgag[:, 0:512], ps_ag[0][:])
                            e.tensor_copy(dbgag[:, 512:1024], ps_ag[1][:])
                            e.tensor_copy(dbgag[:, 1024:1536], gel[:])
                        e.tensor_mul(h4[:, 0:bw], ps_ag[0][:, 0:bw],
                                     gel[:, 0:bw]).then_inc(sv, 1)
                        if dbg:
                            e.wait_ge(sv, m)
                            e.tensor_copy(dbgag[:, 1536:2048], h4[:])
                    S.add("vector", doh4)
                    nch = (bw + 127) // 128
                    seq = (g * 6 + bi) * 8 + tl
                    n["sp"] += 1
                    trp_mark[0] = n["sp"]
                    def dotrh(e, bw=bw, nch=nch, hm=h4_mark[0], m=n["sp"],
                              seq=seq, ew=evh.get(seq - 2, 0)):
                        e.wait_ge(sv, hm)
                        if ew:
                            e.wait_ge(sv, ew)
                        settle(e, ps_m[0:1, 0:1], ones128b[:])
                        for j in range(nch):
                            cw = min(128, bw - j * 128)
                            e.transpose(ps_t[seq % 2][0:cw, j * 128: j * 128 + 128],
                                        h4[:, j * 128: j * 128 + cw], idb[:])
                        e.matmul(ps_m[0:1, 0:1], ones128b[:], ones128b[:],
                                 start=True, stop=True).then_inc(sp, 1)
                    S.add("tensor", dotrh)
                    n["sv"] += 1
                    evh[seq] = n["sv"]
                    def doevh(e, tl=tl, bo=bo, bw=bw, nch=nch, seq=seq,
                              pm=trp_mark[0], m=n["sv"],
                              uw=(upd_done.get(g - 1, 0) if (bi == 0 and tl == 0 and g) else 0)):
                        e.wait_ge(sp, pm)
                        if uw:
                            e.wait_ge(sp, uw)
                        i = None
                        for j in range(nch):
                            cw = min(128, bw - j * 128)
                            hc = bo // 128 + j
                            i = e.tensor_copy(
                                h4T[0:cw, hc * 1024 + tl * 128: hc * 1024 + (tl + 1) * 128],
                                ps_t[seq % 2][0:cw, j * 128: j * 128 + 128])
                        i.then_inc(sv, 1)
                    S.add("vector", doevh)
            h4_all = n["sv"]
            if g == 0 and "dbg_ag" in H:
                n["sg"] += 16
                def dodbg4(e, m=h4_all):
                    e.wait_ge(sv, m)
                    e.dma_start(H["dbg_ag"], dbgag[:]).then_inc(sg, 16)
                S.add("gpsimd", dodbg4)
            if g == 0 and "dbg_h4T" in H:
                n["sg"] += 16
                def dodbg3(e, m=h4_all):
                    e.wait_ge(sv, m)
                    e.dma_start(H["dbg_h4T"], h4T[:]).then_inc(sg, 16)
                S.add("gpsimd", dodbg3)

            for oc in range(8):
                lw2 = 0
                for hc in range(NHC):
                    kw = 128 if hc < 21 else 42
                    w_ = []
                    if hc == 0 and (g or oc):
                        w_ = [(sw, w2cast.get((g, oc - 1), 0) or w2cast.get((g - 1, 7), 0))]
                    lw2 = load(w2st[0:kw, hc * 128:(hc + 1) * 128],
                               bass.AP(H["mf_w2"].tensor,
                                       l * 2730 * 1024 + (hc * 128) * 1024 + oc * 128,
                                       [[1024, kw], [1, 128]]), w_)
                n["sw"] += 1
                w2cast[(g, oc)] = n["sw"]
                def dc2(e, m=lw2, sm=n["sw"], uw=upd_m.get((g, oc - 1), 0) or upd_m.get((g - 1, 7), 0)):
                    e.wait_ge(sd, m)
                    if uw:
                        e.wait_ge(sp, uw)
                    ecopy(e, w2b[:], w2st[:]).then_inc(sw, 1)
                S.add("scalar", dc2)
                for half in range(2):
                    s = (g * 8 + oc) * 2 + half
                    par = s % 2
                    prev_add = add_m.get(s - 2, 0)
                    prev_st = st_m.get(s - 2, 0)
                    lw = [(sv, prev_add)] if prev_add else []
                    lm = load(xsl[par][:], xf_ap(xf_in, oc, 1 + g0 + half * 512, 512), lw)
                    n["sp"] += 1
                    um = n["sp"]
                    upd_m[(g, oc)] = um
                    def doum(e, par=par, half=half, hm=h4_all, wm=w2cast[(g, oc)],
                             m=um, aw=prev_add):
                        e.wait_ge(sv, hm)
                        e.wait_ge(sw, wm)
                        if aw:
                            e.wait_ge(sv, aw)
                        for hc in range(NHC):
                            kw = 128 if hc < 21 else 42
                            i = e.matmul(ps_u[par][:],
                                         w2b[0:kw, hc * 128:(hc + 1) * 128],
                                         h4T[0:kw, hc * 1024 + half * 512:
                                             hc * 1024 + (half + 1) * 512],
                                         start=(hc == 0), stop=(hc == NHC - 1))
                        i.then_inc(sp, 1)
                    S.add("tensor", doum)
                    n["sv"] += 1
                    add_m[s] = n["sv"]
                    def doadd(e, par=par, oc=oc, pm=um, lm=lm, m=n["sv"], gw=prev_st):
                        e.wait_ge(sp, pm)
                        e.wait_ge(sd, lm)
                        if gw:
                            e.wait_ge(sg, gw)
                        e.scalar_tensor_tensor(osl[par][:], ps_u[par][:],
                                               b2[:, oc:oc + 1], xsl[par][:],
                                               ALU.add, ALU.add).then_inc(sv, 1)
                    S.add("vector", doadd)
                    n["sg"] += 16
                    st_m[s] = n["sg"]
                    def dost2(e, par=par, oc=oc, g0=g0, half=half, m=add_m[s]):
                        e.wait_ge(sv, m)
                        e.dma_start(xf_ap(xf_out, oc, 1 + g0 + half * 512, 512),
                                    osl[par][:]).then_inc(sg, 16)
                    S.add("gpsimd", dost2)
            upd_done[g] = n["sp"]

        def fin(e, tot=n["sg"]):
            e.wait_ge(sg, tot)
        S.add("gpsimd", fin)
        S.run(nc)
    return



def ph_final(nc, H, tag, y_buf, xf_buf, out_buf):
    with ExitStack() as st:
        P = Phase(nc, st, tag)
        xf_sb = P.sb("xf_sb", [128, 8 * (T + 1)], F32)
        xst = [P.sb(f"xst{i}", [128, D], F32) for i in range(2)]
        yt = [P.sb(f"yt{i}", [128, D], F32) for i in range(2)]
        ot = [P.sb(f"ot{i}", [128, D], F32) for i in range(2)]
        idf = P.sb("idf", [128, 128], F32)
        ps_t = [P.ps(f"pst{i}", [128, D]) for i in range(2)]
        ps_dm = P.ps("ps_dm", [1, 1])
        sd = P.sem("sd"); sp = P.sem("sp"); sv = P.sem("sv"); sg = P.sem("sg")
        S = Sched()
        n = dict(sd=0, sp=0, sv=0, sg=0)

        def load(dst, src_ap, waits=()):
            n["sd"] += 16
            def f(e, d=dst, s=src_ap, w=tuple(waits)):
                for sem, val in w:
                    e.wait_ge(sem, val)
                e.dma_start(d, s).then_inc(sd, 16)
            S.add("sync", f)
            return n["sd"]

        load(idf[:], H["ident_f32"])
        for c in range(8):
            load(xf_sb[:, c * (T + 1):(c + 1) * (T + 1)],
                 xf_buf[c * 128:(c + 1) * 128, :])
        xf_ld = n["sd"]
        ev_m = {}; st_m = {}; tr_m = {}
        for ti in range(NT):
            par = ti % 2
            lw = [(sv, ev_m[ti - 2])] if ti >= 2 else []
            lm = load(yt[par][:], y_buf[ti * 128:(ti + 1) * 128, :], lw)
            n["sv"] += 1
            stg_m = n["sv"]
            def dostg(e, par=par, ti=ti, first=(ti == 0), m=n["sv"],
                      tw=tr_m.get(ti - 2, 0)):
                if first:
                    e.wait_ge(sd, xf_ld)
                if tw:
                    e.wait_ge(sp, tw)
                for c in range(8):
                    i = e.tensor_copy(xst[par][:, c * 128:(c + 1) * 128],
                                      xf_sb[:, c * (T + 1) + 1 + ti * 128:
                                            c * (T + 1) + 1 + (ti + 1) * 128])
                i.then_inc(sv, 1)
            S.add("vector", dostg)
            n["sp"] += 1
            tr_m[ti] = n["sp"]
            def dotr(e, par=par, ti=ti, m=tr_m[ti], sm=stg_m,
                     ew=ev_m.get(ti - 2, 0)):
                e.wait_ge(sv, sm)
                if ew:
                    e.wait_ge(sv, ew)
                settle(e, ps_dm[0:1, 0:1], idf[:, 0:1], k=2)
                for c in range(8):
                    e.transpose(ps_t[par][:, c * 128:(c + 1) * 128],
                                xst[par][:, c * 128:(c + 1) * 128], idf[:])
                e.matmul(ps_dm[0:1, 0:1], idf[:, 0:1], idf[:, 0:1],
                         start=True, stop=True).then_inc(sp, 1)
            S.add("tensor", dotr)
            n["sv"] += 1
            ev_m[ti] = n["sv"]
            def doev(e, par=par, pm=tr_m[ti], lm=lm, ti=ti, m=n["sv"],
                     gw=st_m.get(ti - 2, 0)):
                e.wait_ge(sp, pm)
                e.wait_ge(sd, lm)
                if gw:
                    e.wait_ge(sg, gw)
                e.tensor_add(ot[par][:], ps_t[par][:], yt[par][:]).then_inc(sv, 1)
            S.add("vector", doev)
            n["sg"] += 16
            st_m[ti] = n["sg"]
            def dost(e, par=par, ti=ti, m=ev_m[ti]):
                e.wait_ge(sv, m)
                e.dma_start(out_buf[ti * 128:(ti + 1) * 128, :], ot[par][:]).then_inc(sg, 16)
            S.add("gpsimd", dost)

        def fin(e, tot=n["sg"]):
            e.wait_ge(sg, tot)
        S.add("gpsimd", fin)
        S.run(nc)
    return



PHASES = [
    ("ln0", lambda nc, H: ph_ln_to_fm(nc, H, "x0", "xn0b", "ln0"),
     ["ident_bf"], ["x0"], ["xn0b"]),
    ("qkv0", lambda nc, H: ph_qkv01(nc, H, "qkv0", H["xn0b"], H["l0_wqT"],
                                    H["l0_wkT"], H["l0_wvT"], H["q0"], H["k0"], H["v0"]),
     ["l0_wqT", "l0_wkT", "l0_wvT", "ones_f32"], ["xn0b"], ["q0", "k0", "v0"]),
    ("attn0", lambda nc, H: ph_attn(nc, H, "attn0", H["q0"], H["k0"], H["v0"],
                                    H["ao0"], ATTN0_CFG),
     ["ident_bf"] + [f"m0_{i}" for i in range(6)], ["q0", "k0", "v0"], ["ao0"]),
]

ATTN0_CFG = AttnCfg(nh=16, v_cols=1024, scale=0.125, ranges=_l0_ranges,
                    bias=("mask", lambda qt: (f"m0_{min(qt, 5)}", 0)),
                    head_flush=8, lw=768)

ATTN1_CFG = AttnCfg(nh=16, v_cols=1024, scale=0.125,
                    ranges=lambda qt: [(0, T)], bias=("E", "ebias"),
                    head_flush=8, lw=T)

PHASES.append(
    ("comb0", lambda nc, H: ph_proj(nc, H, "comb0", H["ao0"], "l0_woT",
                                    H["x0"], H["x1"], bias_name="l0_bias",
                                    l0ffn=("l0_fw", "l0_alpha", H["xn0b"]),
                                    ln_fm_out="xn1b"),
     ["l0_woT", "l0_fw", "l0_alpha", "l0_bias", "ident_bf", "ones_f32",
      "ones128c_f32"], ["x0", "ao0", "xn0b"], ["x1", "xn1b"]))
PHASES.append(
    ("qkv1", lambda nc, H: ph_qkv01(nc, H, "qkv1", H["xn1b"], H["l1_wq"],
                                    H["l1_wk"], H["l1_wv"], H["q1"], H["k1"],
                                    H["v1"], bqk=("l1_bq", "l1_bk"), bv="l1_bv"),
     ["l1_wq", "l1_wk", "l1_wv", "l1_bq", "l1_bk", "l1_bv", "ones_f32"],
     ["xn1b"], ["q1", "k1", "v1"]))
PHASES.append(
    ("attn1", lambda nc, H: ph_attn(nc, H, "attn1", H["q1"], H["k1"], H["v1"],
                                    H["ao1"], ATTN1_CFG),
     ["ident_bf", "ebias"], ["q1", "k1", "v1"], ["ao1"]))
PHASES.append(
    ("out1", lambda nc, H: ph_proj(nc, H, "out1", H["ao1"], "l1_wo",
                                   H["x1"], H["y"], bias_name="l1_bo"),
     ["l1_wo", "l1_bo", "ident_bf", "ones_f32", "ones128c_f32"],
     ["x1", "ao1"], ["y"]))

MATTN_CFG = AttnCfg(nh=8, v_cols=512, scale=1.0, ranges=_mini_ranges,
                    bias=("mask", lambda qt: (f"mw_{qt % 4}", 512 if qt < 4 else 0)),
                    head_flush=8, lw=1024)

_ROT = ["rot_cs", "rot_ss", "rot_ci", "rot_si"]
PHASES.append(
    ("mini_init", lambda nc, H: ph_ln_to_fm(nc, H, "x1", "xf_a", "mini_init",
                                            out_f32=True),
     ["ident_f32"], ["x1"], ["xf_a"]))
for _l in (0, 1):
    _xin, _xmid = ("xf_a", "xf_b")
    PHASES.append(
        (f"mqkv{_l}", (lambda l, xin: lambda nc, H: ph_mqkv(
            nc, H, f"mqkv{l}", l, H[xin], H["qf"], H["kf"], H["vf"]))(_l, _xin),
         ["mf_wqkv", "mf_qs", "mf_ks", "ones64c_f32", "ones_f32"] + _ROT,
         [_xin], ["qf", "kf", "vf"]))
    PHASES.append(
        (f"mattn{_l}", (lambda l: lambda nc, H: ph_attn(
            nc, H, f"mattn{l}", H["qf"], H["kf"], H["vf"], H["aof"], MATTN_CFG))(_l),
         ["ident_bf"] + [f"mw_{i}" for i in range(4)], ["qf", "kf", "vf"], ["aof"]))
    PHASES.append(
        (f"mwo{_l}", (lambda l, xin, xout: lambda nc, H: ph_mwo(
            nc, H, f"mwo{l}", l, H[xin], H[xout], H["aof"]))(_l, _xin, _xmid),
         ["mf_wo", "ident_bf"], [_xin, "aof"], [_xmid]))
    PHASES.append(
        (f"mffn{_l}", (lambda l, xin, xout: lambda nc, H: ph_mffn(
            nc, H, f"mffn{l}", l, H[xin], H[xout]))(_l, _xmid, _xin),
         ["mf_w1", "mf_b1", "mf_w2", "mf_b2T", "mf_lgT", "mf_lbT",
          "ident_bf", "ones_f32", "ones128c_bf"], [_xmid], [_xin]))
PHASES.append(
    ("final", lambda nc, H: ph_final(nc, H, "final", H["y"], H["xf_a"], H["out"]),
     ["ident_f32"], ["y", "xf_a"], ["out"]))


def _build_phase_nc(name, emit, shared_in, buf_in, buf_out):
    nc = bass.Bass("TRN2", target_bir_lowering=False)
    H = {}
    for nm in shared_in:
        shape, dt = SHARED_SPECS[nm]
        H[nm] = nc.dram_tensor(nm, list(shape), dt, kind="ExternalInput").ap()
    for nm in buf_in:
        shape, dt = BUF_SPECS[nm]
        H[nm] = nc.dram_tensor(nm, list(shape), dt, kind="ExternalInput").ap()
    for nm in buf_out:
        shape, dt = BUF_SPECS[nm]
        H[nm] = nc.dram_tensor(nm, list(shape), dt, kind="ExternalOutput").ap()
    emit(nc, H)
    return nc


def run_phases(inputs, phases=None, state=None):
    phases = phases if phases is not None else PHASES
    shared = prep_shared_inputs(inputs)
    x0_full = np.ascontiguousarray(np.asarray(inputs["L0_attn_x"], np.float32))
    state = state if state is not None else {}
    state["x0"] = [np.ascontiguousarray(x0_full[b]) for b in range(NC_CORES)]
    for (name, emit, shared_in, buf_in, buf_out) in phases:
        nc = _build_phase_nc(name, emit, shared_in, buf_in, buf_out)
        in_maps = []
        for b in range(NC_CORES):
            m = {nm: shared[nm] for nm in shared_in}
            for nm in buf_in:
                m[nm] = state[nm][b]
            in_maps.append(m)
        res = run_bass_kernel_spmd(nc, in_maps, core_ids=list(range(NC_CORES)))
        for nm in buf_out:
            state[nm] = [np.asarray(res.results[b][nm]) for b in range(NC_CORES)]
    return state


def _fused_used_shared():
    names = []
    for (_, _, shared_in, _, _) in PHASES:
        for nm in shared_in:
            if nm not in names:
                names.append(nm)
    return names


def build_fused():
    nc = bass.Bass("TRN2", target_bir_lowering=False)
    H = {}
    for nm in _fused_used_shared():
        shape, dt = SHARED_SPECS[nm]
        H[nm] = nc.dram_tensor(nm, list(shape), dt, kind="ExternalInput").ap()
    for nm, (shape, dt) in BUF_SPECS.items():
        kind = ("ExternalInput" if nm == "x0" else
                "ExternalOutput" if nm == "out" else "Internal")
        H[nm] = nc.dram_tensor(nm, list(shape), dt, kind=kind).ap()
    with ExitStack() as st:
        bar = st.enter_context(nc.semaphore("phase_bar"))
        for k, (name, emit, shared_in, buf_in, buf_out) in enumerate(PHASES):
            emit(nc, H)
            with nc.Block() as blk:
                for eng in Sched.ENGINES:
                    def f(e, k=k):
                        e.nop().then_inc(bar, 1)
                        e.wait_ge(bar, 5 * (k + 1))
                    getattr(blk, eng)(f)
    return nc


def kernel_fused(**inputs):
    shared = prep_shared_inputs(inputs)
    x0_full = np.ascontiguousarray(np.asarray(inputs["L0_attn_x"], np.float32))
    nc = build_fused()
    used = _fused_used_shared()
    in_maps = []
    for b in range(NC_CORES):
        m = {nm: shared[nm] for nm in used}
        m["x0"] = np.ascontiguousarray(x0_full[b])
        in_maps.append(m)
    res = run_bass_kernel_spmd(nc, in_maps, core_ids=list(range(NC_CORES)))
    return np.stack([np.asarray(res.results[b]["out"]) for b in range(NC_CORES)], axis=0)


def kernel(**inputs):
    return kernel_fused(**inputs)


# version 04 (final)
# speedup vs baseline: 1.1850x; 1.1850x over previous
import numpy as np
import ml_dtypes
from contextlib import ExitStack

import concourse.bass as bass
import concourse.mybir as mybir
from concourse.bass_utils import run_bass_kernel_spmd

F32 = mybir.dt.float32
BF16 = mybir.dt.bfloat16
AF = mybir.ActivationFunctionType
ALU = mybir.AluOpType
AX = mybir.AxisListType

T = 4096
D = 1024
NT = T // 128
NC_CORES = 8
H0 = 16
H1 = 16
HF = 8
DH = 64
NEG = -1e30


def _build_l0_masks():
    masks = {}
    for tc in range(6):
        t = tc if tc < 5 else 5
        k0 = max(0, 128 * (t - 4))
        wlen = 128 * (t - max(0, t - 4) + 1)
        m = np.full((128, 128 + wlen), NEG, np.float32)
        m[:, 0:16] = 0.0
        i = 128 * t + np.arange(128)[:, None]
        j = k0 + np.arange(wlen)[None, :]
        valid = (j >= 16) & (i >= 16) & (i - j >= 0) & (i - j <= 512)
        m[:, 128:][valid] = 0.0
        masks[tc] = m
    return masks


def _build_wf_masks():
    mw = np.full((4, 128, 1024), NEG, np.float32)
    for qt in range(4):
        r = np.arange(128)[:, None]
        c = np.arange(1024)[None, :]
        valid = (c >= 128 * qt + r) & (c <= 512 + 128 * qt + r)
        mw[qt][valid] = 0.0
    return mw


def _build_ebias(rel_bias):
    E = np.empty((16, 128, 8064), np.float32)
    for h in range(16):
        r = np.asarray(rel_bias[:, h], np.float32)
        rpad = np.concatenate([np.repeat(r[:1], 3071), r, np.repeat(r[-1:], 3071)])
        rpadf = rpad[::-1]
        for i in range(128):
            E[h, i, :] = rpadf[127 - i: 127 - i + 8064]
    return E


def _build_rotary():
    half_idx = np.arange(0, 64, 2, dtype=np.float64)
    inv_freq = 1.0 / 10000.0 ** (half_idx / 64)
    tt = np.arange(T, dtype=np.float64)
    freqs = tt[:, None] * inv_freq[None, :]
    freqs = np.concatenate([freqs, freqs], -1)
    base = (half_idx + 0.4 * 64) / (1.4 * 64)
    power = (tt - T // 2) / 256.0
    scale = base[None, :] ** power[:, None]
    scale = np.concatenate([scale, scale], -1)
    cs = (np.cos(freqs) * scale).T.astype(np.float32)
    ss = (np.sin(freqs) * scale).T.astype(np.float32)
    ci = (np.cos(freqs) / scale).T.astype(np.float32)
    si = (np.sin(freqs) / scale).T.astype(np.float32)
    return cs, ss, ci, si


def prep_shared_inputs(ins):
    c = {}
    Wqkv0 = np.asarray(ins["L0_attn_W_qkv"], np.float32)
    c["l0_wqT"] = np.ascontiguousarray(Wqkv0[0:1024].T)
    c["l0_wkT"] = np.ascontiguousarray(Wqkv0[1024:2048].T)
    c["l0_wvT"] = np.ascontiguousarray(Wqkv0[2048:3072].T)
    c["l0_woT"] = np.ascontiguousarray(np.asarray(ins["L0_attn_W_out"], np.float32).T)
    c["l0_fw"] = np.asarray(ins["L0_ffn_weight"], np.float32)
    c["l0_alpha"] = np.asarray(ins["L0_ffn_alpha"], np.float32).reshape(1, 1)
    c["l0_bias"] = np.asarray(ins["L0_ffn_bias"], np.float32).reshape(1, D)
    Wqkv1 = np.asarray(ins["L1_attn_W_qkv"], np.float32)
    c["l1_wq"] = np.ascontiguousarray(Wqkv1[:, 0:1024])
    c["l1_wk"] = np.ascontiguousarray(Wqkv1[:, 1024:2048])
    c["l1_wv"] = np.ascontiguousarray(Wqkv1[:, 2048:3072])
    bq = np.asarray(ins["L1_attn_b_qkv"], np.float32)
    c["l1_bq"] = np.ascontiguousarray(bq[0:1024].reshape(16, 64).T)
    c["l1_bk"] = np.ascontiguousarray(bq[1024:2048].reshape(16, 64).T)
    c["l1_bv"] = np.ascontiguousarray(bq[2048:3072].reshape(1, D))
    c["l1_wo"] = np.asarray(ins["L1_attn_W_out"], np.float32)
    c["l1_bo"] = np.asarray(ins["L1_attn_b_out"], np.float32).reshape(1, D)
    c["ebias"] = _build_ebias(np.asarray(ins["L1_attn_rel_bias"], np.float32))
    c["mf_wqkv"] = np.asarray(ins["L1_ffn_wqkv"], np.float32)
    qs = np.asarray(ins["L1_ffn_q_scale"], np.float32)
    ks = np.asarray(ins["L1_ffn_k_scale"], np.float32)
    c["mf_qs"] = np.concatenate([qs, qs], axis=1)
    c["mf_ks"] = np.concatenate([ks, ks], axis=1)
    c["mf_wo"] = np.asarray(ins["L1_ffn_wo"], np.float32)
    c["mf_lgT"] = np.ascontiguousarray(
        np.asarray(ins["L1_ffn_ln_g"], np.float32).reshape(2, 8, 128).transpose(0, 2, 1))
    c["mf_lbT"] = np.ascontiguousarray(
        np.asarray(ins["L1_ffn_ln_b"], np.float32).reshape(2, 8, 128).transpose(0, 2, 1))
    c["mf_w1"] = np.asarray(ins["L1_ffn_w1"], np.float32)
    c["mf_b1"] = np.asarray(ins["L1_ffn_b1"], np.float32)
    c["mf_w2"] = np.asarray(ins["L1_ffn_w2"], np.float32)
    c["mf_b2T"] = np.ascontiguousarray(
        np.asarray(ins["L1_ffn_b2"], np.float32).reshape(2, 8, 128).transpose(0, 2, 1))
    m0 = _build_l0_masks()
    for tc in range(6):
        c[f"m0_{tc}"] = m0[tc]
    mw = _build_wf_masks()
    for i in range(4):
        c[f"mw_{i}"] = np.ascontiguousarray(mw[i])
    cs, ss, ci, si = _build_rotary()
    c["rot_cs"] = cs; c["rot_ss"] = ss; c["rot_ci"] = ci; c["rot_si"] = si
    c["ident_bf"] = np.eye(128, dtype=ml_dtypes.bfloat16)
    c["ident_f32"] = np.eye(128, dtype=np.float32)
    c["ones_f32"] = np.ones((1, 128), np.float32)
    c["ones128c_bf"] = np.ones((128, 1), ml_dtypes.bfloat16)
    c["ones64c_f32"] = np.ones((64, 1), np.float32)
    c["ones128c_f32"] = np.ones((128, 1), np.float32)
    return c


SHARED_SPECS = {
    "l0_wqT": ((1024, 1024), F32), "l0_wkT": ((1024, 1024), F32),
    "l0_wvT": ((1024, 1024), F32), "l0_woT": ((1024, 1024), F32),
    "l0_fw": ((1024, 1024), F32), "l0_alpha": ((1, 1), F32), "l0_bias": ((1, D), F32),
    "l1_wq": ((1024, 1024), F32), "l1_wk": ((1024, 1024), F32), "l1_wv": ((1024, 1024), F32),
    "l1_bq": ((64, 16), F32), "l1_bk": ((64, 16), F32), "l1_bv": ((1, D), F32),
    "l1_wo": ((1024, 1024), F32), "l1_bo": ((1, D), F32),
    "ebias": ((16, 128, 8064), F32),
    "mf_wqkv": ((2, 1024, 1536), F32), "mf_qs": ((2, 128), F32), "mf_ks": ((2, 128), F32),
    "mf_wo": ((2, 512, 1024), F32), "mf_lgT": ((2, 128, 8), F32), "mf_lbT": ((2, 128, 8), F32),
    "mf_w1": ((2, 1024, 5460), F32), "mf_b1": ((2, 5460), F32),
    "mf_w2": ((2, 2730, 1024), F32), "mf_b2T": ((2, 128, 8), F32),
    "m0_0": ((128, 128 + 128), F32), "m0_1": ((128, 128 + 256), F32),
    "m0_2": ((128, 128 + 384), F32), "m0_3": ((128, 128 + 512), F32),
    "m0_4": ((128, 128 + 640), F32), "m0_5": ((128, 128 + 640), F32),
    "mw_0": ((128, 1024), F32), "mw_1": ((128, 1024), F32),
    "mw_2": ((128, 1024), F32), "mw_3": ((128, 1024), F32),
    "rot_cs": ((64, T), F32), "rot_ss": ((64, T), F32),
    "rot_ci": ((64, T), F32), "rot_si": ((64, T), F32),
    "ident_bf": ((128, 128), BF16), "ident_f32": ((128, 128), F32),
    "ones_f32": ((1, 128), F32), "ones128c_bf": ((128, 1), BF16),
    "ones64c_f32": ((64, 1), F32), "ones128c_f32": ((128, 1), F32),
}

BUF_SPECS = {
    "x0": ((T, D), F32),
    "xn0b": ((D, T), BF16),
    "q0": ((H0, 64, T), BF16), "k0": ((H0, 64, T), BF16), "v0": ((NT, 128, D), BF16),
    "ao0": ((NT, 128, D), BF16),
    "x1": ((T, D), F32),
    "xn1b": ((D, T), BF16),
    "q1": ((H1, 64, T), BF16), "k1": ((H1, 64, T), BF16), "v1": ((NT, 128, D), BF16),
    "ao1": ((NT, 128, D), BF16),
    "y": ((T, D), F32),
    "xf_a": ((D, T + 1), F32),
    "xf_b": ((D, T + 1), F32),
    "qf": ((HF, 64, T), BF16), "kf": ((HF, 64, T), BF16), "vf": ((NT, 128, 512), BF16),
    "aof": ((NT, 128, 512), BF16),
    "out": ((T, D), F32),
}



class Phase:

    def __init__(self, nc, st, tag):
        self.nc = nc
        self.st = st
        self.tag = tag
        self.n = {}

    def sb(self, name, shape, dt):
        return self.st.enter_context(self.nc.sbuf_tensor(f"{self.tag}_{name}", shape, dt))

    def ps(self, name, shape, dt=F32):
        return self.st.enter_context(self.nc.psum_tensor(f"{self.tag}_{name}", shape, dt))

    def sem(self, name):
        s = self.st.enter_context(self.nc.semaphore(f"{self.tag}_{name}"))
        self.n[name] = 0
        return s

    def inc(self, name, amt=1):
        self.n[name] += amt
        return self.n[name]



def ph_ln_to_fm(nc, H, src_name, dst_name, tag, out_f32=False):
    src = H[src_name]
    dst = H[dst_name]
    ODT = F32 if out_f32 else BF16
    dst_off = 1 if out_f32 else 0
    dst_stride = dst.shape[1]
    with ExitStack() as st:
        P = Phase(nc, st, tag)
        xt = [P.sb(f"xt{i}", [128, D], F32) for i in range(2)]
        x2 = [P.sb(f"x2{i}", [128, D], F32) for i in range(2)]
        xnb = [P.sb(f"xnb{i}", [128, D], ODT) for i in range(2)]
        fmb = [P.sb(f"fmb{i}", [128, D], ODT) for i in range(2)]
        stats = P.sb("stats", [128, 8], F32)
        idb = P.sb("idb", [128, 128], ODT)
        pst = [P.ps(f"pst{i}", [128, D], ODT) for i in range(2)]
        psdm = P.ps("psdm", [1, 1], F32)
        sd = P.sem("sd"); sa = P.sem("sa"); sv = P.sem("sv")
        sp = P.sem("sp"); sg = P.sem("sg")
        DPT = 2 if "dbg" in H else 1
        with nc.Block() as blk:
            @blk.sync
            def _(sync):
                sync.dma_start(idb[:], H["ident_f32" if out_f32 else "ident_bf"]).then_inc(sd, 16)
                for i in range(NT):
                    if i >= 2:
                        sync.wait_ge(sv, (i - 2) * 7 + 6)
                    sync.dma_start(xt[i % 2][:], src[i * 128:(i + 1) * 128, :]).then_inc(sd, 16)

            @blk.scalar
            def _(scalar):
                for i in range(NT):
                    scalar.wait_ge(sd, 16 * (i + 2))
                    if i >= 2:
                        scalar.wait_ge(sv, (i - 2) * 7 + 2)
                    scalar.activation(x2[i % 2][:], xt[i % 2][:], AF.Square).then_inc(sa, 1)
                    scalar.wait_ge(sv, i * 7 + 4)
                    scalar.activation(stats[:, 6:7], stats[:, 5:6], AF.Ln).then_inc(sa, 1)
                    scalar.wait_ge(sv, i * 7 + 5)
                    scalar.activation(stats[:, 7:8], stats[:, 6:7], AF.Exp).then_inc(sa, 1)

            @blk.vector
            def _(vector):
                for i in range(NT):
                    vector.wait_ge(sd, 16 * (i + 2))
                    vector.reduce_sum(stats[:, 0:1], xt[i % 2][:], AX.X).then_inc(sv, 1)
                    vector.wait_ge(sv, i * 7 + 1)
                    vector.tensor_scalar_mul(stats[:, 2:3], stats[:, 0:1], 1.0 / D)
                    vector.wait_ge(sa, i * 3 + 1)
                    vector.reduce_sum(stats[:, 1:2], x2[i % 2][:], AX.X).then_inc(sv, 1)
                    vector.wait_ge(sv, i * 7 + 2)
                    vector.tensor_scalar_mul(stats[:, 3:4], stats[:, 1:2], 1.0 / D)
                    vector.tensor_mul(stats[:, 4:5], stats[:, 2:3], stats[:, 2:3]).then_inc(sv, 1)
                    vector.wait_ge(sv, i * 7 + 3)
                    vector.scalar_tensor_tensor(stats[:, 5:6], stats[:, 3:4], 1e-5,
                                                stats[:, 4:5], ALU.add, ALU.subtract).then_inc(sv, 1)
                    vector.wait_ge(sa, i * 3 + 2)
                    vector.tensor_scalar_mul(stats[:, 6:7], stats[:, 6:7], -0.5).then_inc(sv, 1)
                    vector.wait_ge(sa, i * 3 + 3)
                    if i >= 2:
                        vector.wait_ge(sp, i - 1)
                    vector.tensor_scalar(xnb[i % 2][:], xt[i % 2][:], stats[:, 2:3],
                                         stats[:, 7:8], ALU.subtract, ALU.mult).then_inc(sv, 1)
                    vector.wait_ge(sp, i + 1)
                    if i >= 2:
                        vector.wait_ge(sg, 16 * (DPT * (i - 2) + 1))
                    vector.tensor_copy(fmb[i % 2][:], pst[i % 2][:]).then_inc(sv, 1)

            @blk.tensor
            def _(tensor):
                tensor.wait_ge(sd, 16)
                for i in range(NT):
                    tensor.wait_ge(sv, i * 7 + 6)
                    if i >= 2:
                        tensor.wait_ge(sv, (i - 2) * 7 + 7)
                    settle(tensor, psdm[0:1, 0:1], idb[:, 0:1], k=2)
                    for cidx in range(8):
                        tensor.transpose(pst[i % 2][:, cidx * 128:(cidx + 1) * 128],
                                         xnb[i % 2][:, cidx * 128:(cidx + 1) * 128], idb[:])
                    tensor.matmul(psdm[0:1, 0:1], idb[:, 0:1], idb[:, 0:1],
                                  start=True, stop=True).then_inc(sp, 1)

            @blk.gpsimd
            def _(gpsimd):
                ndma = 0
                for i in range(NT):
                    gpsimd.wait_ge(sv, i * 7 + 7)
                    dap = bass.AP(dst.tensor, dst_off + i * 128,
                                  [[dst_stride, 128], [128 * dst_stride, 8], [1, 128]])
                    gpsimd.dma_start(dap, fmb[i % 2].ap().rearrange("p (c t) -> p c t", c=8)).then_inc(sg, 16)
                    ndma += 1
                    if "dbg" in H:
                        gpsimd.dma_start(H["dbg"][i * 128:(i + 1) * 128, :], xnb[i % 2][:]).then_inc(sg, 16)
                        ndma += 1
                gpsimd.wait_ge(sg, 16 * ndma)
    return



def settle(e, ps_ap, opa, k=3):
    for _ in range(k):
        e.matmul(ps_ap, opa, opa, start=True, stop=True)


def ecopy(e, out, in_):
    if hasattr(e, "tensor_copy"):
        return e.tensor_copy(out, in_)
    return e.copy(out, in_)


class Sched:
    ENGINES = ("sync", "tensor", "vector", "scalar", "gpsimd")

    def __init__(self):
        self.ops = {e: [] for e in self.ENGINES}

    def add(self, eng, fn):
        self.ops[eng].append(fn)

    def run(self, nc):
        with nc.Block() as blk:
            for eng in self.ENGINES:
                if not self.ops[eng]:
                    continue
                fns = self.ops[eng]
                getattr(blk, eng)(lambda e, fns=fns: [f(e) for f in fns])



def ph_qkv01(nc, H, tag, src, wq, wk, wv, q_out, k_out, v_out,
             bqk=None, bv=None):
    with ExitStack() as st:
        P = Phase(nc, st, tag)
        xn = P.sb("xn", [128, 8 * T], BF16)
        wst = P.sb("wst", [128, 8 * 1024], F32)
        wqb = P.sb("wqb", [128, 8 * 1024], BF16)
        wkb = P.sb("wkb", [128, 8 * 1024], BF16)
        wvb = P.sb("wvb", [128, 8 * 1024], BF16)
        bq_sb = P.sb("bq", [64, 16], F32) if bqk else None
        bk_sb = P.sb("bk", [64, 16], F32) if bqk else None
        bv_sb = P.sb("bv", [1, D], F32) if bv else None
        ones1 = P.sb("ones1", [1, 128], F32) if bv else None
        qe = [P.sb(f"qe{i}", [64, 1024], BF16) for i in range(2)]
        ve = [P.sb(f"ve{i}", [128, D], BF16) for i in range(2)]
        ps_q = [P.ps(f"psq{i}", [64, 1024]) for i in range(2)]
        ps_v = [P.ps(f"psv{i}", [128, 512]) for i in range(2)]
        sd = P.sem("sd"); sw = P.sem("sw"); sp = P.sem("sp"); spv = P.sem("spv")
        sv = P.sem("sv"); sa = P.sem("sa"); sg = P.sem("sg"); so = P.sem("so")
        S = Sched()
        nld = [0]

        def load(dst, src_ap, pre_wait=None):
            nld[0] += 1
            def f(e, d=dst, s=src_ap, pw=pre_wait):
                if pw is not None:
                    e.wait_ge(pw[0], pw[1])
                e.dma_start(d, s).then_inc(sd, 16)
            S.add("sync", f)
            return 16 * nld[0]

        for c in range(8):
            load(xn[:, c * T:(c + 1) * T], src[c * 128:(c + 1) * 128, :])
        for wi, (wraw, wcast) in enumerate([(wq, wqb), (wk, wkb), (wv, wvb)]):
            for c in range(8):
                pw = (sw, 8 * wi) if (wi > 0 and c == 0) else None
                load(wst[:, c * 1024:(c + 1) * 1024],
                     wraw[c * 128:(c + 1) * 128, :], pre_wait=pw)
            lded = 16 * nld[0]
            for c in range(8):
                def cast(e, c=c, wc=wcast, l=lded, first=(c == 0)):
                    if first:
                        e.wait_ge(sd, l)
                    ecopy(e, wc[:, c * 1024:(c + 1) * 1024],
                          wst[:, c * 1024:(c + 1) * 1024]).then_inc(sw, 1)
                S.add("scalar", cast)
        if bqk:
            load(bq_sb[:], H[bqk[0]])
            load(bk_sb[:], H[bqk[1]])
        if bv:
            load(bv_sb[:], H[bv])
            load(ones1[:], H["ones_f32"])
        all_loads = 16 * nld[0]

        for side, (wcast, obuf, bsb) in enumerate(
                [(wqb, q_out, bq_sb), (wkb, k_out, bk_sb)]):
            for h in range(16):
                for quar in range(4):
                    g = side * 64 + h * 4 + quar
                    par = g % 2

                    def pre(e, g=g):
                        if g == 0:
                            e.wait_ge(sd, all_loads)
                            e.wait_ge(sw, 16)
                        if g >= 2:
                            e.wait_ge(sa, g - 1)
                    S.add("tensor", pre)
                    for nq in range(2):
                        for c in range(8):
                            last = (nq == 1 and c == 7)
                            def mm(e, par=par, nq=nq, c=c, h=h, quar=quar,
                                   wcast=wcast, last=last):
                                i = e.matmul(
                                    ps_q[par][:, nq * 512:(nq + 1) * 512],
                                    wcast[:, c * 1024 + h * 64: c * 1024 + h * 64 + 64],
                                    xn[:, c * T + quar * 1024 + nq * 512:
                                       c * T + quar * 1024 + (nq + 1) * 512],
                                    start=(c == 0), stop=(c == 7))
                                if last:
                                    i.then_inc(sp, 1)
                            S.add("tensor", mm)

                    def evac(e, par=par, h=h, bsb=bsb, g=g):
                        e.wait_ge(sp, g + 1)
                        if g >= 2:
                            e.wait_ge(so, 16 * (g - 1))
                        if bsb is not None:
                            e.activation(qe[par][:], ps_q[par][:], AF.Identity,
                                         bias=bsb[:, h:h + 1]).then_inc(sa, 1)
                        else:
                            e.activation(qe[par][:], ps_q[par][:], AF.Identity).then_inc(sa, 1)
                    S.add("scalar", evac)

                    def store(e, par=par, h=h, quar=quar, obuf=obuf, g=g):
                        e.wait_ge(sa, g + 1)
                        e.dma_start(obuf[h, :, quar * 1024:(quar + 1) * 1024],
                                    qe[par][:]).then_inc(so, 16)
                    S.add("gpsimd", store)

        for ti in range(NT):
            par = ti % 2

            def prev(e, ti=ti):
                if ti == 0:
                    e.wait_ge(sw, 24)
                if ti >= 1:
                    e.wait_ge(sv, 2 * ti)
            S.add("tensor", prev)
            for nq in range(2):
                for c in range(8):
                    last = (c == 7 and bv is None)
                    def mmv(e, nq=nq, c=c, ti=ti, last=last):
                        i = e.matmul(ps_v[nq][:],
                                     xn[:, c * T + ti * 128: c * T + (ti + 1) * 128],
                                     wvb[:, c * 1024 + nq * 512: c * 1024 + (nq + 1) * 512],
                                     start=(c == 0), stop=(c == 7))
                        if last:
                            i.then_inc(spv, 1)
                    S.add("tensor", mmv)
                if bv:
                    def mmb(e, nq=nq):
                        e.matmul(ps_v[nq][:], ones1[:],
                                 bv_sb[:, nq * 512:(nq + 1) * 512],
                                 start=False, stop=True).then_inc(spv, 1)
                    S.add("tensor", mmb)

                def vevac(e, par=par, nq=nq, ti=ti):
                    e.wait_ge(spv, 2 * ti + nq + 1)
                    if ti >= 2 and nq == 0:
                        e.wait_ge(sg, 16 * (ti - 1))
                    e.tensor_copy(ve[par][:, nq * 512:(nq + 1) * 512],
                                  ps_v[nq][:]).then_inc(sv, 1)
                S.add("vector", vevac)

            def vstore(e, par=par, ti=ti):
                e.wait_ge(sv, 2 * ti + 2)
                e.dma_start(v_out[ti], ve[par][:]).then_inc(sg, 16)
            S.add("gpsimd", vstore)

        def fin(e):
            e.wait_ge(so, 16 * 128)
            e.wait_ge(sg, 16 * NT)
        S.add("gpsimd", fin)
        S.run(nc)
    return



class AttnCfg:
    def __init__(self, nh, v_cols, scale, ranges, bias, head_flush, lw,
                 qt_list=None):
        self.nh = nh; self.v_cols = v_cols; self.scale = scale
        self.ranges = ranges; self.bias = bias
        self.head_flush = head_flush; self.lw = lw
        self.qt_list = qt_list if qt_list is not None else list(range(NT))


def ph_attn(nc, H, tag, q_buf, k_buf, v_buf, ao_buf, cfg):
    nh, v_cols, lw = cfg.nh, cfg.v_cols, cfg.lw
    hf = cfg.head_flush
    gcols = hf * 64
    use_E = cfg.bias[0] == "E"
    with ExitStack() as st:
        P = Phase(nc, st, tag)
        v_sb = P.sb("v", [128, NT * v_cols], BF16)
        ao_sb = P.sb("ao", [128, NT * gcols], BF16)
        q_sb = P.sb("q", [64, T], BF16)
        k_sb = P.sb("k", [64, T], BF16)
        u_sb = [P.sb(f"u{i}", [128, lw], F32) for i in range(2)]
        p_sb = [P.sb(f"p{i}", [128, lw], BF16) for i in range(2)]
        npt = 1 if use_E else 2
        pt_sb = [P.sb(f"pt{i}", [128, lw], BF16) for i in range(npt)]
        negm = [P.sb(f"negm{i}", [128, 1], F32) for i in range(2)]
        l_sb = [P.sb(f"l{i}", [128, 1], F32) for i in range(2)]
        rl_sb = [P.sb(f"rl{i}", [128, 1], F32) for i in range(2)]
        idb = P.sb("idb", [128, 128], BF16)
        E_sb = P.sb("E", [128, 8064], F32) if use_E else None
        masks = {}
        if not use_E:
            for qt in range(NT):
                mn, _ = cfg.bias[1](qt)
                if mn not in masks:
                    shape = SHARED_SPECS[mn][0]
                    masks[mn] = P.sb(f"msk_{mn}", list(shape), F32)
        ps_s = [P.ps(f"pss{i}", [128, 1024]) for i in range(2)]
        ps_t = [P.ps(f"pst{i}", [128, 1024], BF16) for i in range(2)]
        ps_av = P.ps("psav", [128, 64])
        psdm = P.ps("psdm", [1, 1], F32)
        sd = P.sem("sd"); sp = P.sem("sp"); sv = P.sem("sv")
        sa = P.sem("sa"); sg = P.sem("sg")
        S = Sched()
        n = dict(sd=0, sp=0, sv=0, sa=0, sg=0)

        def load(dst, src_ap, waits=()):
            n["sd"] += 16
            def f(e, d=dst, s=src_ap, w=tuple(waits)):
                for sem, val in w:
                    e.wait_ge(sem, val)
                e.dma_start(d, s).then_inc(sd, 16)
            S.add("sync", f)
            return n["sd"]

        load(idb[:], H["ident_bf"])
        for mn, msb in masks.items():
            load(msb[:], H[mn])
        for ti in range(NT):
            load(v_sb[:, ti * v_cols:(ti + 1) * v_cols], v_buf[ti])

        qk_end = {}; stt_end = {}; exp_m = {}; transp_m = {}; av_m = {}
        ao_m = {}; flush_m = {}
        pss_last_stt = [0, 0]; pst_last_evac = [0, 0]
        spar = [0]; tpar = [0]
        qseq = [0]
        pend = []
        seq_of = {}
        info = {}

        def emit_late(h, qt):
            seq = seq_of[(h, qt)]
            qpar = seq % 2
            ppar = seq % npt
            inf = info[seq]
            W = inf["W"]
            npieces = W // 128
            glist = [list(range(g * 8, min((g + 1) * 8, npieces)))
                     for g in range((npieces + 7) // 8)]
            ptevac_m = []
            for gi, gp in enumerate(glist):
                tp = tpar[0] % 2
                tpar[0] += 1
                twaits = [(sa, exp_m[(h, qt)])]
                if pst_last_evac[tp]:
                    twaits.append((sv, pst_last_evac[tp]))
                n["sp"] += 1
                def dotr(e, gp=tuple(gp), tp=tp, tw=tuple(twaits), qpar=qpar, m=n["sp"]):
                    for sem, val in tw:
                        e.wait_ge(sem, val)
                    settle(e, psdm[0:1, 0:1], idb[:, 0:1], k=2)
                    for j in gp:
                        e.transpose(ps_t[tp][:, (j % 8) * 128:(j % 8 + 1) * 128],
                                    p_sb[qpar][:, j * 128:(j + 1) * 128], idb[:])
                    e.matmul(psdm[0:1, 0:1], idb[:, 0:1], idb[:, 0:1],
                             start=True, stop=True).then_inc(sp, 1)
                S.add("tensor", dotr)
                tr_m = n["sp"]
                n["sv"] += 1
                pst_last_evac[tp] = n["sv"]
                ptevac_m.append(n["sv"])
                pav = av_m.get(seq - npt)
                def doev(e, gp=tuple(gp), tp=tp, tr_m=tr_m, ppar=ppar,
                         first=(gi == 0), pav=pav):
                    e.wait_ge(sp, tr_m)
                    if first and pav:
                        e.wait_ge(sp, pav)
                    e.tensor_copy(pt_sb[ppar][:, gp[0] * 128: (gp[-1] + 1) * 128],
                                  ps_t[tp][:, 0:(len(gp)) * 128]).then_inc(sv, 1)
                S.add("vector", doev)
            transp_m[(h, qt)] = n["sp"]
            vtiles = inf["vtiles"]
            pao = ao_m.get(seq - 1)
            n["sp"] += 1
            av_m[seq] = n["sp"]
            hloc = h % hf
            def doav(e, vtiles=tuple(vtiles), pev=tuple(ptevac_m), h=h,
                     pao=pao, ppar=ppar, m=n["sp"]):
                for j, vt in enumerate(vtiles):
                    if j % 8 == 0:
                        e.wait_ge(sv, pev[j // 8])
                        settle(e, psdm[0:1, 0:1], idb[:, 0:1], k=2)
                    if j == 0 and pao:
                        e.wait_ge(sv, pao)
                    i = e.matmul(ps_av[:], pt_sb[ppar][:, j * 128:(j + 1) * 128],
                                 v_sb[:, vt * v_cols + h * 64: vt * v_cols + h * 64 + 64],
                                 start=(j == 0), stop=(j == len(vtiles) - 1))
                i.then_inc(sp, 1)
            S.add("tensor", doav)
            n["sv"] += 2
            ao_m[seq] = n["sv"]
            fl = inf["flush_wait"]
            def dono(e, qt=qt, hloc=hloc, avm=av_m[seq], qpar=qpar, m=n["sv"], fl=fl):
                e.wait_ge(sp, avm)
                e.reciprocal(rl_sb[qpar][:], l_sb[qpar][:]).then_inc(sv, 1)
                e.wait_ge(sv, m - 1)
                if fl:
                    e.wait_ge(sg, fl)
                e.tensor_scalar_mul(
                    ao_sb[:, qt * gcols + hloc * 64: qt * gcols + hloc * 64 + 64],
                    ps_av[:], rl_sb[qpar][:, 0:1]).then_inc(sv, 1)
            S.add("vector", dono)

        for h in range(nh):
            waits = []
            if h > 0:
                waits.append((sp, qk_end[h - 1]))
            ldm = load(q_sb[:], q_buf[h], waits)
            ldm = load(k_sb[:], k_buf[h])
            if use_E:
                ew = [(sv, stt_end[h - 1])] if h > 0 else []
                ldm = load(E_sb[:], H[cfg.bias[1]][h], ew)
            load_m = ldm

            for qt in cfg.qt_list:
                seq = qseq[0]
                qseq[0] += 1
                seq_of[(h, qt)] = seq
                qpar = seq % 2
                ranges = cfg.ranges(qt)
                W = sum(w for _, w in ranges)
                if use_E:
                    bias_ap = E_sb[:, 3968 - 128 * qt: 3968 - 128 * qt + T]
                else:
                    mn, moff = cfg.bias[1](qt)
                    bias_ap = masks[mn][:, moff:]
                prev_exp_u = exp_m.get(("s", seq - 2))
                pieces = []
                off = 0
                for koff, w in ranges:
                    o2 = 0
                    while o2 < w:
                        so = off + o2
                        pw = min(512 - so % 512, w - o2)
                        pieces.append((so, koff + o2, pw))
                        o2 += pw
                    off += w
                quarters = []
                for so, ko, pw in pieces:
                    qi = so // 1024
                    while len(quarters) <= qi:
                        quarters.append([])
                    quarters[qi].append((so, ko, pw))
                for qi, qp in enumerate(quarters):
                    par = spar[0] % 2
                    spar[0] += 1
                    qoff = qi * 1024
                    qw = sum(pw for _, _, pw in qp)
                    prewaits = []
                    if pss_last_stt[par]:
                        prewaits.append((sv, pss_last_stt[par]))
                    if h == 0 and qt == cfg.qt_list[0] and qi == 0:
                        prewaits.append((sd, n["sd"]))
                    elif qt == cfg.qt_list[0] and qi == 0:
                        prewaits.append((sd, load_m))
                    for pi, (so, ko, pw) in enumerate(qp):
                        last = pi == len(qp) - 1
                        if last:
                            n["sp"] += 1
                        def mm(e, par=par, so=so, ko=ko, pw=pw, qt=qt,
                               qoff=qoff, last=last, prewaits=tuple(prewaits),
                               first=(pi == 0)):
                            if first:
                                for sem, val in prewaits:
                                    e.wait_ge(sem, val)
                            i = e.matmul(ps_s[par][:, so - qoff: so - qoff + pw],
                                         q_sb[:, qt * 128:(qt + 1) * 128],
                                         k_sb[:, ko: ko + pw],
                                         start=True, stop=True)
                            if last:
                                i.then_inc(sp, 1)
                        S.add("tensor", mm)
                    qk_m = n["sp"]
                    n["sv"] += 1
                    pss_last_stt[par] = n["sv"]
                    def stt(e, par=par, qoff=qoff, qw=qw, qi=qi, qpar=qpar,
                            bias_ap=bias_ap, qk_m=qk_m, m=n["sv"],
                            peu=prev_exp_u):
                        e.wait_ge(sp, qk_m)
                        if qi == 0 and peu:
                            e.wait_ge(sa, peu)
                        e.scalar_tensor_tensor(
                            u_sb[qpar][:, qoff: qoff + qw], ps_s[par][:, 0:qw],
                            cfg.scale, bias_ap[:, qoff: qoff + qw],
                            ALU.mult, ALU.add).then_inc(sv, 1)
                    S.add("vector", stt)
                qk_end[h] = n["sp"]
                stt_end[h] = n["sv"]
                n["sv"] += 1
                max_m = n["sv"]
                pe2 = exp_m.get(("s", seq - 2))
                def domax(e, W=W, sm=n["sv"], pe=pe2, qpar=qpar):
                    e.wait_ge(sv, sm - 1)
                    if pe:
                        e.wait_ge(sa, pe)
                    e.reduce_max(negm[qpar][:], u_sb[qpar][:, 0:W], AX.X,
                                 negate=True).then_inc(sv, 1)
                S.add("vector", domax)
                n["sa"] += 1
                exp_m[(h, qt)] = n["sa"]
                exp_m[("s", seq)] = n["sa"]
                prev_tr = transp_m.get(("s", seq - 2))
                prev_rec = ao_m.get(seq - 2)
                def doexp(e, W=W, mm_=max_m, pt_=prev_tr, qpar=qpar, pr=prev_rec):
                    e.wait_ge(sv, mm_)
                    if pt_:
                        e.wait_ge(sp, pt_)
                    if pr:
                        e.wait_ge(sv, pr)
                    e.activation(p_sb[qpar][:, 0:W], u_sb[qpar][:, 0:W], AF.Exp,
                                 bias=negm[qpar][:, 0:1],
                                 accum_out=l_sb[qpar][:, 0:1]).then_inc(sa, 1)
                S.add("scalar", doexp)
                info[seq] = dict(W=W, vtiles=[(koff + o2) // 128
                                              for koff, w in ranges
                                              for o2 in range(0, w, 128)],
                                 flush_wait=(flush_m.get(h // hf - 1)
                                             if (h % hf == 0 and h and qt == cfg.qt_list[0]) else None))
                if pend:
                    ph_, pq_ = pend.pop(0)
                    emit_late(ph_, pq_)
                    transp_m[("s", seq_of[(ph_, pq_)])] = transp_m[(ph_, pq_)]
                pend.append((h, qt))

            if (h + 1) % hf == 0:
                while pend:
                    ph_, pq_ = pend.pop(0)
                    emit_late(ph_, pq_)
                    transp_m[("s", seq_of[(ph_, pq_)])] = transp_m[(ph_, pq_)]
                grp = h // hf
                aodone = n["sv"]
                for ti in range(NT):
                    n["sg"] += 16
                    def dofl(e, ti=ti, grp=grp, aodone=aodone, first=(ti == 0)):
                        if first:
                            e.wait_ge(sv, aodone)
                        e.dma_start(
                            ao_buf[ti, :, grp * gcols:(grp + 1) * gcols],
                            ao_sb[:, ti * gcols:(ti + 1) * gcols]).then_inc(sg, 16)
                    S.add("gpsimd", dofl)
                flush_m[grp] = n["sg"]

        while pend:
            ph_, pq_ = pend.pop(0)
            emit_late(ph_, pq_)
        def fin(e, tot=n["sg"]):
            e.wait_ge(sg, tot)
        S.add("gpsimd", fin)
        S.run(nc)
    return


def _l0_ranges(qt):
    k0 = max(0, 128 * (qt - 4))
    wlen = 128 * (qt - max(0, qt - 4) + 1)
    return [(0, 128), (k0, wlen)]


def _mini_ranges(qt):
    w = qt // 4
    k0 = max(0, 512 * (w - 1))
    return [(k0, 512 * w + 512 - k0)]



def ph_proj(nc, H, tag, ao_buf, wo_name, resid_buf, out_buf, bias_name=None,
            l0ffn=None, ln_fm_out=None):
    with ExitStack() as st:
        P = Phase(nc, st, tag)
        wo_st = P.sb("wo_st", [128, 8 * 1024], F32)
        wo_bf = P.sb("wo_bf", [128, 8 * 1024], BF16)
        xt = [P.sb(f"xt{i}", [128, D], F32) for i in range(2)]
        at = [P.sb(f"at{i}", [128, D], BF16) for i in range(2)]
        aoT = [P.sb(f"aoT{i}", [128, D], BF16) for i in range(2)]
        ot = [P.sb(f"ot{i}", [128, D], F32) for i in range(2)]
        idb = P.sb("idb", [128, 128], BF16)
        bias_sb = P.sb("bias", [1, D], F32) if bias_name else None
        ones1 = P.sb("ones1", [1, 128], F32)
        ones128c = P.sb("ones128c", [128, 1], F32)
        if l0ffn:
            fw = wo_st
            weff = P.sb("weff", [128, 8 * 1024], BF16)
            weffT = P.sb("weffT", [128, 8 * 1024], BF16)
            xnf = P.sb("xnf", [128, 8 * T], BF16)
            tabs = P.sb("tabs", [128, 1024], F32)
            t2 = P.sb("t2", [128, 1024], F32)
            srow = P.sb("srow", [128, 4], F32)
            tot1 = P.sb("tot1", [1, 1], F32)
            alph = P.sb("alph", [128, 1], F32)
        if ln_fm_out:
            x2 = P.sb("x2", [128, D], F32)
            xnb = [P.sb(f"xnb{i}", [128, D], BF16) for i in range(2)]
            fmb = [P.sb(f"fmb{i}", [128, D], BF16) for i in range(2)]
            stats = P.sb("stats", [128, 8], F32)
        ps_t0 = [P.ps(f"pst0{i}", [128, 1024], BF16) for i in range(2)]
        ps_o = [P.ps(f"pso{i}", [128, 512]) for i in range(2)]
        ps_t1 = [P.ps(f"pst1{i}", [128, 1024], BF16) for i in range(2)]
        ps_sc = P.ps("pssc", [128, 1])
        psdm = P.ps("psdm", [1, 1], F32)
        sd = P.sem("sd"); sp = P.sem("sp"); sv = P.sem("sv")
        sa = P.sem("sa"); sw = P.sem("sw"); sg = P.sem("sg")
        S = Sched()
        n = dict(sd=0, sp=0, sv=0, sa=0, sw=0, sg=0)

        def load(dst, src_ap, waits=()):
            n["sd"] += 16
            def f(e, d=dst, s=src_ap, w=tuple(waits)):
                for sem, val in w:
                    e.wait_ge(sem, val)
                e.dma_start(d, s).then_inc(sd, 16)
            S.add("sync", f)
            return n["sd"]

        load(idb[:], H["ident_bf"])
        load(ones1[:], H["ones_f32"])
        load(ones128c[:], H["ones128c_f32"])
        if bias_name:
            load(bias_sb[:], H[bias_name])
        for c in range(8):
            load(wo_st[:, c * 1024:(c + 1) * 1024],
                 H[wo_name][c * 128:(c + 1) * 128, :])
        wo_ld = n["sd"]
        for c in range(8):
            n["sw"] += 1
            def cast(e, c=c, first=(c == 0), m=wo_ld):
                if first:
                    e.wait_ge(sd, m)
                e.tensor_copy(wo_bf[:, c * 1024:(c + 1) * 1024],
                              wo_st[:, c * 1024:(c + 1) * 1024]).then_inc(sw, 1)
            S.add("gpsimd", cast)
        wo_cast = n["sw"]

        tern_done = 0
        if l0ffn:
            fw_name, alpha_name, xn_buf = l0ffn
            load(alph[:], bass.AP(H[alpha_name].tensor, 0, [[0, 128], [1, 1]]))
            for c in range(8):
                pw = [(sw, wo_cast)] if c == 0 else []
                load(fw[:, c * 1024:(c + 1) * 1024],
                     H[fw_name][c * 128:(c + 1) * 128, :], pw)
            fw_ld = n["sd"]
            for c in range(8):
                load(xnf[:, c * T:(c + 1) * T], xn_buf[c * 128:(c + 1) * 128, :])
            abs_m = []
            red_m = []
            acc_m = []
            for c in range(8):
                n["sa"] += 1
                abs_m.append(n["sa"])
                def doabs(e, c=c, first=(c == 0), m=fw_ld,
                          tw=(red_m[c - 1] if c else 0)):
                    if first:
                        e.wait_ge(sd, m)
                    else:
                        e.wait_ge(sv, tw)
                    e.activation(tabs[:], fw[:, c * 1024:(c + 1) * 1024],
                                 AF.Abs).then_inc(sa, 1)
                S.add("scalar", doabs)
                n["sv"] += 2
                red_m.append(n["sv"] - 1)
                acc_m.append(n["sv"])
                def dosum(e, c=c, am=abs_m[c], m=n["sv"]):
                    e.wait_ge(sa, am)
                    e.reduce_sum(srow[:, 0:1], tabs[:], AX.X).then_inc(sv, 1)
                    e.wait_ge(sv, m - 1)
                    if c == 0:
                        e.tensor_copy(srow[:, 1:2], srow[:, 0:1]).then_inc(sv, 1)
                    else:
                        e.tensor_add(srow[:, 1:2], srow[:, 1:2], srow[:, 0:1]).then_inc(sv, 1)
                    e.wait_ge(sv, m)
                S.add("vector", dosum)
            n["sp"] += 1
            totm = n["sp"]
            def dotot(e, m=acc_m[-1], sm=n["sp"]):
                e.wait_ge(sv, m)
                e.matmul(ps_sc[0:1, 0:1], ones128c[:], srow[:, 1:2],
                         start=True, stop=True).then_inc(sp, 1)
            S.add("tensor", dotot)
            n["sv"] += 1
            tot_ev = n["sv"]
            def dototev(e, pm=totm, m=n["sv"]):
                e.wait_ge(sp, pm)
                e.tensor_copy(tot1[:], ps_sc[0:1, 0:1]).then_inc(sv, 1)
                e.wait_ge(sv, m)
            S.add("vector", dototev)
            n["sp"] += 1
            bcm = n["sp"]
            def dobc(e, m=tot_ev, sm=n["sp"]):
                e.wait_ge(sv, m)
                e.matmul(ps_sc[:, 0:1], ones1[:], tot1[:],
                         start=True, stop=True).then_inc(sp, 1)
            S.add("tensor", dobc)
            n["sv"] += 2
            del_m = n["sv"]
            def dodel(e, pm=bcm, m=n["sv"]):
                e.wait_ge(sp, pm)
                e.tensor_scalar_mul(srow[:, 2:3], ps_sc[:, 0:1],
                                    0.7 / (1024.0 * 1024.0)).then_inc(sv, 1)
                e.tensor_scalar_mul(srow[:, 3:4], ps_sc[:, 0:1],
                                    -0.7 / (1024.0 * 1024.0)).then_inc(sv, 1)
                e.wait_ge(sv, m)
            S.add("vector", dodel)
            al_m = []
            for c in range(8):
                n["sv"] += 3
                tv = n["sv"]
                def dotern(e, c=c, m=tv, dm=del_m,
                           aw=(al_m[c - 1] if c else 0)):
                    if c == 0:
                        e.wait_ge(sv, dm)
                    else:
                        e.wait_ge(sa, aw)
                    e.tensor_scalar(tabs[:], fw[:, c * 1024:(c + 1) * 1024],
                                    srow[:, 2:3], None, ALU.is_gt).then_inc(sv, 1)
                    e.tensor_scalar(t2[:], fw[:, c * 1024:(c + 1) * 1024],
                                    srow[:, 3:4], -1.0, ALU.is_lt, ALU.mult).then_inc(sv, 1)
                    e.wait_ge(sv, m - 1)
                    e.tensor_add(tabs[:], tabs[:], t2[:]).then_inc(sv, 1)
                    e.wait_ge(sv, m)
                S.add("vector", dotern)
                n["sa"] += 1
                al_m.append(n["sa"])
                def doal(e, c=c, vm=tv, am=n["sa"]):
                    e.wait_ge(sv, vm)
                    e.mul(weff[:, c * 1024:(c + 1) * 1024], tabs[:],
                          alph[:, 0:1]).then_inc(sa, 1)
                S.add("scalar", doal)
            al_done = al_m[-1]
            wtr_m = []
            wev_m = []
            for co in range(8):
                n["sp"] += 1
                wtr_m.append(n["sp"])
                def dotr(e, co=co, am=al_done, m=n["sp"],
                         ew=(wev_m[co - 1] if co else 0)):
                    if co == 0:
                        e.wait_ge(sa, am)
                    else:
                        e.wait_ge(sv, ew)
                    settle(e, psdm[0:1, 0:1], idb[:, 0:1], k=2)
                    for ci in range(8):
                        e.transpose(
                            ps_t0[0][:, ci * 128:(ci + 1) * 128],
                            weff[:, co * 1024 + ci * 128: co * 1024 + (ci + 1) * 128],
                            idb[:])
                    e.matmul(psdm[0:1, 0:1], idb[:, 0:1], idb[:, 0:1],
                             start=True, stop=True).then_inc(sp, 1)
                S.add("tensor", dotr)
                n["sv"] += 1
                wev_m.append(n["sv"])
                def doev(e, co=co, pm=wtr_m[co], m=n["sv"]):
                    e.wait_ge(sp, pm)
                    e.tensor_copy(
                        weffT.ap().rearrange("p (ci co x) -> p ci co x",
                                             ci=8, co=8)[:, :, co, :],
                        ps_t0[0].ap().rearrange("p (ci x) -> p ci x", ci=8)
                    ).then_inc(sv, 1)
                S.add("vector", doev)
            tern_done = wev_m[-1]

        all_ld = n["sd"]

        ev_marks = {}; tr_marks = {}; aoT_marks = {}; st_marks = {}
        xnb_marks = {}; trl_marks = {}; fm_marks = {}; stf_marks = {}; sq_marks = {}
        for ti in range(NT):
            par = ti % 2
            lw1 = [(sv, ev_marks[ti - 2])] if ti >= 2 else []
            lm1 = load(xt[par][:], resid_buf[ti * 128:(ti + 1) * 128, :], lw1)
            lw2 = [(sp, tr_marks[ti - 2])] if ti >= 2 else []
            lm2 = load(at[par][:], ao_buf[ti], lw2)
            n["sp"] += 1
            tr_marks[ti] = n["sp"]
            def dotr0(e, par=par, ti=ti, lm=lm2, tern=tern_done,
                      aw=(aoT_marks.get(ti - 2, 0))):
                e.wait_ge(sd, lm)
                if ti == 0 and tern:
                    e.wait_ge(sv, tern)
                if ti >= 2:
                    e.wait_ge(sv, aw)
                for c in range(8):
                    e.transpose(ps_t0[par][:, c * 128:(c + 1) * 128],
                                at[par][:, c * 128:(c + 1) * 128], idb[:])
                e.matmul(psdm[0:1, 0:1], idb[:, 0:1], idb[:, 0:1],
                         start=True, stop=True).then_inc(sp, 1)
            S.add("tensor", dotr0)
            n["sv"] += 1
            aoT_marks[ti] = n["sv"]
            def doaoT(e, par=par, pm=tr_marks[ti], m=n["sv"]):
                e.wait_ge(sp, pm)
                e.tensor_copy(aoT[par][:], ps_t0[par][:]).then_inc(sv, 1)
            S.add("vector", doaoT)
            aoT_m = n["sv"]
            mm_first = n["sp"] + 1
            for half in range(2):
                n["sp"] += 1
                def domm(e, par=par, half=half, ti=ti, am=aoT_m,
                         has_f=bool(l0ffn), first=(ti == 0 and half == 0),
                         evw=(ev_marks.get(ti - 1, 0))):
                    e.wait_ge(sv, am)
                    if first:
                        e.wait_ge(sw, wo_cast)
                        e.wait_ge(sd, all_ld)
                    if ti >= 1 and half == 0:
                        e.wait_ge(sv, evw)
                    settle(e, psdm[0:1, 0:1], idb[:, 0:1], k=2)
                    specs = [(aoT[par][:, c * 128:(c + 1) * 128],
                              wo_bf[:, c * 1024 + half * 512: c * 1024 + (half + 1) * 512])
                             for c in range(8)]
                    if has_f:
                        specs += [(xnf[:, c * T + ti * 128: c * T + (ti + 1) * 128],
                                   weffT[:, c * 1024 + half * 512: c * 1024 + (half + 1) * 512])
                                  for c in range(8)]
                    if bias_name:
                        specs.append((ones1[:], bias_sb[:, half * 512:(half + 1) * 512]))
                    for idx, (lh, rh) in enumerate(specs):
                        i = e.matmul(ps_o[half][:], lh, rh,
                                     start=(idx == 0), stop=(idx == len(specs) - 1))
                    i.then_inc(sp, 1)
                S.add("tensor", domm)
            mm_m = n["sp"]
            n["sv"] += 2
            ev_marks[ti] = n["sv"]
            def doev2(e, par=par, pm=mm_m, lm=lm1, ti=ti, m=n["sv"],
                      stw=(st_marks.get(ti - 2, 0)),
                      xw=(xnb_marks.get(ti - 2, 0) if ln_fm_out else 0)):
                e.wait_ge(sp, pm)
                e.wait_ge(sd, lm)
                if ti >= 2:
                    e.wait_ge(sg, stw)
                    if xw:
                        e.wait_ge(sv, xw)
                e.tensor_add(ot[par][:, 0:512], ps_o[0][:], xt[par][:, 0:512]).then_inc(sv, 1)
                e.tensor_add(ot[par][:, 512:1024], ps_o[1][:], xt[par][:, 512:1024]).then_inc(sv, 1)
            S.add("vector", doev2)
            ot_m = n["sv"]
            n["sg"] += 16
            st_marks[ti] = n["sg"]
            def dost(e, par=par, ti=ti, m=ot_m):
                e.wait_ge(sv, m)
                e.dma_start(out_buf[ti * 128:(ti + 1) * 128, :], ot[par][:]).then_inc(sg, 16)
            S.add("gpsimd", dost)
            if ln_fm_out:
                base = ot_m
                n["sa"] += 3
                sq_marks[ti] = n["sa"] - 2
                def doln_a(e, par=par, base=base, sq=n["sa"] - 2):
                    e.wait_ge(sv, base)
                    e.activation(x2[:], ot[par][:], AF.Square).then_inc(sa, 1)
                    e.wait_ge(sv, base + 4)
                    e.activation(stats[:, 6:7], stats[:, 5:6], AF.Ln).then_inc(sa, 1)
                    e.wait_ge(sv, base + 5)
                    e.activation(stats[:, 7:8], stats[:, 6:7], AF.Exp).then_inc(sa, 1)
                S.add("scalar", doln_a)
                n["sv"] += 5
                def doln_v(e, par=par, base=base, sq=sq_marks[ti]):
                    e.wait_ge(sv, base)
                    e.reduce_sum(stats[:, 0:1], ot[par][:], AX.X).then_inc(sv, 1)
                    e.wait_ge(sv, base + 1)
                    e.tensor_scalar_mul(stats[:, 2:3], stats[:, 0:1], 1.0 / D)
                    e.wait_ge(sa, sq)
                    e.reduce_sum(stats[:, 1:2], x2[:], AX.X).then_inc(sv, 1)
                    e.wait_ge(sv, base + 2)
                    e.tensor_scalar_mul(stats[:, 3:4], stats[:, 1:2], 1.0 / D)
                    e.tensor_mul(stats[:, 4:5], stats[:, 2:3], stats[:, 2:3]).then_inc(sv, 1)
                    e.wait_ge(sv, base + 3)
                    e.scalar_tensor_tensor(stats[:, 5:6], stats[:, 3:4], 1e-5,
                                           stats[:, 4:5], ALU.add, ALU.subtract).then_inc(sv, 1)
                    e.wait_ge(sa, sq + 1)
                    e.tensor_scalar_mul(stats[:, 6:7], stats[:, 6:7], -0.5).then_inc(sv, 1)
                S.add("vector", doln_v)
                n["sv"] += 1
                xnb_marks[ti] = n["sv"]
                def doxn(e, par=par, am=sq_marks[ti] + 2, ti=ti, m=n["sv"],
                         tw=(trl_marks.get(ti - 2, 0))):
                    e.wait_ge(sa, am)
                    if ti >= 2:
                        e.wait_ge(sp, tw)
                    e.tensor_scalar(xnb[par][:], ot[par][:], stats[:, 2:3],
                                    stats[:, 7:8], ALU.subtract, ALU.mult).then_inc(sv, 1)
                S.add("vector", doxn)
                n["sp"] += 1
                trl_marks[ti] = n["sp"]
                def dotr1(e, par=par, vm=xnb_marks[ti], ti=ti,
                          fw_=(fm_marks.get(ti - 2, 0))):
                    e.wait_ge(sv, vm)
                    if ti >= 2:
                        e.wait_ge(sv, fw_)
                    settle(e, psdm[0:1, 0:1], idb[:, 0:1], k=2)
                    for c in range(8):
                        e.transpose(ps_t1[par][:, c * 128:(c + 1) * 128],
                                    xnb[par][:, c * 128:(c + 1) * 128], idb[:])
                    e.matmul(psdm[0:1, 0:1], idb[:, 0:1], idb[:, 0:1],
                             start=True, stop=True).then_inc(sp, 1)
                S.add("tensor", dotr1)
                n["sv"] += 1
                fm_marks[ti] = n["sv"]
                def dofm(e, par=par, pm=trl_marks[ti], ti=ti,
                         sw_=(stf_marks.get(ti - 2, 0))):
                    e.wait_ge(sp, pm)
                    if ti >= 2:
                        e.wait_ge(sg, sw_)
                    e.tensor_copy(fmb[par][:], ps_t1[par][:]).then_inc(sv, 1)
                S.add("vector", dofm)
                n["sg"] += 16
                stf_marks[ti] = n["sg"]
                def dostf(e, par=par, ti=ti, m=fm_marks[ti]):
                    e.wait_ge(sv, m)
                    dst = H[ln_fm_out]
                    dap = bass.AP(dst.tensor, ti * 128,
                                  [[T, 128], [128 * T, 8], [1, 128]])
                    e.dma_start(dap, fmb[par].ap().rearrange("p (c t) -> p c t", c=8)).then_inc(sg, 16)
                S.add("gpsimd", dostf)

        def fin(e, tot=n["sg"]):
            e.wait_ge(sg, tot)
        S.add("gpsimd", fin)
        S.run(nc)
    return



def ph_mqkv(nc, H, tag, l, xf_buf, qf_buf, kf_buf, vf_buf):
    with ExitStack() as st:
        P = Phase(nc, st, tag)
        hbf = P.sb("hbf", [128, 8 * T], BF16)
        wst = P.sb("wst", [128, 8192], F32)
        wqk = P.sb("wqk", [128, 8192], BF16)
        wv = P.sb("wv", [128, 4096], BF16)
        cA = P.sb("cA", [64, T], F32)
        cB = P.sb("cB", [64, T], F32)
        qsc = P.sb("qsc", [64, 2], F32)
        q_sb = P.sb("q_sb", [64, 2048], F32)
        qsq = P.sb("qsq", [64, 2048], F32)
        rtmp = P.sb("rtmp", [64, 2048], F32)
        qout = P.sb("qout", [64, 2048], BF16)
        nsq = P.sb("nsq", [1, 2048], F32)
        rn = P.sb("rn", [1, 2048], F32)
        ones64 = P.sb("ones64", [64, 1], F32)
        ones1 = P.sb("ones1", [1, 128], F32)
        ve = [P.sb(f"ve{i}", [128, 512], BF16) for i in range(2)]
        ps_q = [P.ps(f"psq{i}", [64, 1024]) for i in range(2)]
        ps_n = P.ps("psn", [1, 512])
        ps_b = P.ps("psb", [64, 1024])
        ps_v = P.ps("psv", [128, 512])
        sd = P.sem("sd"); sw = P.sem("sw"); sp = P.sem("sp")
        sv = P.sem("sv"); sa = P.sem("sa"); sg = P.sem("sg")
        S = Sched()
        n = dict(sd=0, sw=0, sp=0, sv=0, sa=0, sg=0)

        def load(dst, src_ap, waits=()):
            n["sd"] += 16
            def f(e, d=dst, s=src_ap, w=tuple(waits)):
                for sem, val in w:
                    e.wait_ge(sem, val)
                e.dma_start(d, s).then_inc(sd, 16)
            S.add("sync", f)
            return n["sd"]

        load(ones64[:], H["ones64c_f32"])
        load(ones1[:], H["ones_f32"])
        for c in range(8):
            load(wst[:, c * 1024:(c + 1) * 1024],
                 H["mf_wqkv"][l, c * 128:(c + 1) * 128, 0:1024])
        m = n["sd"]
        for c in range(8):
            n["sw"] += 1
            def cst(e, c=c, first=(c == 0), m=m):
                if first:
                    e.wait_ge(sd, m)
                ecopy(e, wqk[:, c * 1024:(c + 1) * 1024],
                      wst[:, c * 1024:(c + 1) * 1024]).then_inc(sw, 1)
            S.add("scalar", cst)
        qk_cast = n["sw"]
        for c in range(8):
            pw = [(sw, qk_cast)] if c == 0 else []
            load(wst[:, c * 512:(c + 1) * 512],
                 H["mf_wqkv"][l, c * 128:(c + 1) * 128, 1024:1536], pw)
        m = n["sd"]
        for c in range(8):
            n["sw"] += 1
            def cst2(e, c=c, first=(c == 0), m=m):
                if first:
                    e.wait_ge(sd, m)
                ecopy(e, wv[:, c * 512:(c + 1) * 512],
                      wst[:, c * 512:(c + 1) * 512]).then_inc(sw, 1)
            S.add("scalar", cst2)
        v_cast = n["sw"]
        prev_hc = v_cast
        for c in range(8):
            pw = [(sw, prev_hc)]
            m = load(wst[:, 0:T + 1], xf_buf[c * 128:(c + 1) * 128, :], pw)
            nincs = 2 if c >= 4 else 1
            n["sw"] += nincs
            prev_hc = n["sw"]
            off = 1 if c < 4 else 0
            def hcst(e, c=c, m=m, off=off, fin=prev_hc):
                e.wait_ge(sd, m)
                if c >= 4:
                    e.memset(wst[:, 0:1], 0.0).then_inc(sw, 1)
                    e.wait_ge(sw, fin - 1)
                e.tensor_copy(hbf[:, c * T:(c + 1) * T],
                              wst[:, off:off + T]).then_inc(sw, 1)
            S.add("vector", hcst)
        h_cast = n["sw"]
        all_ld = n["sd"]

        evq_m = [0, 0]
        side_done = 0
        prev_rot = 0
        prev_qst = 0
        for side in range(2):
            cnames = ("rot_cs", "rot_ss") if side == 0 else ("rot_ci", "rot_si")
            sname = "mf_qs" if side == 0 else "mf_ks"
            obuf = qf_buf if side == 0 else kf_buf
            wcol = side * 512
            guard = [(sv, side_done)] if side else []
            load(cA[:], H[cnames[0]], guard)
            load(cB[:], H[cnames[1]])
            load(qsc[:, 0:1], bass.AP(H[sname].tensor, l * 128, [[1, 64], [1, 1]]))
            lq2 = load(qsc[:, 1:2], bass.AP(H[sname].tensor, l * 128 + 32, [[1, 64], [1, 1]]))
            n["sv"] += 2
            cdone = n["sv"]
            def doconst(e, lds=lq2, m=n["sv"], mul2=(8.0 if side == 0 else None)):
                e.wait_ge(sd, lds)
                if mul2 is not None:
                    e.tensor_scalar(cA[:], cA[:], qsc[:, 0:1], mul2,
                                    ALU.mult, ALU.mult).then_inc(sv, 1)
                    e.tensor_scalar(cB[:], cB[:], qsc[:, 1:2], mul2,
                                    ALU.mult, ALU.mult).then_inc(sv, 1)
                else:
                    e.tensor_scalar(cA[:], cA[:], qsc[:, 0:1], None,
                                    ALU.mult).then_inc(sv, 1)
                    e.tensor_scalar(cB[:], cB[:], qsc[:, 1:2], None,
                                    ALU.mult).then_inc(sv, 1)
                e.wait_ge(sv, m)
            S.add("vector", doconst)
            for h in range(HF):
                for hh in range(2):
                    tok0 = hh * 2048
                    qm = []
                    for quar in range(2):
                        n["sp"] += 1
                        qm.append(n["sp"])
                        def domm(e, h=h, quar=quar, tok0=tok0, wcol=wcol,
                                 first=(side == 0 and h == 0 and hh == 0 and quar == 0),
                                 ev=evq_m[quar], m=n["sp"]):
                            if first:
                                e.wait_ge(sd, all_ld)
                                e.wait_ge(sw, h_cast)
                            if ev:
                                e.wait_ge(sa, ev)
                            settle(e, ps_v[0:1, 0:1], wqk[:, 0:1], k=2)
                            for nq in range(2):
                                for c in range(8):
                                    i = e.matmul(
                                        ps_q[quar][:, nq * 512:(nq + 1) * 512],
                                        wqk[:, c * 1024 + wcol + h * 64:
                                            c * 1024 + wcol + h * 64 + 64],
                                        hbf[:, c * T + tok0 + quar * 1024 + nq * 512:
                                            c * T + tok0 + quar * 1024 + (nq + 1) * 512],
                                        start=(c == 0), stop=(c == 7))
                            i.then_inc(sp, 1)
                        S.add("tensor", domm)
                    for quar in range(2):
                        n["sa"] += 1
                        evq_m[quar] = n["sa"]
                        def doev(e, quar=quar, pm=qm[quar], am=n["sa"], rw=prev_rot):
                            e.wait_ge(sp, pm)
                            if rw:
                                e.wait_ge(sv, rw)
                            e.activation(q_sb[:, quar * 1024:(quar + 1) * 1024],
                                         ps_q[quar][:], AF.Copy).then_inc(sa, 1)
                            e.wait_ge(sa, am)
                        S.add("scalar", doev)
                    n["sa"] += 1
                    sq_m = n["sa"]
                    def dosq(e):
                        e.activation(qsq[:], q_sb[:], AF.Square).then_inc(sa, 1)
                    S.add("scalar", dosq)
                    nev = []
                    for j in range(4):
                        n["sp"] += 1
                        pmj = n["sp"]
                        def donm(e, j=j, am=sq_m, m=pmj,
                                 ew=(nev[j - 1] if j >= 1 else 0)):
                            if j == 0:
                                e.wait_ge(sa, am)
                            else:
                                e.wait_ge(sv, ew)
                            settle(e, ps_v[0:1, 0:1], wqk[:, 0:1], k=1)
                            e.matmul(ps_n[0:1, :], ones64[:],
                                     qsq[:, j * 512:(j + 1) * 512],
                                     start=True, stop=True).then_inc(sp, 1)
                        S.add("tensor", donm)
                        n["sv"] += 1
                        nev.append(n["sv"])
                        def donev(e, j=j, pm=pmj, m=n["sv"]):
                            e.wait_ge(sp, pm)
                            e.tensor_copy(nsq[0:1, j * 512:(j + 1) * 512],
                                          ps_n[0:1, :]).then_inc(sv, 1)
                        S.add("vector", donev)
                    n["sa"] += 1
                    sqrt_m = n["sa"]
                    def dosqrt(e, vm=nev[-1]):
                        e.wait_ge(sv, vm)
                        e.activation(rn[:], nsq[:], AF.Sqrt).then_inc(sa, 1)
                    S.add("scalar", dosqrt)
                    n["sv"] += 2
                    rec_m = n["sv"]
                    def dorec(e, am=sqrt_m, m=n["sv"]):
                        e.wait_ge(sa, am)
                        e.tensor_scalar_max(rn[:], rn[:], 1e-12).then_inc(sv, 1)
                        e.wait_ge(sv, m - 1)
                        e.reciprocal(rn[:], rn[:]).then_inc(sv, 1)
                        e.wait_ge(sv, m)
                    S.add("vector", dorec)
                    qh_m = []
                    for quar in range(2):
                        n["sp"] += 1
                        bm = n["sp"]
                        def dobc(e, quar=quar, rm=rec_m, m=bm,
                                 qw=(qh_m[quar - 1] if quar else 0)):
                            e.wait_ge(sv, rm)
                            if quar:
                                e.wait_ge(sv, qw)
                            settle(e, ps_v[0:1, 0:1], wqk[:, 0:1], k=2)
                            for j in range(2):
                                i = e.matmul(ps_b[:, j * 512:(j + 1) * 512],
                                             ones1[0:1, 0:64],
                                             rn[0:1, quar * 1024 + j * 512:
                                                quar * 1024 + (j + 1) * 512],
                                             start=True, stop=True)
                            i.then_inc(sp, 1)
                        S.add("tensor", dobc)
                        n["sv"] += 1
                        qh_m.append(n["sv"])
                        def doqh(e, quar=quar, pm=bm, m=n["sv"]):
                            e.wait_ge(sp, pm)
                            e.tensor_mul(q_sb[:, quar * 1024:(quar + 1) * 1024],
                                         q_sb[:, quar * 1024:(quar + 1) * 1024],
                                         ps_b[:]).then_inc(sv, 1)
                            e.wait_ge(sv, m)
                        S.add("vector", doqh)
                    n["sv"] += 5
                    rot_done = n["sv"]
                    prev_rot = rot_done
                    def dorot(e, tok0=tok0, m=rot_done, gw=prev_qst):
                        e.tensor_scalar_mul(rtmp[0:32, :], q_sb[32:64, :], -1.0).then_inc(sv, 1)
                        e.tensor_copy(rtmp[32:64, :], q_sb[0:32, :]).then_inc(sv, 1)
                        e.wait_ge(sv, m - 3)
                        e.tensor_mul(q_sb[:], q_sb[:],
                                     cA[:, tok0:tok0 + 2048]).then_inc(sv, 1)
                        e.tensor_mul(rtmp[:], rtmp[:],
                                     cB[:, tok0:tok0 + 2048]).then_inc(sv, 1)
                        e.wait_ge(sv, m - 1)
                        if gw:
                            e.wait_ge(sg, gw)
                        e.tensor_add(qout[:], q_sb[:], rtmp[:]).then_inc(sv, 1)
                        e.wait_ge(sv, m)
                    S.add("vector", dorot)
                    n["sg"] += 16
                    prev_qst = n["sg"]
                    def dostq(e, obuf=obuf, h=h, tok0=tok0, vm=rot_done):
                        e.wait_ge(sv, vm)
                        e.dma_start(obuf[h, :, tok0:tok0 + 2048],
                                    qout[:]).then_inc(sg, 16)
                    S.add("gpsimd", dostq)
            side_done = n["sv"]
        qk_stores = n["sg"]

        vev_m = {}
        for ti in range(NT):
            par = ti % 2
            n["sp"] += 1
            vm = n["sp"]
            def dovm(e, ti=ti, m=vm, ew=vev_m.get(ti - 1, 0)):
                if ew:
                    e.wait_ge(sv, ew)
                for c in range(8):
                    i = e.matmul(ps_v[:],
                                 hbf[:, c * T + ti * 128: c * T + (ti + 1) * 128],
                                 wv[:, c * 512:(c + 1) * 512],
                                 start=(c == 0), stop=(c == 7))
                i.then_inc(sp, 1)
            S.add("tensor", dovm)
            n["sv"] += 1
            vev_m[ti] = n["sv"]
            def dovev(e, par=par, pm=vm, ti=ti, base=qk_stores):
                e.wait_ge(sp, pm)
                if ti >= 2:
                    e.wait_ge(sg, base + 16 * (ti - 1))
                e.tensor_copy(ve[par][:], ps_v[:]).then_inc(sv, 1)
            S.add("vector", dovev)
            n["sg"] += 16
            def dovst(e, par=par, ti=ti, vm2=vev_m[ti]):
                e.wait_ge(sv, vm2)
                e.dma_start(vf_buf[ti], ve[par][:]).then_inc(sg, 16)
            S.add("gpsimd", dovst)

        def fin(e, tot=n["sg"]):
            e.wait_ge(sg, tot)
        S.add("gpsimd", fin)
        S.run(nc)
    return



MWO_GROUPS = 8
MWO_PARTS = 3

def ph_mwo(nc, H, tag, l, xf_in, xf_out, aof_buf):
    with ExitStack() as st:
        P = Phase(nc, st, tag)
        wst = P.sb("wst", [128, 4096], F32)
        wo_bf = P.sb("wo_bf", [128, 4096], BF16)
        at = [P.sb(f"at{i}", [128, 512], BF16) for i in range(2)]
        aT = P.sb("aT", [128, 2048], BF16)
        xsl = [P.sb(f"xsl{i}", [128, 512], F32) for i in range(2)]
        osl = [P.sb(f"osl{i}", [128, 512], F32) for i in range(2)]
        idb = P.sb("idb", [128, 128], BF16)
        ps_t = [P.ps(f"ps_t{i}", [128, 512], BF16) for i in range(2)]
        ps_u = [P.ps(f"psu{i}", [128, 512]) for i in range(2)]
        psdm = P.ps("psdm", [1, 1], F32)
        sd = P.sem("sd"); sw = P.sem("sw"); sp = P.sem("sp")
        sv = P.sem("sv"); sg = P.sem("sg")
        S = Sched()
        n = dict(sd=0, sw=0, sp=0, sv=0, sg=0)

        def load(dst, src_ap, waits=()):
            n["sd"] += 16
            def f(e, d=dst, s=src_ap, w=tuple(waits)):
                for sem, val in w:
                    e.wait_ge(sem, val)
                e.dma_start(d, s).then_inc(sd, 16)
            S.add("sync", f)
            return n["sd"]

        def xf_slice_ap(buf, oc, g):
            return bass.AP(buf.tensor, oc * 128 * (T + 1) + 1 + g * 512,
                           [[T + 1, 128], [1, 512]])

        load(idb[:], H["ident_bf"])
        for c in range(4):
            load(wst[:, c * 1024:(c + 1) * 1024],
                 H["mf_wo"][l, c * 128:(c + 1) * 128, :])
        m = n["sd"]
        for c in range(4):
            n["sw"] += 1
            def cst(e, c=c, first=(c == 0), m=m):
                if first:
                    e.wait_ge(sd, m)
                ecopy(e, wo_bf[:, c * 1024:(c + 1) * 1024],
                      wst[:, c * 1024:(c + 1) * 1024]).then_inc(sw, 1)
            S.add("scalar", cst)
        w_cast = n["sw"]

        tr_m = {}; evT_m = {}; um_m = {}; add_m = {}; st_m = {}; evT_prev = {}
        for g in range(MWO_GROUPS):
            for tl in range(4 if MWO_PARTS & 1 else 0):
                ti = g * 4 + tl
                par = ti % 2
                lw = [(sp, tr_m[ti - 2])] if ti >= 2 else []
                lm = load(at[par][:], aof_buf[ti], lw)
                n["sp"] += 1
                tr_m[ti] = n["sp"]
                def dotr2(e, par=par, tl=tl, lm=lm, ti=ti,
                          ew=(evT_prev.get(ti - 2, 0))):
                    e.wait_ge(sd, lm)
                    if ew:
                        e.wait_ge(sv, ew)
                    for c in range(4):
                        e.transpose(ps_t[ti % 2][:, c * 128:(c + 1) * 128],
                                    at[par][:, c * 128:(c + 1) * 128], idb[:])
                    e.matmul(psdm[0:1, 0:1], idb[:, 0:1], idb[:, 0:1],
                             start=True, stop=True).then_inc(sp, 1)
                S.add("tensor", dotr2)
                n["sv"] += 1
                evT_m[(g, tl)] = n["sv"]
                def doevT(e, tl=tl, ti=ti, pm=tr_m[ti], m=n["sv"],
                          uw=(um_m.get(g - 1, 0) if tl == 0 else 0)):
                    e.wait_ge(sp, pm)
                    if uw:
                        e.wait_ge(sp, uw)
                    for c in range(4):
                        i = e.tensor_copy(
                            aT[:, c * 512 + tl * 128: c * 512 + (tl + 1) * 128],
                            ps_t[ti % 2][:, c * 128:(c + 1) * 128])
                    i.then_inc(sv, 1)
                S.add("vector", doevT)
                evT_prev[ti] = n["sv"]
            aT_done = n["sv"]
            for oc in range(8 if MWO_PARTS & 2 else 0):
                par = oc % 2
                prev_add = (add_m.get((g, oc - 2), 0) if oc >= 2 else
                            add_m.get((g - 1, 6 + oc), 0))
                prev_st = (st_m.get((g, oc - 2), 0) if oc >= 2 else
                           st_m.get((g - 1, 6 + oc), 0))
                lw = [(sv, prev_add)] if prev_add else []
                lm = load(xsl[par][:], xf_slice_ap(xf_in, oc, g), lw)
                n["sp"] += 1
                um = n["sp"]
                def doum(e, par=par, oc=oc, am=aT_done, m=um,
                         first=(g == 0 and oc == 0), aw=prev_add):
                    if am:
                        e.wait_ge(sv, am)
                    if first:
                        e.wait_ge(sw, w_cast)
                    if aw:
                        e.wait_ge(sv, aw)
                    settle(e, psdm[0:1, 0:1], idb[:, 0:1])
                    for c in range(4):
                        i = e.matmul(ps_u[par][:],
                                     wo_bf[:, c * 1024 + oc * 128: c * 1024 + (oc + 1) * 128],
                                     aT[:, c * 512:(c + 1) * 512],
                                     start=(c == 0), stop=(c == 3))
                    i.then_inc(sp, 1)
                S.add("tensor", doum)
                um_m[g] = n["sp"]
                n["sv"] += 1
                add_m[(g, oc)] = n["sv"]
                def doadd(e, par=par, pm=um, lm=lm, m=n["sv"], gw=prev_st):
                    e.wait_ge(sp, pm)
                    e.wait_ge(sd, lm)
                    if gw:
                        e.wait_ge(sg, gw)
                    e.tensor_add(osl[par][:], ps_u[par][:], xsl[par][:]).then_inc(sv, 1)
                S.add("vector", doadd)
                n["sg"] += 16
                st_m[(g, oc)] = n["sg"]
                def dostx(e, par=par, oc=oc, g=g, m=add_m[(g, oc)]):
                    e.wait_ge(sv, m)
                    e.dma_start(xf_slice_ap(xf_out, oc, g), osl[par][:]).then_inc(sg, 16)
                S.add("gpsimd", dostx)

        def fin(e, tot=n["sg"]):
            e.wait_ge(sg, tot)
        S.add("gpsimd", fin)
        S.run(nc)
    return



MFFN_CAST_DVE = True

def ph_mffn(nc, H, tag, l, xf_in, xf_out):
    HID = 2730
    NHC = 22
    with ExitStack() as st:
        P = Phase(nc, st, tag)
        hst = P.sb("hst", [128, 8 * 1024], F32)
        hbfc = P.sb("hbfc", [128, 512], BF16)
        sqc = P.sb("sqc", [128, 512], BF16)
        xb = P.sb("xb", [128, 8 * 1024], BF16)
        mrow = P.sb("mrow", [1, 1024], F32)
        qrow = P.sb("qrow", [1, 1024], F32)
        vrow = P.sb("vrow", [1, 1024], F32)
        rrow = P.sb("rrow", [1, 1024], F32)
        lg = P.sb("lg", [128, 8], F32)
        lb = P.sb("lb", [128, 8], F32)
        b2 = P.sb("b2", [128, 8], F32)
        wst = P.sb("wstf", [128, 4096], F32)
        wa = P.sb("wa", [128, 4096], BF16)
        wg = P.sb("wg", [128, 4096], BF16)
        w2st = P.sb("w2st", [128, NHC * 128], F32)
        w2b = P.sb("w2b", [128, NHC * 128], BF16)
        b1r = P.sb("b1r", [1, 5460], F32)
        gel = P.sb("gel", [128, 512], BF16)
        h4 = P.sb("h4", [128, 512], BF16)
        h4T = P.sb("h4T", [128, NHC * 1024], BF16)
        xsl = [P.sb(f"xsl{i}", [128, 512], F32) for i in range(2)]
        osl = [P.sb(f"osl{i}", [128, 512], F32) for i in range(2)]
        ones128b = P.sb("ones128b", [128, 1], BF16)
        dbgag = P.sb("dbgag", [128, 2048], F32) if "dbg_ag" in H else None
        ones1 = P.sb("ones1", [1, 128], F32)
        idb = P.sb("idb", [128, 128], BF16)
        ps_m = P.ps("ps_m", [1, 512])
        ps_q = P.ps("ps_q", [1, 512])
        ps_ag = [P.ps(f"psag{i}", [128, 512]) for i in range(2)]
        ps_t = [P.ps(f"pstf{i}", [128, 1024], BF16) for i in range(2)]
        ps_u = [P.ps(f"psuf{i}", [128, 512]) for i in range(2)]
        sd = P.sem("sd"); sw = P.sem("sw"); sp = P.sem("sp")
        sv = P.sem("sv"); sa = P.sem("sa"); sg = P.sem("sg")
        S = Sched()
        n = dict(sd=0, sw=0, sp=0, sv=0, sa=0, sg=0)

        def load(dst, src_ap, waits=()):
            n["sd"] += 16
            def f(e, d=dst, s=src_ap, w=tuple(waits)):
                for sem, val in w:
                    e.wait_ge(sem, val)
                e.dma_start(d, s).then_inc(sd, 16)
            S.add("sync", f)
            return n["sd"]

        load(idb[:], H["ident_bf"])
        load(ones1[:], H["ones_f32"])
        load(ones128b[:], H["ones128c_bf"])
        load(lg[:], H["mf_lgT"][l])
        load(lb[:], H["mf_lbT"][l])
        load(b2[:], H["mf_b2T"][l])
        load(b1r[:], bass.AP(H["mf_b1"].tensor, l * 5460, [[1, 1], [1, 5460]]))

        def xf_ap(buf, oc, coloff, width):
            return bass.AP(buf.tensor, oc * 128 * (T + 1) + coloff,
                           [[T + 1, 128], [1, width]])

        stats_done = {}
        stats_read = [0]
        h4_mark = [0]
        trp_mark = [0]
        evh = {}
        upd_done = {}
        add_m = {}
        st_m = {}
        w1cast_last = [0]
        w2cast = {}
        upd_m = {}

        for g in range(4):
            g0 = g * 1024
            hld = 0
            for c in range(8):
                off = 1 + g0 - (1 if c >= 4 else 0)
                w_ = [(sv, stats_done.get(g - 1, 0))] if (g and c == 0) else []
                hld = load(hst[:, c * 1024:(c + 1) * 1024], xf_ap(xf_in, c, off, 1024), w_)
            zs = 0
            if g == 0:
                n["sw"] += 1
                zs = n["sw"]
                def dozero(e, m=hld, sm=n["sw"]):
                    e.wait_ge(sd, m)
                    for c in range(4, 8):
                        i = e.memset(hst[:, c * 1024: c * 1024 + 1], 0.0)
                    i.then_inc(sw, 1)
                    e.wait_ge(sw, sm)
                S.add("vector", dozero)
            sub_rstd = []
            prev_stats_v = stats_read[0]
            for sb2 in range(2):
                so = sb2 * 512
                stat_pm = 0
                for c in range(8):
                    n["sw"] += 1
                    cm = n["sw"]
                    def docst(e, c=c, so=so, m=hld, zs=zs, cm=cm, pw=stat_pm):
                        if c == 0:
                            e.wait_ge(sd, m)
                            if zs:
                                e.wait_ge(sw, zs)
                        if pw:
                            e.wait_ge(sp, pw)
                        e.tensor_copy(hbfc[:], hst[:, c * 1024 + so: c * 1024 + so + 512]).then_inc(sw, 1)
                    S.add("vector" if MFFN_CAST_DVE else "gpsimd", docst)
                    n["sa"] += 1
                    am = n["sa"]
                    def dosqa(e, c=c, so=so, pw=stat_pm, am=am, m=hld, zs=zs):
                        if c == 0:
                            e.wait_ge(sd, m)
                            if zs:
                                e.wait_ge(sw, zs)
                        if pw:
                            e.wait_ge(sp, pw)
                        e.activation(sqc[:], hst[:, c * 1024 + so: c * 1024 + so + 512],
                                     AF.Square).then_inc(sa, 1)
                    S.add("scalar", dosqa)
                    n["sp"] += 1
                    stat_pm = n["sp"]
                    def dostat(e, c=c, cm=cm, am=am, m=stat_pm, pv=prev_stats_v):
                        e.wait_ge(sw, cm)
                        e.wait_ge(sa, am)
                        if c == 0 and pv:
                            e.wait_ge(sv, pv)
                        settle(e, ps_ag[0][0:1, 0:1], ones128b[:], k=2)
                        e.matmul(ps_m[0:1, :], ones128b[:], hbfc[:],
                                 start=(c == 0), stop=(c == 7))
                        e.matmul(ps_q[0:1, :], ones128b[:], sqc[:],
                                 start=(c == 0), stop=(c == 7)).then_inc(sp, 1)
                    S.add("tensor", dostat)
                n["sv"] += 4
                vm = n["sv"]
                prev_stats_v = vm - 2
                def dostats(e, so=so, pm=stat_pm, m=vm):
                    e.wait_ge(sp, pm)
                    e.tensor_scalar_mul(mrow[0:1, so:so + 512], ps_m[0:1, :], 1.0 / D).then_inc(sv, 1)
                    e.tensor_scalar_mul(qrow[0:1, so:so + 512], ps_q[0:1, :], 1.0 / D).then_inc(sv, 1)
                    e.wait_ge(sv, m - 2)
                    e.tensor_mul(vrow[0:1, so:so + 512], mrow[0:1, so:so + 512],
                                 mrow[0:1, so:so + 512]).then_inc(sv, 1)
                    e.wait_ge(sv, m - 1)
                    e.scalar_tensor_tensor(vrow[0:1, so:so + 512], qrow[0:1, so:so + 512],
                                           1e-5, vrow[0:1, so:so + 512],
                                           ALU.subtract if False else ALU.add, ALU.subtract).then_inc(sv, 1)
                    e.wait_ge(sv, m)
                S.add("vector", dostats)
                n["sa"] += 1
                lnm = n["sa"]
                def doln(e, so=so, vm=vm, am=lnm):
                    e.wait_ge(sv, vm)
                    e.activation(rrow[0:1, so:so + 512], vrow[0:1, so:so + 512],
                                 AF.Ln).then_inc(sa, 1)
                S.add("scalar", doln)
                n["sv"] += 1
                hm = n["sv"]
                def dohalf(e, so=so, am=lnm, m=hm):
                    e.wait_ge(sa, am)
                    e.tensor_scalar_mul(rrow[0:1, so:so + 512],
                                        rrow[0:1, so:so + 512], -0.5).then_inc(sv, 1)
                S.add("vector", dohalf)
                n["sa"] += 1
                em = n["sa"]
                def doexp(e, so=so, vm=hm, am=em):
                    e.wait_ge(sv, vm)
                    e.activation(rrow[0:1, so:so + 512], rrow[0:1, so:so + 512],
                                 AF.Exp).then_inc(sa, 1)
                S.add("scalar", doexp)
                sub_rstd.append(em)
                stats_read[0] = vm
            last_norm = 0
            for sb2 in range(2):
                so = sb2 * 512
                n["sp"] += 1
                bm = n["sp"]
                def dobc(e, so=so, am=sub_rstd[sb2], m=bm, xw=last_norm):
                    e.wait_ge(sa, am)
                    if xw:
                        e.wait_ge(sv, xw)
                    e.matmul(ps_ag[0][:], ones1[:], mrow[0:1, so:so + 512],
                             start=True, stop=True)
                    e.matmul(ps_ag[1][:], ones1[:], rrow[0:1, so:so + 512],
                             start=True, stop=True).then_inc(sp, 1)
                S.add("tensor", dobc)
                for c in range(8):
                    n["sv"] += 3
                    last_norm = n["sv"]
                    def donorm(e, c=c, so=so, pm=bm, m=n["sv"]):
                        e.wait_ge(sp, pm)
                        e.tensor_sub(hst[:, c * 1024 + so: c * 1024 + so + 512],
                                     hst[:, c * 1024 + so: c * 1024 + so + 512],
                                     ps_ag[0][:]).then_inc(sv, 1)
                        e.wait_ge(sv, m - 2)
                        e.tensor_mul(hst[:, c * 1024 + so: c * 1024 + so + 512],
                                     hst[:, c * 1024 + so: c * 1024 + so + 512],
                                     ps_ag[1][:]).then_inc(sv, 1)
                        e.wait_ge(sv, m - 1)
                        e.tensor_scalar(xb[:, c * 1024 + so: c * 1024 + so + 512],
                                        hst[:, c * 1024 + so: c * 1024 + so + 512],
                                        lg[:, c:c + 1], lb[:, c:c + 1],
                                        ALU.mult, ALU.add).then_inc(sv, 1)
                        e.wait_ge(sv, m)
                    S.add("vector", donorm)
            stats_done[g] = n["sv"]
            xb_done = n["sv"]
            if g == 0 and "dbg_xb" in H:
                n["sg"] += 32
                def dodbg(e, m=xb_done):
                    e.wait_ge(sv, m)
                    e.dma_start(H["dbg_xb"], xb[:]).then_inc(sg, 16)
                    e.dma_start(H["dbg_rows"][0:1, :], mrow[:]).then_inc(sg, 16)
                S.add("gpsimd", dodbg)
                n["sg"] += 32
                def dodbg2(e, m=xb_done):
                    e.wait_ge(sv, m)
                    e.dma_start(H["dbg_rows"][1:2, :], vrow[:]).then_inc(sg, 16)
                    e.dma_start(H["dbg_rows"][2:3, :], rrow[:]).then_inc(sg, 16)
                S.add("gpsimd", dodbg2)

            blocks = [(b * 512, min(512, HID - b * 512)) for b in range(6)]
            for bi, (bo, bw) in enumerate(blocks):
                for half_w, (wdst, wcoff) in enumerate([(wa, 0), (wg, 2730)]):
                    lw1 = 0
                    for c in range(8):
                        w_ = [(sw, w1cast_last[0])] if c == 0 else []
                        lw1 = load(wst[:, c * 512:c * 512 + bw],
                                   bass.AP(H["mf_w1"].tensor,
                                           l * 1024 * 5460 + (c * 128) * 5460 + wcoff + bo,
                                           [[5460, 128], [1, bw]]), w_)
                    for c in range(8):
                        n["sw"] += 1
                        def dcw(e, c=c, m=lw1, first=(c == 0), bw=bw, wdst=wdst):
                            if first:
                                e.wait_ge(sd, m)
                            ecopy(e, wdst[:, c * 512:c * 512 + bw],
                              wst[:, c * 512:c * 512 + bw]).then_inc(sw, 1)
                        S.add("scalar", dcw)
                    w1cast_last[0] = n["sw"]
                wc = n["sw"]
                for tl in range(8):
                    n["sp"] += 1
                    agm = n["sp"]
                    def doag(e, tl=tl, bo=bo, bw=bw, xm=xb_done, wm=wc, m=agm,
                             hw=h4_mark[0], first=(tl == 0)):
                        if first:
                            e.wait_ge(sv, xm)
                            e.wait_ge(sw, wm)
                        if hw:
                            e.wait_ge(sv, hw)
                        settle(e, ps_m[0:1, 0:1], ones128b[:])
                        for c in range(8):
                            e.matmul(ps_ag[0][:, 0:bw],
                                     xb[:, c * 1024 + tl * 128: c * 1024 + (tl + 1) * 128],
                                     wa[:, c * 512:c * 512 + bw],
                                     start=(c == 0), stop=False)
                        e.matmul(ps_ag[0][:, 0:bw], ones1[:],
                                 b1r[0:1, bo:bo + bw], start=False, stop=True)
                        for c in range(8):
                            e.matmul(ps_ag[1][:, 0:bw],
                                     xb[:, c * 1024 + tl * 128: c * 1024 + (tl + 1) * 128],
                                     wg[:, c * 512:c * 512 + bw],
                                     start=(c == 0), stop=False)
                        e.matmul(ps_ag[1][:, 0:bw], ones1[:],
                                 b1r[0:1, 2730 + bo:2730 + bo + bw],
                                 start=False, stop=True).then_inc(sp, 1)
                    S.add("tensor", doag)
                    n["sa"] += 1
                    gm = n["sa"]
                    def dogel(e, pm=agm, am=gm, bw=bw):
                        e.wait_ge(sp, pm)
                        e.activation(gel[:, 0:bw], ps_ag[1][:, 0:bw],
                                     AF.Gelu).then_inc(sa, 1)
                    S.add("scalar", dogel)
                    n["sv"] += 1
                    h4_mark[0] = n["sv"]
                    dbg_first = (g == 0 and bi == 0 and tl == 0 and "dbg_ag" in H)
                    def doh4(e, am=gm, m=n["sv"], bw=bw, tw=trp_mark[0], dbg=dbg_first):
                        e.wait_ge(sa, am)
                        if tw:
                            e.wait_ge(sp, tw)
                        if dbg:
                            e.tensor_copy(dbgag[:, 0:512], ps_ag[0][:])
                            e.tensor_copy(dbgag[:, 512:1024], ps_ag[1][:])
                            e.tensor_copy(dbgag[:, 1024:1536], gel[:])
                        e.tensor_mul(h4[:, 0:bw], ps_ag[0][:, 0:bw],
                                     gel[:, 0:bw]).then_inc(sv, 1)
                        if dbg:
                            e.wait_ge(sv, m)
                            e.tensor_copy(dbgag[:, 1536:2048], h4[:])
                    S.add("vector", doh4)
                    nch = (bw + 127) // 128
                    seq = (g * 6 + bi) * 8 + tl
                    n["sp"] += 1
                    trp_mark[0] = n["sp"]
                    def dotrh(e, bw=bw, nch=nch, hm=h4_mark[0], m=n["sp"],
                              seq=seq, ew=evh.get(seq - 2, 0)):
                        e.wait_ge(sv, hm)
                        if ew:
                            e.wait_ge(sv, ew)
                        settle(e, ps_m[0:1, 0:1], ones128b[:])
                        for j in range(nch):
                            cw = min(128, bw - j * 128)
                            e.transpose(ps_t[seq % 2][0:cw, j * 128: j * 128 + 128],
                                        h4[:, j * 128: j * 128 + cw], idb[:])
                        e.matmul(ps_m[0:1, 0:1], ones128b[:], ones128b[:],
                                 start=True, stop=True).then_inc(sp, 1)
                    S.add("tensor", dotrh)
                    n["sv"] += 1
                    evh[seq] = n["sv"]
                    def doevh(e, tl=tl, bo=bo, bw=bw, nch=nch, seq=seq,
                              pm=trp_mark[0], m=n["sv"],
                              uw=(upd_done.get(g - 1, 0) if (bi == 0 and tl == 0 and g) else 0)):
                        e.wait_ge(sp, pm)
                        if uw:
                            e.wait_ge(sp, uw)
                        i = None
                        for j in range(nch):
                            cw = min(128, bw - j * 128)
                            hc = bo // 128 + j
                            i = e.tensor_copy(
                                h4T[0:cw, hc * 1024 + tl * 128: hc * 1024 + (tl + 1) * 128],
                                ps_t[seq % 2][0:cw, j * 128: j * 128 + 128])
                        i.then_inc(sv, 1)
                    S.add("vector", doevh)
            h4_all = n["sv"]
            if g == 0 and "dbg_ag" in H:
                n["sg"] += 16
                def dodbg4(e, m=h4_all):
                    e.wait_ge(sv, m)
                    e.dma_start(H["dbg_ag"], dbgag[:]).then_inc(sg, 16)
                S.add("gpsimd", dodbg4)
            if g == 0 and "dbg_h4T" in H:
                n["sg"] += 16
                def dodbg3(e, m=h4_all):
                    e.wait_ge(sv, m)
                    e.dma_start(H["dbg_h4T"], h4T[:]).then_inc(sg, 16)
                S.add("gpsimd", dodbg3)

            for oc in range(8):
                lw2 = 0
                for hc in range(NHC):
                    kw = 128 if hc < 21 else 42
                    w_ = []
                    if hc == 0 and (g or oc):
                        w_ = [(sw, w2cast.get((g, oc - 1), 0) or w2cast.get((g - 1, 7), 0))]
                    lw2 = load(w2st[0:kw, hc * 128:(hc + 1) * 128],
                               bass.AP(H["mf_w2"].tensor,
                                       l * 2730 * 1024 + (hc * 128) * 1024 + oc * 128,
                                       [[1024, kw], [1, 128]]), w_)
                n["sw"] += 1
                w2cast[(g, oc)] = n["sw"]
                def dc2(e, m=lw2, sm=n["sw"], uw=upd_m.get((g, oc - 1), 0) or upd_m.get((g - 1, 7), 0)):
                    e.wait_ge(sd, m)
                    if uw:
                        e.wait_ge(sp, uw)
                    ecopy(e, w2b[:], w2st[:]).then_inc(sw, 1)
                S.add("scalar", dc2)
                for half in range(2):
                    s = (g * 8 + oc) * 2 + half
                    par = s % 2
                    prev_add = add_m.get(s - 2, 0)
                    prev_st = st_m.get(s - 2, 0)
                    lw = [(sv, prev_add)] if prev_add else []
                    lm = load(xsl[par][:], xf_ap(xf_in, oc, 1 + g0 + half * 512, 512), lw)
                    n["sp"] += 1
                    um = n["sp"]
                    upd_m[(g, oc)] = um
                    def doum(e, par=par, half=half, hm=h4_all, wm=w2cast[(g, oc)],
                             m=um, aw=prev_add):
                        e.wait_ge(sv, hm)
                        e.wait_ge(sw, wm)
                        if aw:
                            e.wait_ge(sv, aw)
                        for hc in range(NHC):
                            kw = 128 if hc < 21 else 42
                            i = e.matmul(ps_u[par][:],
                                         w2b[0:kw, hc * 128:(hc + 1) * 128],
                                         h4T[0:kw, hc * 1024 + half * 512:
                                             hc * 1024 + (half + 1) * 512],
                                         start=(hc == 0), stop=(hc == NHC - 1))
                        i.then_inc(sp, 1)
                    S.add("tensor", doum)
                    n["sv"] += 1
                    add_m[s] = n["sv"]
                    def doadd(e, par=par, oc=oc, pm=um, lm=lm, m=n["sv"], gw=prev_st):
                        e.wait_ge(sp, pm)
                        e.wait_ge(sd, lm)
                        if gw:
                            e.wait_ge(sg, gw)
                        e.scalar_tensor_tensor(osl[par][:], ps_u[par][:],
                                               b2[:, oc:oc + 1], xsl[par][:],
                                               ALU.add, ALU.add).then_inc(sv, 1)
                    S.add("vector", doadd)
                    n["sg"] += 16
                    st_m[s] = n["sg"]
                    def dost2(e, par=par, oc=oc, g0=g0, half=half, m=add_m[s]):
                        e.wait_ge(sv, m)
                        e.dma_start(xf_ap(xf_out, oc, 1 + g0 + half * 512, 512),
                                    osl[par][:]).then_inc(sg, 16)
                    S.add("gpsimd", dost2)
            upd_done[g] = n["sp"]

        def fin(e, tot=n["sg"]):
            e.wait_ge(sg, tot)
        S.add("gpsimd", fin)
        S.run(nc)
    return



def ph_final(nc, H, tag, y_buf, xf_buf, out_buf):
    with ExitStack() as st:
        P = Phase(nc, st, tag)
        xf_sb = P.sb("xf_sb", [128, 8 * (T + 1)], F32)
        xst = [P.sb(f"xst{i}", [128, D], F32) for i in range(2)]
        yt = [P.sb(f"yt{i}", [128, D], F32) for i in range(2)]
        ot = [P.sb(f"ot{i}", [128, D], F32) for i in range(2)]
        idf = P.sb("idf", [128, 128], F32)
        ps_t = [P.ps(f"pst{i}", [128, D]) for i in range(2)]
        ps_dm = P.ps("ps_dm", [1, 1])
        sd = P.sem("sd"); sp = P.sem("sp"); sv = P.sem("sv"); sg = P.sem("sg")
        S = Sched()
        n = dict(sd=0, sp=0, sv=0, sg=0)

        def load(dst, src_ap, waits=()):
            n["sd"] += 16
            def f(e, d=dst, s=src_ap, w=tuple(waits)):
                for sem, val in w:
                    e.wait_ge(sem, val)
                e.dma_start(d, s).then_inc(sd, 16)
            S.add("sync", f)
            return n["sd"]

        load(idf[:], H["ident_f32"])
        for c in range(8):
            load(xf_sb[:, c * (T + 1):(c + 1) * (T + 1)],
                 xf_buf[c * 128:(c + 1) * 128, :])
        xf_ld = n["sd"]
        ev_m = {}; st_m = {}; tr_m = {}
        for ti in range(NT):
            par = ti % 2
            lw = [(sv, ev_m[ti - 2])] if ti >= 2 else []
            lm = load(yt[par][:], y_buf[ti * 128:(ti + 1) * 128, :], lw)
            n["sv"] += 1
            stg_m = n["sv"]
            def dostg(e, par=par, ti=ti, first=(ti == 0), m=n["sv"],
                      tw=tr_m.get(ti - 2, 0)):
                if first:
                    e.wait_ge(sd, xf_ld)
                if tw:
                    e.wait_ge(sp, tw)
                for c in range(8):
                    i = e.tensor_copy(xst[par][:, c * 128:(c + 1) * 128],
                                      xf_sb[:, c * (T + 1) + 1 + ti * 128:
                                            c * (T + 1) + 1 + (ti + 1) * 128])
                i.then_inc(sv, 1)
            S.add("vector", dostg)
            n["sp"] += 1
            tr_m[ti] = n["sp"]
            def dotr(e, par=par, ti=ti, m=tr_m[ti], sm=stg_m,
                     ew=ev_m.get(ti - 2, 0)):
                e.wait_ge(sv, sm)
                if ew:
                    e.wait_ge(sv, ew)
                settle(e, ps_dm[0:1, 0:1], idf[:, 0:1], k=2)
                for c in range(8):
                    e.transpose(ps_t[par][:, c * 128:(c + 1) * 128],
                                xst[par][:, c * 128:(c + 1) * 128], idf[:])
                e.matmul(ps_dm[0:1, 0:1], idf[:, 0:1], idf[:, 0:1],
                         start=True, stop=True).then_inc(sp, 1)
            S.add("tensor", dotr)
            n["sv"] += 1
            ev_m[ti] = n["sv"]
            def doev(e, par=par, pm=tr_m[ti], lm=lm, ti=ti, m=n["sv"],
                     gw=st_m.get(ti - 2, 0)):
                e.wait_ge(sp, pm)
                e.wait_ge(sd, lm)
                if gw:
                    e.wait_ge(sg, gw)
                e.tensor_add(ot[par][:], ps_t[par][:], yt[par][:]).then_inc(sv, 1)
            S.add("vector", doev)
            n["sg"] += 16
            st_m[ti] = n["sg"]
            def dost(e, par=par, ti=ti, m=ev_m[ti]):
                e.wait_ge(sv, m)
                e.dma_start(out_buf[ti * 128:(ti + 1) * 128, :], ot[par][:]).then_inc(sg, 16)
            S.add("gpsimd", dost)

        def fin(e, tot=n["sg"]):
            e.wait_ge(sg, tot)
        S.add("gpsimd", fin)
        S.run(nc)
    return



PHASES = [
    ("ln0", lambda nc, H: ph_ln_to_fm(nc, H, "x0", "xn0b", "ln0"),
     ["ident_bf"], ["x0"], ["xn0b"]),
    ("qkv0", lambda nc, H: ph_qkv01(nc, H, "qkv0", H["xn0b"], H["l0_wqT"],
                                    H["l0_wkT"], H["l0_wvT"], H["q0"], H["k0"], H["v0"]),
     ["l0_wqT", "l0_wkT", "l0_wvT", "ones_f32"], ["xn0b"], ["q0", "k0", "v0"]),
    ("attn0", lambda nc, H: ph_attn(nc, H, "attn0", H["q0"], H["k0"], H["v0"],
                                    H["ao0"], ATTN0_CFG),
     ["ident_bf"] + [f"m0_{i}" for i in range(6)], ["q0", "k0", "v0"], ["ao0"]),
]

ATTN0_CFG = AttnCfg(nh=16, v_cols=1024, scale=0.125, ranges=_l0_ranges,
                    bias=("mask", lambda qt: (f"m0_{min(qt, 5)}", 0)),
                    head_flush=8, lw=768)

ATTN1_CFG = AttnCfg(nh=16, v_cols=1024, scale=0.125,
                    ranges=lambda qt: [(0, T)], bias=("E", "ebias"),
                    head_flush=2, lw=T)

PHASES.append(
    ("comb0", lambda nc, H: ph_proj(nc, H, "comb0", H["ao0"], "l0_woT",
                                    H["x0"], H["x1"], bias_name="l0_bias",
                                    l0ffn=("l0_fw", "l0_alpha", H["xn0b"]),
                                    ln_fm_out="xn1b"),
     ["l0_woT", "l0_fw", "l0_alpha", "l0_bias", "ident_bf", "ones_f32",
      "ones128c_f32"], ["x0", "ao0", "xn0b"], ["x1", "xn1b"]))
PHASES.append(
    ("qkv1", lambda nc, H: ph_qkv01(nc, H, "qkv1", H["xn1b"], H["l1_wq"],
                                    H["l1_wk"], H["l1_wv"], H["q1"], H["k1"],
                                    H["v1"], bqk=("l1_bq", "l1_bk"), bv="l1_bv"),
     ["l1_wq", "l1_wk", "l1_wv", "l1_bq", "l1_bk", "l1_bv", "ones_f32"],
     ["xn1b"], ["q1", "k1", "v1"]))
PHASES.append(
    ("attn1", lambda nc, H: ph_attn(nc, H, "attn1", H["q1"], H["k1"], H["v1"],
                                    H["ao1"], ATTN1_CFG),
     ["ident_bf", "ebias"], ["q1", "k1", "v1"], ["ao1"]))
PHASES.append(
    ("out1", lambda nc, H: ph_proj(nc, H, "out1", H["ao1"], "l1_wo",
                                   H["x1"], H["y"], bias_name="l1_bo"),
     ["l1_wo", "l1_bo", "ident_bf", "ones_f32", "ones128c_f32"],
     ["x1", "ao1"], ["y"]))

MATTN_CFG = AttnCfg(nh=8, v_cols=512, scale=1.0, ranges=_mini_ranges,
                    bias=("mask", lambda qt: (f"mw_{qt % 4}", 512 if qt < 4 else 0)),
                    head_flush=8, lw=1024)

_ROT = ["rot_cs", "rot_ss", "rot_ci", "rot_si"]
PHASES.append(
    ("mini_init", lambda nc, H: ph_ln_to_fm(nc, H, "x1", "xf_a", "mini_init",
                                            out_f32=True),
     ["ident_f32"], ["x1"], ["xf_a"]))
for _l in (0, 1):
    _xin, _xmid = ("xf_a", "xf_b")
    PHASES.append(
        (f"mqkv{_l}", (lambda l, xin: lambda nc, H: ph_mqkv(
            nc, H, f"mqkv{l}", l, H[xin], H["qf"], H["kf"], H["vf"]))(_l, _xin),
         ["mf_wqkv", "mf_qs", "mf_ks", "ones64c_f32", "ones_f32"] + _ROT,
         [_xin], ["qf", "kf", "vf"]))
    PHASES.append(
        (f"mattn{_l}", (lambda l: lambda nc, H: ph_attn(
            nc, H, f"mattn{l}", H["qf"], H["kf"], H["vf"], H["aof"], MATTN_CFG))(_l),
         ["ident_bf"] + [f"mw_{i}" for i in range(4)], ["qf", "kf", "vf"], ["aof"]))
    PHASES.append(
        (f"mwo{_l}", (lambda l, xin, xout: lambda nc, H: ph_mwo(
            nc, H, f"mwo{l}", l, H[xin], H[xout], H["aof"]))(_l, _xin, _xmid),
         ["mf_wo", "ident_bf"], [_xin, "aof"], [_xmid]))
    PHASES.append(
        (f"mffn{_l}", (lambda l, xin, xout: lambda nc, H: ph_mffn(
            nc, H, f"mffn{l}", l, H[xin], H[xout]))(_l, _xmid, _xin),
         ["mf_w1", "mf_b1", "mf_w2", "mf_b2T", "mf_lgT", "mf_lbT",
          "ident_bf", "ones_f32", "ones128c_bf"], [_xmid], [_xin]))
PHASES.append(
    ("final", lambda nc, H: ph_final(nc, H, "final", H["y"], H["xf_a"], H["out"]),
     ["ident_f32"], ["y", "xf_a"], ["out"]))


def _build_phase_nc(name, emit, shared_in, buf_in, buf_out):
    nc = bass.Bass("TRN2", target_bir_lowering=False)
    H = {}
    for nm in shared_in:
        shape, dt = SHARED_SPECS[nm]
        H[nm] = nc.dram_tensor(nm, list(shape), dt, kind="ExternalInput").ap()
    for nm in buf_in:
        shape, dt = BUF_SPECS[nm]
        H[nm] = nc.dram_tensor(nm, list(shape), dt, kind="ExternalInput").ap()
    for nm in buf_out:
        shape, dt = BUF_SPECS[nm]
        H[nm] = nc.dram_tensor(nm, list(shape), dt, kind="ExternalOutput").ap()
    emit(nc, H)
    return nc


def run_phases(inputs, phases=None, state=None):
    phases = phases if phases is not None else PHASES
    shared = prep_shared_inputs(inputs)
    x0_full = np.ascontiguousarray(np.asarray(inputs["L0_attn_x"], np.float32))
    state = state if state is not None else {}
    state["x0"] = [np.ascontiguousarray(x0_full[b]) for b in range(NC_CORES)]
    for (name, emit, shared_in, buf_in, buf_out) in phases:
        nc = _build_phase_nc(name, emit, shared_in, buf_in, buf_out)
        in_maps = []
        for b in range(NC_CORES):
            m = {nm: shared[nm] for nm in shared_in}
            for nm in buf_in:
                m[nm] = state[nm][b]
            in_maps.append(m)
        res = run_bass_kernel_spmd(nc, in_maps, core_ids=list(range(NC_CORES)))
        for nm in buf_out:
            state[nm] = [np.asarray(res.results[b][nm]) for b in range(NC_CORES)]
    return state


def _fused_used_shared():
    names = []
    for (_, _, shared_in, _, _) in PHASES:
        for nm in shared_in:
            if nm not in names:
                names.append(nm)
    return names


def build_fused():
    nc = bass.Bass("TRN2", target_bir_lowering=False)
    H = {}
    for nm in _fused_used_shared():
        shape, dt = SHARED_SPECS[nm]
        H[nm] = nc.dram_tensor(nm, list(shape), dt, kind="ExternalInput").ap()
    for nm, (shape, dt) in BUF_SPECS.items():
        kind = ("ExternalInput" if nm == "x0" else
                "ExternalOutput" if nm == "out" else "Internal")
        H[nm] = nc.dram_tensor(nm, list(shape), dt, kind=kind).ap()
    with ExitStack() as st:
        bar = st.enter_context(nc.semaphore("phase_bar"))
        for k, (name, emit, shared_in, buf_in, buf_out) in enumerate(PHASES):
            emit(nc, H)
            with nc.Block() as blk:
                for eng in Sched.ENGINES:
                    def f(e, k=k):
                        e.nop().then_inc(bar, 1)
                        e.wait_ge(bar, 5 * (k + 1))
                    getattr(blk, eng)(f)
    return nc


def kernel_fused(**inputs):
    shared = prep_shared_inputs(inputs)
    x0_full = np.ascontiguousarray(np.asarray(inputs["L0_attn_x"], np.float32))
    nc = build_fused()
    used = _fused_used_shared()
    in_maps = []
    for b in range(NC_CORES):
        m = {nm: shared[nm] for nm in used}
        m["x0"] = np.ascontiguousarray(x0_full[b])
        in_maps.append(m)
    res = run_bass_kernel_spmd(nc, in_maps, core_ids=list(range(NC_CORES)))
    return np.stack([np.asarray(res.results[b]["out"]) for b in range(NC_CORES)], axis=0)


def kernel(**inputs):
    return kernel_fused(**inputs)
